# version 14
# baseline (speedup 1.0000x reference)
import numpy as np
import concourse.bass as bass
import concourse.mybir as mybir
from concourse.bass_utils import run_bass_kernel_spmd

F32 = mybir.dt.float32
BF16 = mybir.dt.bfloat16
AF = mybir.ActivationFunctionType
ALU = mybir.AluOpType
AX = mybir.AxisListType

N_CORES = 8
B_PER_CORE = 2
T = 2048
D = 1024
H = 16
DH = 64
NCH = D // 128
NT = T // 128
L = 256
E = 8
EPS_A = float(np.finfo(np.float32).eps)
EPS_F = 1e-6
EPS_LN = 1e-5
NEG = -1.0e30

STAGES = ("A", "B", "C", "F")
DBG = ()

KEYBLK = 2048


class Sched:
    def __init__(self):
        self.ops = []
        self.last_w = {}
        self.readers = {}

    def add(self, ch, emit, r=(), w=(), dma=None):
        i = len(self.ops)
        raw = set()
        other = set()
        for k in r:
            j = self.last_w.get(k)
            if j is not None:
                raw.add(j)
        for k in w:
            j = self.last_w.get(k)
            if j is not None:
                other.add(j)
            for j2 in self.readers.get(k, ()):
                other.add(j2)
        for k in r:
            self.readers.setdefault(k, []).append(i)
        for k in w:
            self.last_w[k] = i
            self.readers[k] = []
        self.ops.append({"ch": ch, "emit": emit, "dma": dma, "raw": raw,
                         "other": other})
        return i

    def finalize(self):
        ops = self.ops
        n = len(ops)
        deps = [None] * n
        for i, op in enumerate(ops):
            d = set()
            for j in op["raw"]:
                pj = ops[j]
                if (pj["ch"] == op["ch"] == "PE"
                        and pj["dma"] is None and op["dma"] is None):
                    continue
                d.add(j)
            for j in op["other"]:
                pj = ops[j]
                if (pj["ch"] == op["ch"] == "PE"
                        and pj["dma"] is None and op["dma"] is None):
                    continue
                d.add(j)
            d.discard(i)
            deps[i] = d
        need = [False] * n
        for i in range(n):
            for j in deps[i]:
                need[j] = True
        NSUB = {"Pq": 36, "Sq": 36, "POq": 1, "SOq": 1}
        cnt = {}
        inc_val = [None] * n
        sub_of = [None] * n
        dma_cum = [None] * n
        cur_dma = {}
        rr = {}
        for i, op in enumerate(ops):
            dma_cum[i] = dict(cur_dma)
            if need[i]:
                if op["dma"]:
                    q = op["dma"]
                    rr[q] = rr.get(q, -1) + 1
                    sem = f"{q}{rr[q] % NSUB[q]}"
                    sub_of[i] = sem
                    cnt[sem] = cnt.get(sem, 0) + 16
                    inc_val[i] = cnt[sem]
                    cur_dma[sem] = cnt[sem]
                else:
                    sem = op["ch"]
                    cnt[sem] = cnt.get(sem, 0) + 1
                    inc_val[i] = cnt[sem]
        per_ch = {}
        clock = {}
        for i, op in enumerate(ops):
            waits = []
            for j in sorted(deps[i]):
                pj = ops[j]
                if pj["dma"]:
                    sem = sub_of[j]
                    v = dma_cum[i].get(sem, 0)
                else:
                    sem = pj["ch"]
                    v = inc_val[j]
                key = (op["ch"], sem)
                if clock.get(key, 0) >= v:
                    continue
                clock[key] = v
                waits.append((sem, v))
            inc = None
            if need[i]:
                sem = sub_of[i] if op["dma"] else op["ch"]
                inc = (sem, 16 if op["dma"] else 1)
            per_ch.setdefault(op["ch"], []).append(
                {"emit": op["emit"], "waits": waits, "inc": inc})
        self.sem_totals = cnt
        return per_ch


class Arena:
    def __init__(self, nbytes):
        self.free_l = [(0, nbytes)]
        self.used = {}
        self.hw = 0

    def alloc(self, name, nb, hi=False):
        nb = (nb + 31) & ~31
        idxs = range(len(self.free_l))
        if hi:
            idxs = reversed(list(idxs))
        for idx in idxs:
            off, sz = self.free_l[idx]
            if sz >= nb:
                if hi:
                    self.used[name] = (off + sz - nb, nb)
                    if sz == nb:
                        self.free_l.pop(idx)
                    else:
                        self.free_l[idx] = (off, sz - nb)
                    return off + sz - nb
                self.used[name] = (off, nb)
                if sz == nb:
                    self.free_l.pop(idx)
                else:
                    self.free_l[idx] = (off + nb, sz - nb)
                return off
        raise MemoryError(f"arena: cannot alloc {name} ({nb}B); free={self.free_l}")

    def release(self, name):
        off, nb = self.used.pop(name)
        self.free_l.append((off, nb))
        self.free_l.sort()
        merged = []
        for o, s in self.free_l:
            if merged and merged[-1][0] + merged[-1][1] == o:
                merged[-1] = (merged[-1][0], merged[-1][1] + s)
            else:
                merged.append((o, s))
        self.free_l = merged


class Buf:
    def __init__(self, bld, name, dtype, cols, hi=True):
        self.bld = bld
        self.name = name
        self.dtype = dtype
        self.isz = 2 if dtype == BF16 else 4
        self.cols = cols
        self.off_b = bld.arena.alloc(name, cols * self.isz, hi=hi)

    def ap(self, c0=0, c1=None, p0=0, p1=128):
        c1 = self.cols if c1 is None else c1
        h, W = self.bld.handle(self.dtype)
        off = p0 * W + self.off_b // self.isz + c0
        return bass.AP(h, off, [[W, p1 - p0], [1, c1 - c0]])

    def ap3(self, c0, dims, p0=0, p1=128):
        h, W = self.bld.handle(self.dtype)
        off = p0 * W + self.off_b // self.isz + c0
        return bass.AP(h, off, [[W, p1 - p0]] + [[s, c] for s, c in dims])

    def k(self, c0=0, c1=None):
        c1 = self.cols if c1 is None else c1
        b0 = self.off_b + c0 * self.isz
        b1 = self.off_b + c1 * self.isz
        return [("S", x) for x in range(b0 // KEYBLK, (b1 - 1) // KEYBLK + 1)]


class CBuf:

    def __init__(self, bld, name, dtype, nch, cols_per, hi=True):
        self.name = name
        self.cols_per = cols_per
        self.chunks = [Buf(bld, f"{name}.{i}", dtype, cols_per, hi=hi)
                       for i in range(nch)]

    def ap(self, c0=0, c1=None, p0=0, p1=128):
        ch = c0 // self.cols_per
        assert c1 is not None and c1 <= (ch + 1) * self.cols_per, (c0, c1)
        base = ch * self.cols_per
        return self.chunks[ch].ap(c0 - base, c1 - base, p0, p1)

    def ap3(self, c0, dims, p0=0, p1=128):
        ch = c0 // self.cols_per
        base = ch * self.cols_per
        ext = c0 - base + sum(s_ * (n_ - 1) for s_, n_ in dims) + 1
        assert ext <= self.cols_per, (c0, dims)
        return self.chunks[ch].ap3(c0 - base, dims, p0, p1)

    def k(self, c0=0, c1=None):
        if c1 is None:
            c1 = len(self.chunks) * self.cols_per
        out = []
        ch0 = c0 // self.cols_per
        ch1 = (c1 - 1) // self.cols_per
        for ch in range(ch0, ch1 + 1):
            base = ch * self.cols_per
            a = max(c0, base) - base
            bcl = min(c1, base + self.cols_per) - base
            out += self.chunks[ch].k(a, bcl)
        return out

    def release_all(self, arena):
        for c in self.chunks:
            arena.release(c.name)


class Builder:
    def __init__(self, nc, sched, arena_t, arena_cols, psums, dram):
        self.nc = nc
        self.s = sched
        self.arena = Arena(arena_cols * 2)
        self._hb = arena_t.ap()
        self._hf = arena_t.ap().bitcast(F32)
        self._hu = arena_t.ap().bitcast(mybir.dt.uint32)
        self.psf_t = [p.ap().bitcast(F32) for p in psums]
        self.psb_t = [p.ap() for p in psums]
        self.dram = dram
        self.bufs = {}
        self.dbg_keys = []

    def handle(self, dtype):
        ap = (self._hb if dtype == BF16 else
              self._hu if dtype == mybir.dt.uint32 else self._hf)
        return ap.tensor, ap.ap[0][0]

    LOW = {"RES", "Q", "K", "V", "OFMB", "X2B", "EYEF", "EYEB", "MASK",
           "GAIN", "QG", "KG", "IO8", "IND2", "ROW2", "RB8", "ONESC", "RBROW"}

    def _is_low(self, name):
        return name.split(".")[0] in self.LOW

    def buf(self, name, dtype, cols):
        bb = Buf(self, name, dtype, cols, hi=not self._is_low(name))
        self.bufs[name] = bb
        return bb

    def cbuf(self, name, dtype, nch, cols_per):
        bb = CBuf(self, name, dtype, nch, cols_per, hi=not self._is_low(name))
        self.bufs[name] = bb
        return bb

    def freebuf(self, *names):
        for nm in names:
            bb = self.bufs.pop(nm)
            if isinstance(bb, CBuf):
                bb.release_all(self.arena)
            else:
                self.arena.release(bb.name)

    def pf(self, i, c0=0, c1=512, p0=0, p1=128):
        ap = self.psf_t[i]
        return bass.AP(ap.tensor, p0 * 512 + c0, [[512, p1 - p0], [1, c1 - c0]])

    def pb(self, i, c0=0, c1=1024, p0=0, p1=128):
        ap = self.psb_t[i]
        return bass.AP(ap.tensor, p0 * 1024 + c0, [[1024, p1 - p0], [1, c1 - c0]])

    def pk(self, i):
        return [("P", i)]

    def dma(self, q, dst_ap, src_ap, r=(), w=(), transpose=False):
        eng = "POOL" if q in ("Pq", "POq") else "SP"
        self.s.add(eng, lambda e, d_=dst_ap, s_=src_ap, t_=transpose:
                   e.dma_start(d_, s_, transpose=t_), r=r, w=w, dma=q)

    def idma(self, dst_ap, dst_off, src_ap, src_off, r=(), w=()):
        from bass_rust import IndirectOffsetOnAxis
        oo = IndirectOffsetOnAxis(ap=dst_off, axis=0) if dst_off is not None else None
        io = IndirectOffsetOnAxis(ap=src_off, axis=0) if src_off is not None else None
        self.s.add("POOL", lambda e, d_=dst_ap, s_=src_ap, o_=oo, i_=io:
                   e.indirect_dma_start(d_, o_, s_, i_), r=r, w=w, dma="Pq")

    def mm(self, bank, lhsT, rhs, start, stop, r=(), out=None):
        o = out if out is not None else self.pf(bank)
        self.s.add("PE", lambda e, o_=o, l_=lhsT, r_=rhs, st=start, sp=stop:
                   e.matmul(o_, l_, r_, start=st, stop=sp),
                   r=list(r) + self.pk(bank), w=self.pk(bank))

    def transp(self, bank, in_ap, ident_ap, out_ap, r=()):
        self.s.add("PE", lambda e, o_=out_ap, i_=in_ap, d_=ident_ap:
                   e.transpose(o_, i_, d_), r=list(r) + self.pk(bank),
                   w=self.pk(bank))

    def op(self, ch, fn, *args, r=(), w=(), **kw):
        self.s.add(ch, lambda e, a=args, k=kw: getattr(e, fn)(*a, **k), r=r, w=w)


def build_nc():
    nc = bass.Bass("TRN2", target_bir_lowering=False)
    dram = {}

    def din(name, shape):
        dram[name] = nc.dram_tensor(name, list(shape), F32, kind="ExternalInput").ap()

    din("x0", (B_PER_CORE * T, D))
    din("wq_a", (128, 8 * D)); din("wk_a", (128, 8 * D)); din("wv_a", (128, 8 * D))
    din("wo_a", (128, 8 * D))
    din("c_qg", (128, 8)); din("c_kg", (128, 8))
    din("wq_f", (128, 8 * D)); din("wkv_f", (128, 8 * L))
    din("wk_up", (128, 2 * D)); din("wv_up", (128, 2 * D))
    din("wo_f", (128, 8 * D)); din("wr", (128, 8 * E)); din("c_rb", (1, E))
    din("sw1", (2, 128, 8 * 1024)); din("sw3", (2, 128, 8 * 1024))
    din("sw2", (2, 128, 8 * 1024))
    din("ew1", (E, 2, 128, 8 * 1024)); din("ew3", (E, 2, 128, 8 * 1024))
    din("ew2", (E, 2, 128, 8 * 1024))
    din("c_eye", (128, 128)); din("c_mask", (128, 128))
    din("c_cos", (128, 512)); din("c_sin", (128, 512))
    din("c_gain", (128, 24)); din("c_io8", (128, 8))
    din("c_ind2", (128, 2)); din("c_row2", (2, 256)); din("c_exp9", (16, 9 * 128))
    din("c_i128", (1, T)); din("c_r16", (16, 32))
    dram["out"] = nc.dram_tensor("out", [B_PER_CORE * T, D], F32,
                                 kind="ExternalOutput").ap()
    dram["RG"] = nc.dram_tensor("RG", [8 * 1024 + 128, D], BF16).ap()
    dram["OGT"] = nc.dram_tensor("OGT", [8 * 1024 + 128, D], BF16).ap()
    for nm in DBG:
        dram["dbg_" + nm] = nc.dram_tensor(
            "dbg_" + nm, [B_PER_CORE * NT * 128, T], F32, kind="ExternalOutput").ap()

    ARENA_COLS = 106352
    with (
        nc.sbuf_tensor("arena", [128, ARENA_COLS], BF16) as arena_t,
        nc.psum_tensor("ps0", [128, 1024], BF16) as ps0,
        nc.psum_tensor("ps1", [128, 1024], BF16) as ps1,
        nc.psum_tensor("ps2", [128, 1024], BF16) as ps2,
        nc.psum_tensor("ps3", [128, 1024], BF16) as ps3,
        nc.psum_tensor("ps4", [128, 1024], BF16) as ps4,
        nc.psum_tensor("ps5", [128, 1024], BF16) as ps5,
        nc.psum_tensor("ps6", [128, 1024], BF16) as ps6,
        nc.psum_tensor("ps7", [128, 1024], BF16) as ps7,
        nc.semaphore("sPE") as sPE,
        nc.semaphore("sDVE") as sDVE,
        nc.semaphore("sACT") as sACT,
        nc.semaphore("sPOOL") as sPOOL,
        nc.semaphore("sSP") as sSP,
        nc.Block() as block,
    ):
        sched = Sched()
        bld = Builder(nc, sched, arena_t, ARENA_COLS,
                      [ps0, ps1, ps2, ps3, ps4, ps5, ps6, ps7], dram)
        build_program(bld)
        per_ch = sched.finalize()
        sems = {"PE": sPE, "DVE": sDVE, "ACT": sACT, "POOL": sPOOL, "SP": sSP}
        import contextlib
        _stack = contextlib.ExitStack()
        for q, nsub in [("Pq", 36), ("Sq", 36), ("POq", 1), ("SOq", 1)]:
            for si in range(nsub):
                sems[f"{q}{si}"] = _stack.enter_context(
                    nc.semaphore(f"s{q}{si}"))
        for sem, tot in sched.sem_totals.items():
            assert tot < 60000, f"sem {sem} overflow risk: {tot}"

        def emit_ch(section, ch):
            items = per_ch.get(ch, [])

            @section
            def _(eng):
                for it in items:
                    for (sm, v) in it["waits"]:
                        eng.wait_ge(sems[sm], v)
                    inst = it["emit"](eng)
                    if it["inc"]:
                        inst.then_inc(sems[it["inc"][0]], it["inc"][1])

        emit_ch(block.tensor, "PE")
        emit_ch(block.vector, "DVE")
        emit_ch(block.scalar, "ACT")
        emit_ch(block.gpsimd, "POOL")
        emit_ch(block.sync, "SP")
    return nc


def build_program(b):
    d = b.dram
    EYEF = b.buf("EYEF", F32, 128)
    EYEB = b.buf("EYEB", BF16, 128)
    MASK = b.buf("MASK", F32, 128)
    GAIN = b.buf("GAIN", F32, 24)
    QG = b.buf("QG", F32, 8)
    KG = b.buf("KG", F32, 8)
    IO8 = b.buf("IO8", F32, 8)
    IND2 = b.buf("IND2", F32, 2)
    ROW2 = b.buf("ROW2", F32, 256)
    RB8 = b.buf("RB8", F32, 8)
    ONESC = b.buf("ONESC", F32, 16)
    RBROW = b.buf("RBROW", F32, 8)
    EPSC = b.buf("EPSC", F32, 4)

    b.dma("Sq", EYEF.ap(), d["c_eye"], w=EYEF.k())
    b.dma("Sq", MASK.ap(), d["c_mask"], w=MASK.k())
    b.dma("Sq", GAIN.ap(), d["c_gain"], w=GAIN.k())
    b.dma("Sq", QG.ap(), d["c_qg"], w=QG.k())
    b.dma("Sq", KG.ap(), d["c_kg"], w=KG.k())
    b.dma("Sq", IO8.ap(), d["c_io8"], w=IO8.k())
    b.dma("Sq", IND2.ap(), d["c_ind2"], w=IND2.k())
    b.dma("Sq", ROW2.ap(0, 256, 0, 2), d["c_row2"], w=ROW2.k())
    b.dma("Sq", RBROW.ap(0, 8, 0, 1), d["c_rb"], w=RBROW.k())
    b.op("DVE", "tensor_copy", EYEB.ap(), EYEF.ap(), r=EYEF.k(), w=EYEB.k())
    b.op("DVE", "memset", ONESC.ap(0, 1), 1.0, w=ONESC.k())
    b.op("DVE", "memset", EPSC.ap(0, 1), EPS_A, w=EPSC.k())
    b.op("DVE", "memset", EPSC.ap(1, 2), EPS_F, w=EPSC.k())
    b.op("DVE", "memset", EPSC.ap(2, 3), EPS_LN, w=EPSC.k())
    b.op("DVE", "tensor_scalar", QG.ap(), QG.ap(), 1.0, 8.0, ALU.add, ALU.mult,
         r=QG.k(), w=QG.k())
    b.op("DVE", "tensor_scalar", KG.ap(), KG.ap(), 1.0, 8.0, ALU.add, ALU.mult,
         r=KG.k(), w=KG.k())
    b.mm(7, ROW2.ap(128, 256, 0, 1), RBROW.ap(0, 8, 0, 1), True, True,
         r=ROW2.k() + RBROW.k(), out=b.pf(7, 0, 8))
    b.op("DVE", "tensor_copy", RB8.ap(), b.pf(7, 0, 8), r=RB8.k(), w=RB8.k() + b.pk(7))

    cst = dict(EYEF=EYEF, EYEB=EYEB, MASK=MASK, GAIN=GAIN,
               QG=QG, KG=KG, IO8=IO8, IND2=IND2, ROW2=ROW2, RB8=RB8,
               ONESC=ONESC, EPSC=EPSC)

    for bi in range(B_PER_CORE):
        if "A" in STAGES:
            stage_a(b, cst, bi)
        if "B" in STAGES:
            stage_b(b, cst, bi)
        if "C" in STAGES:
            stage_c(b, cst, bi)
        if "F" in STAGES:
            stage_f(b, cst, bi)
    outkeys = [("OUT", bb, tt) for bb in range(B_PER_CORE) for tt in range(NT)]
    outkeys += b.dbg_keys
    b.s.add("SP", lambda e: e.nop(), r=outkeys)


def rinv(b, cst, ap, src_ap, wk, rk):
    b.op("ACT", "activation", ap, src_ap, AF.Ln, r=list(rk) + list(wk), w=wk)
    b.op("ACT", "activation", ap, ap, AF.Exp, scale=-1.0, r=wk, w=wk)


def rsqrt_rows(b, cst, dst_ap, src_ap, epsi, scale, wk, rk):
    b.op("ACT", "activation", dst_ap, src_ap, AF.Ln,
         bias=cst["EPSC"].ap(epsi, epsi + 1, 0, 1), scale=scale,
         r=list(rk) + cst["EPSC"].k() + list(wk), w=wk)
    b.op("ACT", "activation", dst_ap, dst_ap, AF.Exp, scale=-0.5, r=wk, w=wk)

def rmsnorm_rows(b, cst, SRC, SQ, ROWR, bank, epsi):
    for tc in range(4):
        c0 = tc * 512
        for ch in range(NCH):
            b.op("ACT", "activation", SQ.ap(0, 512),
                 SRC.ap(ch * T + c0, ch * T + c0 + 512), AF.Square,
                 r=SRC.k(ch * T + c0, ch * T + c0 + 512), w=SQ.k())
            b.mm(bank, cst["ONESC"].ap(0, 1), SQ.ap(0, 512), ch == 0, ch == NCH - 1,
                 r=cst["ONESC"].k() + SQ.k(), out=b.pf(bank, 0, 512, 0, 1))
        b.op("ACT", "activation", ROWR.ap(c0, c0 + 512, 0, 1),
             b.pf(bank, 0, 512, 0, 1), AF.Ln,
             bias=cst["EPSC"].ap(epsi, epsi + 1, 0, 1), scale=1.0 / D,
             r=cst["EPSC"].k() + b.pk(bank),
             w=ROWR.k(c0, c0 + 512) + b.pk(bank))
        b.op("ACT", "activation", ROWR.ap(c0, c0 + 512, 0, 1),
             ROWR.ap(c0, c0 + 512, 0, 1), AF.Exp, scale=-0.5,
             r=ROWR.k(c0, c0 + 512), w=ROWR.k(c0, c0 + 512))


def bcast_row(b, cst, ROWR, DSTB, bank):
    for tc in range(4):
        c0 = tc * 512
        b.mm(bank, cst["ROW2"].ap(128, 256, 0, 1), ROWR.ap(c0, c0 + 512, 0, 1),
             True, True, r=cst["ROW2"].k() + ROWR.k(c0, c0 + 512))
        b.op("DVE", "tensor_copy", DSTB.ap(c0, c0 + 512), b.pf(bank, 0, 512),
             r=b.pk(bank), w=DSTB.k(c0, c0 + 512) + b.pk(bank))


def layernorm_tm(b, cst, ZTM, SQA, COLS):
    b.op("DVE", "tensor_reduce", COLS.ap(0, 1), ZTM.ap(), axis=AX.X, op=ALU.add,
         r=ZTM.k(), w=COLS.k(0, 1))
    b.op("DVE", "tensor_scalar", COLS.ap(0, 1), COLS.ap(0, 1), 1.0 / D, None,
         ALU.mult, r=COLS.k(0, 1), w=COLS.k(0, 1))
    b.op("ACT", "activation", SQA.ap(), ZTM.ap(), AF.Square,
         r=ZTM.k(), w=SQA.k())
    b.op("DVE", "tensor_reduce", COLS.ap(1, 2), SQA.ap(), axis=AX.X, op=ALU.add,
         r=SQA.k(), w=COLS.k(1, 2))
    b.op("DVE", "tensor_scalar", COLS.ap(1, 2), COLS.ap(1, 2), 1.0 / D, None,
         ALU.mult, r=COLS.k(1, 2), w=COLS.k(1, 2))
    b.op("DVE", "tensor_tensor", COLS.ap(2, 3), COLS.ap(0, 1), COLS.ap(0, 1),
         ALU.mult, r=COLS.k(0, 1), w=COLS.k(2, 3))
    b.op("DVE", "tensor_tensor", COLS.ap(2, 3), COLS.ap(1, 2), COLS.ap(2, 3),
         ALU.subtract, r=COLS.k(1, 2) + COLS.k(2, 3), w=COLS.k(2, 3))
    b.op("ACT", "activation", COLS.ap(2, 3), COLS.ap(2, 3), AF.Ln,
         bias=cst["EPSC"].ap(2, 3), r=COLS.k(2, 3) + cst["EPSC"].k(),
         w=COLS.k(2, 3))
    b.op("ACT", "activation", COLS.ap(2, 3), COLS.ap(2, 3), AF.Exp, scale=-0.5,
         r=COLS.k(2, 3), w=COLS.k(2, 3))
    b.op("DVE", "tensor_scalar", ZTM.ap(), ZTM.ap(), COLS.ap(0, 1), COLS.ap(2, 3),
         ALU.subtract, ALU.mult,
         r=ZTM.k() + COLS.k(0, 1) + COLS.k(2, 3), w=ZTM.k())


def fm_transpose_in(b, cst, SRCTM, sc0, DSTFM, tt):
    for g in range(2):
        bank = 6 + g
        for c in range(4):
            ch = g * 4 + c
            b.transp(bank, SRCTM.ap(sc0 + ch * 128, sc0 + (ch + 1) * 128),
                     cst["EYEF"].ap(),
                     b.pf(bank, c * 128, (c + 1) * 128),
                     r=SRCTM.k(sc0, sc0 + D) + cst["EYEF"].k())
        for c in range(4):
            ch = g * 4 + c
            c0 = ch * T + tt * 128
            b.op("DVE", "tensor_copy", DSTFM.ap(c0, c0 + 128),
                 b.pf(bank, c * 128, (c + 1) * 128), r=b.pk(bank),
                 w=DSTFM.k(c0, c0 + 128) + b.pk(bank))


def dump_bf(b, nm, bi, SRC, ncols=T, nch=NCH):
    if nm not in DBG:
        return
    dst = b.dram["dbg_" + nm]
    r0 = bi * NT * 128
    for ch in range(nch):
        key = ("DBG", nm, bi, ch)
        b.dbg_keys.append(key)
        step = min(ncols, 512)
        for c0 in range(0, ncols, step):
            b.dma("POq", dst[r0 + ch * 128: r0 + (ch + 1) * 128, c0:c0 + step],
                  SRC.ap(ch * ncols + c0, ch * ncols + c0 + step),
                  r=SRC.k(ch * ncols + c0, ch * ncols + c0 + step), w=[key])


def dump_fm(b, nm, bi, SRC):
    if nm not in DBG:
        return
    dst = b.dram["dbg_" + nm]
    r0 = bi * NT * 128
    for ch in range(NCH):
        key = ("DBG", nm, bi, ch)
        b.dbg_keys.append(key)
        b.dma("SOq", dst[r0 + ch * 128: r0 + (ch + 1) * 128, :],
              SRC.ap(ch * T, (ch + 1) * T), r=SRC.k(ch * T, (ch + 1) * T),
              w=[key])


def stage_a(b, cst, bi):
    d = b.dram
    t_off = bi * T

    XTM = b.buf("XTM", F32, D)
    SQA = b.buf("SQA", BF16, D)
    RCOL = b.buf("RCOL", F32, 16)
    RROW1 = b.buf("RROW1", F32, 128)
    RSTDT = b.buf("RSTDT", F32, 128)
    RN = b.cbuf("RN", BF16, NCH, T)
    for tt in range(NT):
        b.dma("Sq", XTM.ap(), d["x0"][t_off + tt * 128: t_off + (tt + 1) * 128, :],
              w=XTM.k())
        b.op("ACT", "activation", SQA.ap(), XTM.ap(), AF.Square,
             r=XTM.k(), w=SQA.k())
        b.op("DVE", "tensor_reduce", RCOL.ap(0, 1), SQA.ap(), axis=AX.X,
             op=ALU.add, r=SQA.k(), w=RCOL.k())
        b.op("ACT", "activation", RCOL.ap(0, 1), RCOL.ap(0, 1), AF.Ln,
             bias=cst["EPSC"].ap(0, 1), scale=1.0 / D,
             r=RCOL.k() + cst["EPSC"].k(), w=RCOL.k())
        b.op("ACT", "activation", RCOL.ap(0, 1), RCOL.ap(0, 1), AF.Exp,
             scale=-0.5, r=RCOL.k(), w=RCOL.k())
        b.transp(2, RCOL.ap(0, 1), cst["EYEF"].ap(), b.pf(2, 0, 128, 0, 1),
                 r=RCOL.k() + cst["EYEF"].k())
        b.op("DVE", "tensor_copy", RROW1.ap(0, 128, 0, 1), b.pf(2, 0, 128, 0, 1),
             r=b.pk(2), w=RROW1.k() + b.pk(2))
        b.mm(3, cst["ROW2"].ap(128, 256, 0, 1), RROW1.ap(0, 128, 0, 1), True, True,
             r=cst["ROW2"].k() + RROW1.k(), out=b.pf(3, 0, 128))
        b.op("DVE", "tensor_copy", RSTDT.ap(), b.pf(3, 0, 128),
             r=b.pk(3), w=RSTDT.k() + b.pk(3))
        for g in range(2):
            bank = 6 + g
            for c in range(4):
                ch = g * 4 + c
                b.transp(bank, XTM.ap(ch * 128, (ch + 1) * 128), cst["EYEF"].ap(),
                         b.pf(bank, c * 128, (c + 1) * 128),
                         r=XTM.k() + cst["EYEF"].k())
            for c in range(4):
                ch = g * 4 + c
                c0 = ch * T + tt * 128
                b.op("DVE", "scalar_tensor_tensor", RN.ap(c0, c0 + 128),
                     b.pf(bank, c * 128, (c + 1) * 128),
                     cst["GAIN"].ap(ch, ch + 1), RSTDT.ap(),
                     op0=ALU.mult, op1=ALU.mult,
                     r=cst["GAIN"].k() + RSTDT.k() + b.pk(bank),
                     w=RN.k(c0, c0 + 128) + b.pk(bank))
    b.freebuf("SQA", "RCOL", "RROW1", "RSTDT", "XTM")
    dump_bf(b, "rn", bi, RN)

    WS0 = b.buf("WSA0", BF16, 8 * D)
    WS1 = b.buf("WSA1", BF16, 8 * D)
    Q = b.cbuf("Q", BF16, NCH, T)
    K = b.cbuf("K", BF16, NCH, T)
    V = b.cbuf("V", BF16, NT, H * 65)
    SQH = b.buf("SQH", F32, 512)
    RNB = b.buf("RNB", F32, 512)
    HROW = b.buf("HROW", F32, 512)
    b.dma("Pq", WS0.ap(), d["wq_a"], w=WS0.k())
    b.dma("Pq", WS1.ap(), d["wk_a"], w=WS1.k())

    def headnorm(DST, GCOL, fc, tc):
        c0 = fc * T + tc * 512
        dk = DST.k(c0, c0 + 512)
        b.op("ACT", "activation", SQH.ap(0, 512), DST.ap(c0, c0 + 512), AF.Square,
             r=dk, w=SQH.k())
        b.mm(2, cst["IND2"].ap(0, 2), SQH.ap(0, 512), True, True,
             r=cst["IND2"].k() + SQH.k(), out=b.pf(2, 0, 512, 0, 2))
        b.op("ACT", "activation", HROW.ap(0, 512, 0, 2), b.pf(2, 0, 512, 0, 2),
             AF.Ln, r=b.pk(2), w=HROW.k() + b.pk(2))
        b.op("ACT", "activation", HROW.ap(0, 512, 0, 2), HROW.ap(0, 512, 0, 2),
             AF.Exp, scale=-0.5, r=HROW.k(), w=HROW.k())
        b.op("DVE", "tensor_scalar", HROW.ap(0, 512, 0, 2), HROW.ap(0, 512, 0, 2),
             1e12, None, ALU.min, r=HROW.k(), w=HROW.k())
        b.mm(3, cst["ROW2"].ap(0, 128, 0, 2), HROW.ap(0, 512, 0, 2), True, True,
             r=cst["ROW2"].k() + HROW.k())
        b.op("DVE", "tensor_copy", RNB.ap(0, 512), b.pf(3, 0, 512), r=b.pk(3),
             w=RNB.k() + b.pk(3))
        b.op("DVE", "scalar_tensor_tensor", DST.ap(c0, c0 + 512),
             DST.ap(c0, c0 + 512), GCOL.ap(fc, fc + 1), RNB.ap(0, 512),
             op0=ALU.mult, op1=ALU.mult, r=dk + GCOL.k() + RNB.k(), w=dk)

    for which, (DST, GCOL, ws) in enumerate(
            [(Q, cst["QG"], WS0), (K, cst["KG"], WS1)]):
        wk = ws.k()
        for fc in range(NCH):
            for tc in range(4):
                bank = tc % 2
                for dc in range(NCH):
                    b.mm(bank, ws.ap(dc * D + fc * 128, dc * D + (fc + 1) * 128),
                         RN.ap(dc * T + tc * 512, dc * T + tc * 512 + 512),
                         dc == 0, dc == NCH - 1,
                         r=wk + RN.k(dc * T + tc * 512, dc * T + tc * 512 + 512))
                c0 = fc * T + tc * 512
                b.op("DVE", "tensor_copy", DST.ap(c0, c0 + 512), b.pf(bank, 0, 512),
                     r=b.pk(bank), w=DST.k(c0, c0 + 512) + b.pk(bank))
                headnorm(DST, GCOL, fc, tc)
        if which == 0:
            b.dma("Pq", WS0.ap(), d["wv_a"], w=WS0.k())

    for tt in range(NT):
        b.op("DVE", "memset", V.ap3(tt * H * 65 + 64, [(65, 16), (1, 1)]), 1.0,
             w=V.k(tt * H * 65, (tt + 1) * H * 65))
        for fcol in range(2):
            bank = fcol % 2
            for dc in range(NCH):
                b.mm(bank, RN.ap(dc * T + tt * 128, dc * T + (tt + 1) * 128),
                     WS0.ap(dc * D + fcol * 512, dc * D + fcol * 512 + 512),
                     dc == 0, dc == NCH - 1,
                     r=WS0.k() + RN.k(dc * T + tt * 128, dc * T + (tt + 1) * 128))
            dst = V.ap3(tt * H * 65 + fcol * 8 * 65, [(65, 8), (1, 64)])
            b.op("DVE", "tensor_copy", dst, b.pf(bank, 0, 512), r=b.pk(bank),
                 w=V.k(tt * H * 65, (tt + 1) * H * 65) + b.pk(bank))
    dump_bf(b, "qh", bi, Q)
    dump_bf(b, "kh", bi, K)
    dump_bf(b, "vv", bi, V, ncols=H * 65, nch=NT)
    b.freebuf("SQH", "RNB", "HROW", "RN", "WSA0")
    b.dma("Pq", WS1.ap(), d["wo_a"], w=WS1.k())

    RES = b.cbuf("RES", F32, NCH, T)
    ICW = 256
    PT = b.buf("PT", BF16, NT * ICW)
    OFM = b.buf("OFM", BF16, NCH * ICW)
    OSC = b.buf("OSC", F32, 256)
    RROW = b.buf("RROW", F32, 256)
    ZTM = b.buf("ZTM", F32, D)
    SQA2 = b.buf("SQA2", BF16, D)
    COLS = b.buf("COLS", F32, 8)
    XT2 = b.buf("XT2", F32, D)
    wo_k = WS1.k()

    TSA = b.cbuf("TSA", BF16, 4, 512)
    for ic in range(T // ICW):
        i0 = ic * ICW

        def norm_part(h):
            qc, prow = h // 2, (h % 2) * 64
            obank = 4 + (h % 2)
            b.op("DVE", "reciprocal", RROW.ap(0, ICW, 0, 1),
                 b.pf(obank, 0, ICW, 64, 65), r=b.pk(obank),
                 w=RROW.k() + b.pk(obank))
            b.mm(6, cst["ROW2"].ap(128, 192, 0, 1), RROW.ap(0, ICW, 0, 1),
                 True, True, r=cst["ROW2"].k() + RROW.k(),
                 out=b.pf(6, 0, ICW, 0, 64))
            b.op("DVE", "tensor_copy", OSC.ap(0, ICW, 0, 64),
                 b.pf(6, 0, ICW, 0, 64), r=b.pk(6), w=OSC.k() + b.pk(6))
            b.op("DVE", "tensor_tensor",
                 OFM.ap(qc * ICW, (qc + 1) * ICW, prow, prow + 64),
                 b.pf(obank, 0, ICW, 0, 64), OSC.ap(0, ICW, 0, 64), ALU.mult,
                 r=OSC.k() + b.pk(obank),
                 w=OFM.k(qc * ICW, (qc + 1) * ICW) + b.pk(obank))

        def av_wave(h, jts):
            obank = 4 + (h % 2)
            for jt in jts:
                b.mm(obank, V.ap(jt * H * 65 + h * 65, jt * H * 65 + h * 65 + 65),
                     PT.ap(jt * ICW, (jt + 1) * ICW), jt == 0, jt == NT - 1,
                     r=V.k(jt * H * 65 + h * 65, jt * H * 65 + (h + 1) * 65)
                       + PT.k(jt * ICW, (jt + 1) * ICW),
                     out=b.pf(obank, 0, ICW, 0, 65))

        def s_wave(h, prs):
            qc, prow = h // 2, (h % 2) * 64
            for p in prs:
                for half in range(2):
                    jt = 2 * p + half
                    sb = p % 4
                    b.mm(sb, K.ap(qc * T + jt * 128, qc * T + (jt + 1) * 128,
                                  prow, prow + 64),
                         Q.ap(qc * T + i0, qc * T + i0 + ICW, prow, prow + 64),
                         True, True,
                         r=K.k(qc * T + jt * 128, qc * T + (jt + 1) * 128)
                           + Q.k(qc * T + i0, qc * T + i0 + ICW),
                         out=b.pf(sb, half * ICW, (half + 1) * ICW))

        def act_wave(prs):
            for p in prs:
                sb = p % 4
                b.op("ACT", "activation", TSA.ap((p % 4) * 512, (p % 4 + 1) * 512),
                     b.pf(sb, 0, 2 * ICW), AF.Tanh, scale=1.0 / 50.0,
                     r=b.pk(sb), w=TSA.k((p % 4) * 512, (p % 4 + 1) * 512) + b.pk(sb))
            for p in prs:
                b.op("ACT", "activation", PT.ap(2 * p * ICW, 2 * (p + 1) * ICW),
                     TSA.ap((p % 4) * 512, (p % 4 + 1) * 512), AF.Exp, scale=6.25,
                     r=TSA.k((p % 4) * 512, (p % 4 + 1) * 512),
                     w=PT.k(2 * p * ICW, 2 * (p + 1) * ICW))

        for h in range(H):
            s_wave(h, range(0, 4))
            if h > 0:
                av_wave(h - 1, range(0, 8))
            act_wave(range(0, 4))
            s_wave(h, range(4, 8))
            if h > 0:
                av_wave(h - 1, range(8, 16))
            act_wave(range(4, 8))
            if h > 0:
                norm_part(h - 1)
        av_wave(H - 1, range(0, 16))
        norm_part(H - 1)
        for t2 in range(2):
            tt = ic * 2 + t2
            b.dma("Sq", XT2.ap(),
                  d["x0"][t_off + tt * 128: t_off + (tt + 1) * 128, :],
                  w=XT2.k())
            for dcol in range(2):
                bank = 2 + dcol
                for c in range(NCH):
                    b.mm(bank, OFM.ap(c * ICW + t2 * 128, c * ICW + t2 * 128 + 128),
                         WS1.ap(c * D + dcol * 512, c * D + dcol * 512 + 512),
                         c == 0, c == NCH - 1,
                         r=OFM.k(c * ICW, (c + 1) * ICW) + wo_k)
                b.op("DVE", "tensor_tensor", ZTM.ap(dcol * 512, dcol * 512 + 512),
                     b.pf(bank, 0, 512),
                     XT2.ap(dcol * 512, dcol * 512 + 512), ALU.add,
                     r=XT2.k() + b.pk(bank),
                     w=ZTM.k(dcol * 512, dcol * 512 + 512) + b.pk(bank))
            if "z0" in DBG:
                key = ("DBG", "z0", bi, tt)
                b.dbg_keys.append(key)
                b.dma("SOq", b.dram["dbg_z0"][bi * NT * 128 + tt * 128:
                                             bi * NT * 128 + (tt + 1) * 128, 0:D],
                      ZTM.ap(), r=ZTM.k(), w=[key])
            layernorm_tm(b, cst, ZTM, SQA2, COLS)
            fm_transpose_in(b, cst, ZTM, 0, RES, tt)
    b.freebuf("PT", "OFM", "OSC", "RROW", "ZTM", "SQA2", "COLS", "XT2", "TSA",
              "Q", "K", "V", "WSA1")
    dump_fm(b, "x1", bi, RES)


def stage_b(b, cst, bi):
    d = b.dram
    RES = b.bufs["RES"]

    COS = b.buf("COS", F32, 512)
    SIN = b.buf("SIN", F32, 512)
    b.dma("Sq", COS.ap(), d["c_cos"], w=COS.k())
    b.dma("Sq", SIN.ap(), d["c_sin"], w=SIN.k())
    cst = dict(cst, COS=COS, SIN=SIN)
    SQ = b.buf("SQB0", F32, 512)
    ROWR = b.buf("ROWRB", F32, T)
    RSTDB = b.buf("RSTDBB", F32, T)
    RN1 = b.cbuf("RN1", BF16, NCH, T)
    rmsnorm_rows(b, cst, RES, SQ, ROWR, 2, 1)
    bcast_row(b, cst, ROWR, RSTDB, 3)
    for ch in range(NCH):
        for tc in range(4):
            c0 = ch * T + tc * 512
            b.op("DVE", "scalar_tensor_tensor",
                 RN1.ap(c0, c0 + 512), RES.ap(c0, c0 + 512),
                 cst["GAIN"].ap(8 + ch, 9 + ch), RSTDB.ap(tc * 512, tc * 512 + 512),
                 op0=ALU.mult, op1=ALU.mult,
                 r=RES.k(c0, c0 + 512) + cst["GAIN"].k()
                   + RSTDB.k(tc * 512, tc * 512 + 512),
                 w=RN1.k(c0, c0 + 512))
    b.freebuf("SQB0", "ROWRB", "RSTDBB")

    WKV = b.buf("WKV", BF16, 8 * L)
    KV = b.buf("KV", BF16, 2 * T)
    b.dma("Pq", WKV.ap(), d["wkv_f"], w=WKV.k())
    for lc in range(2):
        for tc in range(4):
            bank = tc % 2
            for dc in range(NCH):
                b.mm(bank, WKV.ap(dc * L + lc * 128, dc * L + (lc + 1) * 128),
                     RN1.ap(dc * T + tc * 512, dc * T + tc * 512 + 512),
                     dc == 0, dc == NCH - 1,
                     r=WKV.k() + RN1.k(dc * T + tc * 512, dc * T + tc * 512 + 512))
            c0 = lc * T + tc * 512
            b.op("DVE", "tensor_scalar", KV.ap(c0, c0 + 512), b.pf(bank, 0, 512),
                 0.25, None, ALU.mult, r=b.pk(bank), w=KV.k(c0, c0 + 512) + b.pk(bank))
    b.freebuf("WKV")

    WQF = b.buf("WQF", BF16, 8 * D)
    WKU = b.buf("WKU", BF16, 2 * D)
    WVU = b.buf("WVU", BF16, 2 * D)
    Q = b.cbuf("Q", BF16, NCH, T)
    QST = b.buf("QST", BF16, 2 * D)
    TMP = b.buf("TMPR", F32, 4 * 512)
    b.dma("Pq", WQF.ap(), d["wq_f"], w=WQF.k())
    b.dma("Pq", WKU.ap(), d["wk_up"], w=WKU.k())
    b.dma("Pq", WVU.ap(), d["wv_up"], w=WVU.k())

    def rope_tile(tt, sl):
        base = sl * D
        kst = QST.k(base, base + D)
        x1v = QST.ap3(base, [(64, 16), (1, 32)])
        x2v = QST.ap3(base + 32, [(64, 16), (1, 32)])
        cosb = cst["COS"].ap3(tt * 32, [(0, 16), (1, 32)])
        sinb = cst["SIN"].ap3(tt * 32, [(0, 16), (1, 32)])
        tvs = [TMP.ap3(i * 512, [(32, 16), (1, 32)]) for i in range(4)]
        tks = [TMP.k(i * 512, (i + 1) * 512) for i in range(4)]
        b.op("DVE", "tensor_tensor", tvs[0], x1v, cosb, ALU.mult,
             r=kst + cst["COS"].k(), w=tks[0])
        b.op("DVE", "tensor_tensor", tvs[1], x2v, sinb, ALU.mult,
             r=kst + cst["SIN"].k(), w=tks[1])
        b.op("DVE", "tensor_tensor", tvs[2], x1v, sinb, ALU.mult,
             r=kst + cst["SIN"].k(), w=tks[2])
        b.op("DVE", "tensor_tensor", tvs[3], x2v, cosb, ALU.mult,
             r=kst + cst["COS"].k(), w=tks[3])
        b.op("DVE", "tensor_tensor", x1v, tvs[0], tvs[1], ALU.subtract,
             r=tks[0] + tks[1], w=kst)
        b.op("DVE", "tensor_tensor", x2v, tvs[2], tvs[3], ALU.add,
             r=tks[2] + tks[3], w=kst)

    def fm_transpose_bf16(SRC, sc0, DSTFM, tt):
        for g in range(2):
            bank = 6 + g
            for c in range(4):
                ch = g * 4 + c
                b.transp(bank, SRC.ap(sc0 + ch * 128, sc0 + (ch + 1) * 128),
                         cst["EYEB"].ap(), b.pb(bank, c * 128, (c + 1) * 128),
                         r=SRC.k(sc0, sc0 + D) + cst["EYEB"].k())
            for c in range(4):
                ch = g * 4 + c
                c0 = ch * T + tt * 128
                b.op("DVE", "tensor_copy", DSTFM.ap(c0, c0 + 128),
                     b.pb(bank, c * 128, (c + 1) * 128), r=b.pk(bank),
                     w=DSTFM.k(c0, c0 + 128) + b.pk(bank))

    for tt in range(NT):
        sl = tt % 2
        for fcol in range(2):
            bank = fcol
            for dc in range(NCH):
                b.mm(bank, RN1.ap(dc * T + tt * 128, dc * T + (tt + 1) * 128),
                     WQF.ap(dc * D + fcol * 512, dc * D + fcol * 512 + 512),
                     dc == 0, dc == NCH - 1,
                     r=WQF.k() + RN1.k(dc * T + tt * 128, dc * T + (tt + 1) * 128))
            b.op("DVE", "tensor_copy",
                 QST.ap(sl * D + fcol * 512, sl * D + fcol * 512 + 512),
                 b.pf(bank, 0, 512), r=b.pk(bank),
                 w=QST.k(sl * D + fcol * 512, sl * D + fcol * 512 + 512) + b.pk(bank))
        rope_tile(tt, sl)
        fm_transpose_bf16(QST, sl * D, Q, tt)
    b.freebuf("RN1", "WQF")
    K = b.cbuf("K", BF16, NCH, T)
    for tt in range(NT):
        sl = tt % 2
        for fcol in range(2):
            bank = fcol
            for lc in range(2):
                b.mm(bank, KV.ap(lc * T + tt * 128, lc * T + (tt + 1) * 128),
                     WKU.ap(lc * D + fcol * 512, lc * D + fcol * 512 + 512),
                     lc == 0, lc == 1,
                     r=WKU.k() + KV.k(lc * T + tt * 128, lc * T + (tt + 1) * 128))
            b.op("DVE", "tensor_copy",
                 QST.ap(sl * D + fcol * 512, sl * D + fcol * 512 + 512),
                 b.pf(bank, 0, 512), r=b.pk(bank),
                 w=QST.k(sl * D + fcol * 512, sl * D + fcol * 512 + 512) + b.pk(bank))
        rope_tile(tt, sl)
        fm_transpose_bf16(QST, sl * D, K, tt)
    b.freebuf("WKU")
    V = b.cbuf("V", BF16, NT, H * 65)
    for tt in range(NT):
        b.op("DVE", "memset", V.ap3(tt * H * 65 + 64, [(65, 16), (1, 1)]), 1.0,
             w=V.k(tt * H * 65, (tt + 1) * H * 65))
        for fcol in range(2):
            bank = fcol
            for lc in range(2):
                b.mm(bank, KV.ap(lc * T + tt * 128, lc * T + (tt + 1) * 128),
                     WVU.ap(lc * D + fcol * 512, lc * D + fcol * 512 + 512),
                     lc == 0, lc == 1,
                     r=WVU.k() + KV.k(lc * T + tt * 128, lc * T + (tt + 1) * 128))
            dst = V.ap3(tt * H * 65 + fcol * 8 * 65, [(65, 8), (1, 64)])
            b.op("DVE", "tensor_copy", dst, b.pf(bank, 0, 512), r=b.pk(bank),
                 w=V.k(tt * H * 65, (tt + 1) * H * 65) + b.pk(bank))
    b.freebuf("QST", "TMPR", "WVU", "KV", "COS", "SIN")

    P = b.buf("P", BF16, T)
    PT = b.buf("PTB", BF16, T)
    OFMB = b.cbuf("OFMB", BF16, NCH * 4, 512)
    RROW = b.buf("RROWB", F32, 128)
    OSC = b.buf("OSCB", F32, 128)
    CB = b.buf("CB", F32, 8)

    RROW = b.bufs["RROWB"]
    OSC = b.bufs["OSCB"]

    def stg_s(it, h):
        qc, prow = h // 2, (h % 2) * 64
        jmax = (it + 1) * 128
        njc = (jmax + 511) // 512
        for jc in range(njc):
            jw = min(512, jmax - jc * 512)
            b.mm(jc, Q.ap(qc * T + it * 128, qc * T + (it + 1) * 128,
                          prow, prow + 64),
                 K.ap(qc * T + jc * 512, qc * T + jc * 512 + jw, prow, prow + 64),
                 True, True,
                 r=Q.k(qc * T + it * 128, qc * T + (it + 1) * 128)
                   + K.k(qc * T + jc * 512, qc * T + jc * 512 + jw),
                 out=b.pf(jc, 0, jw))
        jcd = (it * 128) // 512
        coff = it * 128 - jcd * 512
        b.op("DVE", "tensor_tensor", b.pf(jcd, coff, coff + 128),
             b.pf(jcd, coff, coff + 128), cst["MASK"].ap(), ALU.add,
             r=cst["MASK"].k() + b.pk(jcd), w=b.pk(jcd))
        for jc in range(njc):
            jw = min(512, jmax - jc * 512)
            b.op("DVE", "tensor_reduce", CB.ap(jc, jc + 1), b.pf(jc, 0, jw),
                 axis=AX.X, op=ALU.max, r=b.pk(jc), w=CB.k() + b.pk(jc))
        for step in range(1, njc):
            b.op("DVE", "tensor_tensor", CB.ap(0, 1), CB.ap(0, 1),
                 CB.ap(step, step + 1), ALU.max, r=CB.k(), w=CB.k())
        b.op("DVE", "tensor_scalar", CB.ap(4, 5), CB.ap(0, 1), -0.125, None,
             ALU.mult, r=CB.k(), w=CB.k())

    def stg_exp(it, h):
        jmax = (it + 1) * 128
        njc = (jmax + 511) // 512
        for jc in range(njc):
            jw = min(512, jmax - jc * 512)
            b.op("ACT", "activation", P.ap(jc * 512, jc * 512 + jw),
                 b.pf(jc, 0, jw), AF.Exp, scale=0.125, bias=CB.ap(4, 5),
                 r=CB.k() + b.pk(jc), w=P.k(jc * 512, jc * 512 + jw) + b.pk(jc))

    def stg_transp(it, h):
        for g in range((it + 1 + 3) // 4):
            gw = min(4, it + 1 - g * 4)
            for c in range(gw):
                jt = g * 4 + c
                b.transp(6, P.ap(jt * 128, (jt + 1) * 128), cst["EYEB"].ap(),
                         b.pb(6, c * 128, (c + 1) * 128),
                         r=P.k(jt * 128, (jt + 1) * 128) + cst["EYEB"].k())
            b.op("DVE", "tensor_copy", PT.ap(g * 512, g * 512 + gw * 128),
                 b.pb(6, 0, gw * 128), r=b.pk(6),
                 w=PT.k(g * 512, g * 512 + gw * 128) + b.pk(6))

    def stg_av(it, h, ob):
        qc, prow = h // 2, (h % 2) * 64
        for jt in range(it + 1):
            b.mm(ob, V.ap(jt * H * 65 + h * 65, jt * H * 65 + h * 65 + 65),
                 PT.ap(jt * 128, (jt + 1) * 128), jt == 0, jt == it,
                 r=V.k(jt * H * 65, (jt + 1) * H * 65)
                   + PT.k(jt * 128, (jt + 1) * 128),
                 out=b.pf(ob, 0, 128, 0, 65))
        rinv(b, cst, RROW.ap(0, 128, 0, 1), b.pf(ob, 0, 128, 64, 65),
             RROW.k(), b.pk(ob))
        b.mm(4, cst["ROW2"].ap(128, 192, 0, 1), RROW.ap(0, 128, 0, 1),
             True, True, r=cst["ROW2"].k() + RROW.k(),
             out=b.pf(4, 0, 128, 0, 64))
        b.op("ACT", "copy", OSC.ap(0, 128, 0, 64), b.pf(4, 0, 128, 0, 64),
             r=b.pk(4), w=OSC.k() + b.pk(4))
        b.op("DVE", "tensor_tensor",
             OFMB.ap(qc * T + it * 128, qc * T + (it + 1) * 128, prow, prow + 64),
             b.pf(ob, 0, 128, 0, 64), OSC.ap(0, 128, 0, 64), ALU.mult,
             r=OSC.k() + b.pk(ob),
             w=OFMB.k(qc * T + it * 128, qc * T + (it + 1) * 128) + b.pk(ob))

    jobs = [(it, h) for it in range(NT) for h in range(H)]
    for k in range(len(jobs) + 3):
        if k >= 2 and k - 2 < len(jobs):
            stg_transp(*jobs[k - 2])
        if k >= 1 and k - 1 < len(jobs):
            stg_exp(*jobs[k - 1])
        if k >= 3:
            stg_av(*jobs[k - 3], 5 if (k - 3) % 2 == 0 else 7)
        if k < len(jobs):
            stg_s(*jobs[k])
    b.freebuf("P", "PTB", "RROWB", "OSCB", "CB", "Q", "K", "V")

    WOF = b.buf("WOF", BF16, 8 * D)
    X2B = b.cbuf("X2B", BF16, NCH * 4, 512)
    X2SC = b.buf("X2SC", F32, 512)
    SQB = b.buf("SQB", F32, 512)
    ROWR2 = b.buf("ROWR2", F32, T)
    b.dma("Pq", WOF.ap(), d["wo_f"], w=WOF.k())
    for grp in range(4):
        g0 = grp * 512
        for dch in range(NCH):
            bank = 2 + (dch % 2)
            for c in range(NCH):
                b.mm(bank, WOF.ap(c * D + dch * 128, c * D + (dch + 1) * 128),
                     OFMB.ap(c * T + g0, c * T + g0 + 512),
                     c == 0, c == NCH - 1,
                     r=WOF.k() + OFMB.k(c * T + g0, c * T + g0 + 512))
            c0 = dch * T + g0
            b.op("DVE", "tensor_tensor", X2SC.ap(0, 512), b.pf(bank, 0, 512),
                 RES.ap(c0, c0 + 512), ALU.add,
                 r=RES.k(c0, c0 + 512) + b.pk(bank), w=X2SC.k() + b.pk(bank))
            b.op("DVE", "tensor_copy", X2B.ap(c0, c0 + 512), X2SC.ap(0, 512),
                 r=X2SC.k(), w=X2B.k(c0, c0 + 512))
            b.op("ACT", "activation", SQB.ap(0, 512), X2SC.ap(0, 512), AF.Square,
                 r=X2SC.k(), w=SQB.k())
            b.mm(7, cst["ONESC"].ap(0, 1), SQB.ap(0, 512), dch == 0,
                 dch == NCH - 1, r=cst["ONESC"].k() + SQB.k(),
                 out=b.pf(7, 0, 512, 0, 1))
            b.op("DVE", "tensor_tensor", RES.ap(c0, c0 + 512), RES.ap(c0, c0 + 512),
                 X2SC.ap(0, 512), ALU.add,
                 r=RES.k(c0, c0 + 512) + X2SC.k(), w=RES.k(c0, c0 + 512))
        b.op("ACT", "activation", ROWR2.ap(g0, g0 + 512, 0, 1),
             b.pf(7, 0, 512, 0, 1), AF.Ln,
             bias=cst["EPSC"].ap(1, 2, 0, 1), scale=1.0 / D,
             r=cst["EPSC"].k() + b.pk(7), w=ROWR2.k(g0, g0 + 512) + b.pk(7))
        b.op("ACT", "activation", ROWR2.ap(g0, g0 + 512, 0, 1),
             ROWR2.ap(g0, g0 + 512, 0, 1), AF.Exp, scale=-0.5,
             r=ROWR2.k(g0, g0 + 512), w=ROWR2.k(g0, g0 + 512))
    b.freebuf("WOF", "OFMB", "X2SC", "SQB")
    RSTDB2 = b.buf("RSTDB2", F32, T)
    bcast_row(b, cst, ROWR2, RSTDB2, 3)
    for ch in range(NCH):
        for tc in range(4):
            c0 = ch * T + tc * 512
            b.op("DVE", "scalar_tensor_tensor", X2B.ap(c0, c0 + 512),
                 X2B.ap(c0, c0 + 512), cst["GAIN"].ap(16 + ch, 17 + ch),
                 RSTDB2.ap(tc * 512, tc * 512 + 512), op0=ALU.mult, op1=ALU.mult,
                 r=X2B.k(c0, c0 + 512) + cst["GAIN"].k()
                   + RSTDB2.k(tc * 512, tc * 512 + 512),
                 w=X2B.k(c0, c0 + 512))
    b.freebuf("ROWR2", "RSTDB2")
    dump_fm(b, "x2s", bi, RES)
    dump_bf(b, "rn2", bi, X2B)


def stage_c(b, cst, bi):
    d = b.dram
    RES = b.bufs["RES"]
    RN2 = b.bufs["X2B"]
    C = 1024
    ZBASE = 8 * C

    WRS = b.buf("WRS", BF16, 8 * E)
    I128 = b.buf("I128", F32, T)
    R16 = b.buf("R16", F32, 32)
    SROWS = b.buf("SROWS", F32, T)
    MR = b.buf("MR", F32, T)
    RNK = b.buf("RNK", F32, T)
    ROWS4 = b.buf("ROWS4", F32, 4 * T)
    ESC = b.buf("ESC", F32, 16)
    DCOL = b.buf("DCOL", mybir.dt.uint32, 16)
    PRB = b.buf("PRB", F32, 8)
    EQ = b.buf("EQ", F32, 8)
    CC = b.buf("CC", F32, 8)
    SMAT = b.buf("SMAT", F32, 16)
    SM2 = b.buf("SM2", F32, 16)
    b.dma("Pq", WRS.ap(), d["wr"], w=WRS.k())
    b.dma("Sq", I128.ap(0, T, 0, 1), d["c_i128"], w=I128.k())
    b.dma("Sq", R16.ap(0, 32, 0, 16), d["c_r16"], w=R16.k())
    b.op("DVE", "memset", SMAT.ap(), 0.0, w=SMAT.k())
    b.op("DVE", "memset", SM2.ap(), 0.0, w=SM2.k())
    for tt in range(NT):
        for dc in range(NCH):
            b.mm(6, RN2.ap(dc * T + tt * 128, dc * T + (tt + 1) * 128),
                 WRS.ap(dc * E, (dc + 1) * E), dc == 0, dc == NCH - 1,
                 r=WRS.k() + RN2.k(dc * T + tt * 128, dc * T + (tt + 1) * 128),
                 out=b.pf(6, 0, E))
        b.op("DVE", "tensor_tensor", PRB.ap(), b.pf(6, 0, E), cst["RB8"].ap(),
             ALU.add, r=cst["RB8"].k() + b.pk(6), w=PRB.k() + b.pk(6))
        b.op("DVE", "tensor_scalar", PRB.ap(), PRB.ap(), -50.0, 50.0,
             ALU.max, ALU.min, r=PRB.k(), w=PRB.k())
        b.op("ACT", "activation", PRB.ap(), PRB.ap(), AF.Sigmoid,
             r=PRB.k(), w=PRB.k())
        b.op("DVE", "tensor_reduce", CC.ap(0, 1), PRB.ap(), axis=AX.X, op=ALU.max,
             r=PRB.k(), w=CC.k(0, 1))
        b.op("DVE", "tensor_scalar", CC.ap(0, 1), CC.ap(0, 1), 1e-8, 1.0,
             ALU.max, ALU.min, r=CC.k(0, 1), w=CC.k(0, 1))
        b.op("DVE", "tensor_scalar", EQ.ap(), PRB.ap(), CC.ap(0, 1), None,
             ALU.is_equal, r=PRB.k() + CC.k(0, 1), w=EQ.k())
        b.op("DVE", "tensor_scalar", EQ.ap(), EQ.ap(), -1e9, 1e9,
             ALU.mult, ALU.add, r=EQ.k(), w=EQ.k())
        b.op("DVE", "tensor_tensor", EQ.ap(), EQ.ap(), cst["IO8"].ap(), ALU.add,
             r=EQ.k() + cst["IO8"].k(), w=EQ.k())
        b.op("DVE", "tensor_reduce", CC.ap(1, 2), EQ.ap(), axis=AX.X, op=ALU.min,
             r=EQ.k(), w=CC.k(1, 2))
        b.op("DVE", "tensor_scalar", CC.ap(2, 3), CC.ap(0, 1), 0.5, 1e-8,
             ALU.add, ALU.add, r=CC.k(0, 1), w=CC.k(2, 3))
        rinv(b, cst, CC.ap(2, 3), CC.ap(2, 3), CC.k(2, 3), CC.k(2, 3))
        b.op("DVE", "tensor_scalar", SMAT.ap(8, 9), CC.ap(2, 3), 0.5, None,
             ALU.mult, r=CC.k(2, 3), w=SMAT.k())
        b.op("DVE", "tensor_tensor", ESC.ap(tt, tt + 1), CC.ap(0, 1), CC.ap(2, 3),
             ALU.mult, r=CC.k(0, 1) + CC.k(2, 3), w=ESC.k())
        for e in range(E):
            b.op("DVE", "tensor_scalar", SM2.ap(e, e + 1), CC.ap(1, 2),
                 float(e), None, ALU.is_equal, r=CC.k(1, 2), w=SM2.k())
        b.op("DVE", "tensor_copy", SM2.ap(8, 9), CC.ap(1, 2),
             r=CC.k(1, 2), w=SM2.k())
        b.transp(7, SMAT.ap(), cst["EYEF"].ap(), b.pf(7, 0, 128, 0, 16),
                 r=SMAT.k() + cst["EYEF"].k())
        b.op("DVE", "tensor_copy", SROWS.ap(tt * 128, (tt + 1) * 128, 0, 16),
             b.pf(7, 0, 128, 0, 16), r=b.pk(7),
             w=SROWS.k(tt * 128, (tt + 1) * 128) + b.pk(7))
        b.transp(7, SM2.ap(), cst["EYEF"].ap(), b.pf(7, 128, 256, 0, 16),
                 r=SM2.k() + cst["EYEF"].k())
        b.op("DVE", "tensor_copy", MR.ap(tt * 128, (tt + 1) * 128, 0, 16),
             b.pf(7, 128, 256, 0, 16), r=b.pk(7),
             w=MR.k(tt * 128, (tt + 1) * 128) + b.pk(7))
    b.freebuf("WRS", "PRB", "EQ", "SMAT", "SM2")

    ones16 = bass.AP(cst["ONESC"].ap().tensor, cst["ONESC"].ap().offset,
                     [list(cst["ONESC"].ap().ap[0])[:1] + [16], [0, T]])
    b.op("DVE", "tensor_tensor_scan", RNK.ap(0, T, 0, 16), ones16,
         MR.ap(0, T, 0, 16), 0.0, op0=ALU.mult, op1=ALU.add,
         r=cst["ONESC"].k() + MR.k(0, T), w=RNK.k(0, T))
    b.op("DVE", "tensor_tensor", RNK.ap(0, T, 0, 16), RNK.ap(0, T, 0, 16),
         MR.ap(0, T, 0, 16), ALU.subtract, r=RNK.k(0, T) + MR.k(0, T),
         w=RNK.k(0, T))
    for tc in range(4):
        b.mm(7, R16.ap(8, 9, 0, 16), MR.ap(tc * 512, tc * 512 + 512, 0, 16),
             True, True, r=R16.k() + MR.k(tc * 512, tc * 512 + 512),
             out=b.pf(7, 0, 512, 0, 1))
        b.op("DVE", "tensor_copy", ROWS4.ap(tc * 512, tc * 512 + 512, 0, 1),
             b.pf(7, 0, 512, 0, 1), r=b.pk(7),
             w=ROWS4.k(tc * 512, tc * 512 + 512) + b.pk(7))
    b.op("DVE", "tensor_tensor", MR.ap(0, T, 0, 16), MR.ap(0, T, 0, 16),
         RNK.ap(0, T, 0, 16), ALU.mult, r=MR.k(0, T) + RNK.k(0, T),
         w=MR.k(0, T))
    for tc in range(4):
        b.mm(7, R16.ap(16, 17, 0, 16), MR.ap(tc * 512, tc * 512 + 512, 0, 16),
             True, True, r=R16.k() + MR.k(tc * 512, tc * 512 + 512),
             out=b.pf(7, 0, 512, 0, 1))
        b.op("DVE", "tensor_copy", ROWS4.ap(T + tc * 512, T + tc * 512 + 512, 0, 1),
             b.pf(7, 0, 512, 0, 1), r=b.pk(7),
             w=ROWS4.k(T + tc * 512, T + tc * 512 + 512) + b.pk(7))
    r_sel = ROWS4.ap(0, T, 0, 1)
    r_rnk = ROWS4.ap(T, 2 * T, 0, 1)
    r_d = ROWS4.ap(2 * T, 3 * T, 0, 1)
    r_t = ROWS4.ap(3 * T, 4 * T, 0, 1)
    kS, kR = ROWS4.k(0, T), ROWS4.k(T, 2 * T)
    kD, kT = ROWS4.k(2 * T, 3 * T), ROWS4.k(3 * T, 4 * T)
    b.op("DVE", "scalar_tensor_tensor", r_d, r_sel, float(C), r_rnk,
         op0=ALU.mult, op1=ALU.add, r=kS + kR, w=kD)
    b.op("DVE", "tensor_scalar", r_t, r_rnk, float(C), None, ALU.is_lt,
         r=kR, w=kT)
    b.op("DVE", "tensor_tensor", r_d, r_d, r_t, ALU.mult, r=kD + kT, w=kD)
    b.op("DVE", "tensor_scalar", r_t, r_t, -1.0, 1.0, ALU.mult, ALU.add,
         r=kT, w=kT)
    b.op("DVE", "tensor_tensor", r_t, r_t, I128.ap(0, T, 0, 1), ALU.mult,
         r=kT + I128.k(), w=kT)
    b.op("DVE", "tensor_tensor", r_d, r_d, r_t, ALU.add, r=kD + kT, w=kD)
    for tt in range(NT):
        b.transp(7, ROWS4.ap(2 * T + tt * 128, 2 * T + (tt + 1) * 128, 0, 1),
                 cst["EYEF"].ap(0, 1, 0, 1), b.pf(7, 0, 1, 0, 128),
                 r=kD + cst["EYEF"].k())
        b.op("DVE", "tensor_copy", DCOL.ap(tt, tt + 1), b.pf(7, 0, 1, 0, 128),
             r=b.pk(7), w=DCOL.k() + b.pk(7))
    b.freebuf("MR", "RNK", "ROWS4", "I128", "R16")

    EXP9 = b.buf("EXP9", F32, 9 * 128)
    b.dma("Sq", EXP9.ap(0, 9 * 128, 0, 16), d["c_exp9"], w=EXP9.k())
    SBC = b.buf("SBC", BF16, T)
    for tc in range(4):
        b.mm(7, EXP9.ap(8 * 128, 9 * 128, 0, 16),
             SROWS.ap(tc * 512, tc * 512 + 512, 0, 16), True, True,
             r=EXP9.k() + SROWS.k(tc * 512, tc * 512 + 512))
        b.op("DVE", "tensor_copy", SBC.ap(tc * 512, tc * 512 + 512),
             b.pf(7, 0, 512), r=b.pk(7),
             w=SBC.k(tc * 512, tc * 512 + 512) + b.pk(7))
    b.freebuf("SROWS", "EXP9")

    if bi == 0:
        ZT = b.buf("ZT", BF16, D)
        b.op("DVE", "memset", ZT.ap(), 0.0, w=ZT.k())
        for rr in range(65):
            b.dma("SOq", d["RG"][rr * 128:(rr + 1) * 128, :], ZT.ap(),
                  r=ZT.k(), w=[("RG",)])
        b.dma("SOq", d["OGT"][ZBASE:ZBASE + 128, :], ZT.ap(), r=ZT.k(),
              w=[("OGTZ",)])
        b.freebuf("ZT")
    TMT = b.buf("TMT", BF16, 2 * D)
    for tt in range(NT):
        sl = tt % 2
        for g in range(2):
            bank = 6 + g
            for c in range(4):
                ch = g * 4 + c
                b.transp(bank, RN2.ap(ch * T + tt * 128, ch * T + (tt + 1) * 128),
                         cst["EYEB"].ap(), b.pb(bank, c * 128, (c + 1) * 128),
                         r=RN2.k(ch * T + tt * 128, ch * T + (tt + 1) * 128)
                           + cst["EYEB"].k())
            b.op("DVE", "tensor_copy",
                 TMT.ap(sl * D + g * 512, sl * D + (g + 1) * 512),
                 b.pb(bank, 0, 512), r=b.pk(bank),
                 w=TMT.k(sl * D + g * 512, sl * D + (g + 1) * 512) + b.pk(bank))
        b.idma(d["RG"], DCOL.ap(tt, tt + 1), TMT.ap(sl * D, (sl + 1) * D), None,
               r=TMT.k(sl * D, (sl + 1) * D) + DCOL.k(), w=[("RG",)])
    b.freebuf("TMT")

    G0 = b.cbuf("G0", BF16, 8, 512)
    G1 = b.cbuf("G1", BF16, 8, 512)
    USC = b.buf("USC", BF16, 512)
    WSL = [b.buf(f"WC{i}", BF16, 8 * 1024) for i in range(4)]
    XG = XGT = OUTC = None
    gbufs = [G0, G1]

    steps = [(8, 0), (8, 1)] + [(e, F_) for e in range(E) for F_ in range(2)]

    def w_src(e, F_, which):
        nm = ["ew1", "ew3", "ew2"][which] if e < E else ["sw1", "sw3", "sw2"][which]
        ap = d[nm]
        if e < E:
            return ap[e:e + 1, F_:F_ + 1]
        return ap[F_:F_ + 1]

    def load_step(si):
        e, F_ = steps[si]
        for j in range(3):
            sl = WSL[(si * 3 + j) % 4]
            b.dma("Pq", sl.ap(), w_src(e, F_, j), w=sl.k())

    load_step(0)
    pend_w2 = []

    def emit_w2_shared(tc, gsl, w2s):
        g = gbufs[gsl]
        for dch in range(NCH):
            bank = 4 + (dch % 2)
            for fc in range(NCH):
                b.mm(bank, w2s.ap(fc * 1024 + dch * 128, fc * 1024 + (dch + 1) * 128),
                     g.ap(fc * 512, (fc + 1) * 512), fc == 0, fc == NCH - 1,
                     r=w2s.k() + g.k(fc * 512, (fc + 1) * 512))
            c0 = dch * T + tc * 512
            b.op("DVE", "tensor_tensor", RES.ap(c0, c0 + 512), b.pf(bank, 0, 512),
                 RES.ap(c0, c0 + 512), ALU.add,
                 r=RES.k(c0, c0 + 512) + b.pk(bank),
                 w=RES.k(c0, c0 + 512) + b.pk(bank))

    def emit_w2_routed(e, F_, tc, gsl, w2s):
        g = gbufs[gsl]
        for st in range(4):
            slot0 = tc * 512 + st * 128
            sti = slot0 // 128
            for dcol in range(2):
                bank = 4 + (st % 2)
                for fc in range(NCH):
                    b.mm(bank, g.ap(fc * 512 + st * 128, fc * 512 + (st + 1) * 128),
                         w2s.ap(fc * 1024 + dcol * 512, fc * 1024 + dcol * 512 + 512),
                         fc == 0, fc == NCH - 1,
                         r=w2s.k(fc * 512, (fc + 1) * 512)
                           + g.k(fc * 512, (fc + 1) * 512))
                oc0 = sti * D + dcol * 512
                if F_ == 0:
                    b.op("DVE", "tensor_copy", OUTC.ap(oc0, oc0 + 512),
                         b.pf(bank, 0, 512), r=b.pk(bank),
                         w=OUTC.k(oc0, oc0 + 512) + b.pk(bank))
                else:
                    b.op("DVE", "tensor_tensor", OUTC.ap(oc0, oc0 + 512),
                         b.pf(bank, 0, 512), OUTC.ap(oc0, oc0 + 512), ALU.add,
                         r=OUTC.k(oc0, oc0 + 512) + b.pk(bank),
                         w=OUTC.k(oc0, oc0 + 512) + b.pk(bank))
            if F_ == 1:
                row0 = e * C + slot0
                b.dma("Pq", d["OGT"][row0:row0 + 128, :], OUTC.ap(sti * D, (sti + 1) * D),
                      r=OUTC.k(sti * D, (sti + 1) * D), w=[("OGT", e, slot0)])

    for si, (e, F_) in enumerate(steps):
        w1s = WSL[(si * 3 + 0) % 4]
        w3s = WSL[(si * 3 + 1) % 4]
        w2s = WSL[(si * 3 + 2) % 4]
        shared = e == 8
        if e == 0 and F_ == 0:
            b.freebuf("X2B")
            XG = [b.cbuf("XGa", BF16, 8, C), b.cbuf("XGb", BF16, 8, C)]
            XGT = b.buf("XGT", BF16, 2 * D)
            OUTC = b.cbuf("OUTC", BF16, 8, D)
        def xg_fill(ee):
            xg = XG[ee % 2]
            for st in range(8):
                sl2 = st % 2
                b.dma("Sq", XGT.ap(sl2 * D, (sl2 + 1) * D),
                      d["RG"][ee * C + st * 128: ee * C + (st + 1) * 128, :],
                      r=[("RG",)], w=XGT.k(sl2 * D, (sl2 + 1) * D))
                for g2 in range(2):
                    bank = 6 + g2
                    for c2 in range(4):
                        ch2 = g2 * 4 + c2
                        b.transp(bank,
                                 XGT.ap(sl2 * D + ch2 * 128, sl2 * D + (ch2 + 1) * 128),
                                 cst["EYEB"].ap(), b.pb(bank, c2 * 128, (c2 + 1) * 128),
                                 r=XGT.k(sl2 * D, (sl2 + 1) * D) + cst["EYEB"].k())
                    for c2 in range(4):
                        ch2 = g2 * 4 + c2
                        c0x = ch2 * C + st * 128
                        b.op("DVE", "tensor_copy", xg.ap(c0x, c0x + 128),
                             b.pb(bank, c2 * 128, (c2 + 1) * 128), r=b.pk(bank),
                             w=xg.k(c0x, c0x + 128) + b.pk(bank))

        if e == 0 and F_ == 0:
            xg_fill(0)
        if not shared and F_ == 1 and e + 1 < E:
            xg_fill(e + 1)
        SRC = RN2 if shared else XG[e % 2]
        ntc = 4 if shared else 2
        srcw = T if shared else C
        for tc in range(ntc):
            gsl = tc % 2
            g = gbufs[gsl]
            for fc in range(NCH):
                hb = fc % 2
                ub = 2 + (fc % 2)
                for dc in range(NCH):
                    b.mm(hb, w1s.ap(dc * 1024 + fc * 128, dc * 1024 + (fc + 1) * 128),
                         SRC.ap(dc * srcw + tc * 512, dc * srcw + tc * 512 + 512),
                         dc == 0, dc == NCH - 1,
                         r=w1s.k() + SRC.k(dc * srcw + tc * 512,
                                           dc * srcw + tc * 512 + 512))
                b.op("ACT", "activation", g.ap(fc * 512, (fc + 1) * 512),
                     b.pf(hb, 0, 512), AF.Silu, r=b.pk(hb),
                     w=g.k(fc * 512, (fc + 1) * 512) + b.pk(hb))
                for dc in range(NCH):
                    b.mm(ub, w3s.ap(dc * 1024 + fc * 128, dc * 1024 + (fc + 1) * 128),
                         SRC.ap(dc * srcw + tc * 512, dc * srcw + tc * 512 + 512),
                         dc == 0, dc == NCH - 1,
                         r=w3s.k() + SRC.k(dc * srcw + tc * 512,
                                           dc * srcw + tc * 512 + 512))
                if shared:
                    b.op("DVE", "tensor_tensor", USC.ap(0, 512), b.pf(ub, 0, 512),
                         SBC.ap(tc * 512, tc * 512 + 512), ALU.mult,
                         r=SBC.k(tc * 512, tc * 512 + 512) + b.pk(ub),
                         w=USC.k() + b.pk(ub))
                    b.op("DVE", "tensor_tensor", g.ap(fc * 512, (fc + 1) * 512),
                         g.ap(fc * 512, (fc + 1) * 512), USC.ap(0, 512), ALU.mult,
                         r=g.k(fc * 512, (fc + 1) * 512) + USC.k(),
                         w=g.k(fc * 512, (fc + 1) * 512))
                else:
                    b.op("DVE", "tensor_tensor", g.ap(fc * 512, (fc + 1) * 512),
                         g.ap(fc * 512, (fc + 1) * 512), b.pf(ub, 0, 512), ALU.mult,
                         r=g.k(fc * 512, (fc + 1) * 512) + b.pk(ub),
                         w=g.k(fc * 512, (fc + 1) * 512) + b.pk(ub))
            pend_w2.append((e, F_, tc, gsl, w2s))
            if tc == ntc - 1 and si + 1 < len(steps):
                load_step(si + 1)
            if len(pend_w2) >= 2:
                ee, ff, tcc, gg, ww = pend_w2.pop(0)
                if ee == 8:
                    emit_w2_shared(tcc, gg, ww)
                else:
                    emit_w2_routed(ee, ff, tcc, gg, ww)
    while pend_w2:
        ee, ff, tcc, gg, ww = pend_w2.pop(0)
        if ee == 8:
            emit_w2_shared(tcc, gg, ww)
        else:
            emit_w2_routed(ee, ff, tcc, gg, ww)
    b.freebuf("SBC", "G0", "G1", "USC", "XGa", "XGb", "XGT", "OUTC",
              "WC0", "WC1", "WC2", "WC3")


def stage_f(b, cst, bi):
    d = b.dram
    RES = b.bufs["RES"]
    ESC = b.bufs["ESC"]
    DCOL = b.bufs["DCOL"]
    C = 1024
    ZTM = b.buf("ZTMF", F32, D)
    SQA = b.buf("SQAF", BF16, D)
    COLS = b.buf("COLSF", F32, 8)
    MOET = b.buf("MOET", BF16, 2 * D)
    ogt_keys = [("OGT", e, s0) for e in range(E) for s0 in range(0, C, 128)]
    ogt_keys += [("OGTZ",)]
    for tt in range(NT):
        sl = tt % 2
        b.idma(MOET.ap(sl * D, (sl + 1) * D), None, d["OGT"],
               DCOL.ap(tt, tt + 1),
               r=ogt_keys + DCOL.k(), w=MOET.k(sl * D, (sl + 1) * D))
        for g in range(2):
            bank = 6 + g
            for c in range(4):
                ch = g * 4 + c
                b.transp(bank, RES.ap(ch * T + tt * 128, ch * T + (tt + 1) * 128),
                         cst["EYEF"].ap(), b.pf(bank, c * 128, (c + 1) * 128),
                         r=RES.k(ch * T + tt * 128, ch * T + (tt + 1) * 128)
                           + cst["EYEF"].k())
            b.op("DVE", "tensor_copy", ZTM.ap(g * 512, g * 512 + 512),
                 b.pf(bank, 0, 512), r=b.pk(bank),
                 w=ZTM.k(g * 512, g * 512 + 512) + b.pk(bank))
        b.op("DVE", "scalar_tensor_tensor", ZTM.ap(), MOET.ap(sl * D, (sl + 1) * D),
             ESC.ap(tt, tt + 1), ZTM.ap(), op0=ALU.mult, op1=ALU.add,
             r=MOET.k(sl * D, (sl + 1) * D) + ESC.k() + ZTM.k(), w=ZTM.k())
        layernorm_tm(b, cst, ZTM, SQA, COLS)
        b.dma("SOq", d["out"][bi * T + tt * 128: bi * T + (tt + 1) * 128, :],
              ZTM.ap(), r=ZTM.k(), w=[("OUT", bi, tt)])
    b.freebuf("ZTMF", "SQAF", "COLSF", "MOET", "RES", "ESC", "DCOL")


def _lay(w):
    din, dout = w.shape
    return np.ascontiguousarray(
        w.reshape(din // 128, 128, dout).transpose(1, 0, 2).reshape(128, -1)
    ).astype(np.float32)


def _layF(w):
    return np.ascontiguousarray(
        w.reshape(8, 128, 2, 1024).transpose(2, 1, 0, 3).reshape(2, 128, 8 * 1024)
    ).astype(np.float32)


def _layF2(w):
    return np.ascontiguousarray(
        w.reshape(2, 8, 128, 1024).transpose(0, 2, 1, 3).reshape(2, 128, 8 * 1024)
    ).astype(np.float32)


def _prep_inputs(inp):
    g = {}
    g["wq_a"] = _lay(inp["L0_attn_Wq"])
    g["wk_a"] = _lay(inp["L0_attn_Wkv"][:, :D])
    g["wv_a"] = _lay(inp["L0_attn_Wkv"][:, D:])
    g["wo_a"] = _lay(inp["L0_attn_Wo"])
    g["c_qg"] = np.ascontiguousarray(
        inp["L0_attn_q_gamma"].reshape(-1).reshape(8, 128).T).astype(np.float32)
    g["c_kg"] = np.ascontiguousarray(
        inp["L0_attn_k_gamma"].reshape(-1).reshape(8, 128).T).astype(np.float32)
    g["wq_f"] = _lay(inp["L0_ffn_Wq"])
    g["wkv_f"] = _lay(inp["L0_ffn_Wkv"])
    g["wk_up"] = _lay(np.ascontiguousarray(
        inp["L0_ffn_Wk_up"].transpose(1, 0, 2).reshape(L, H * DH)))
    g["wv_up"] = _lay(np.ascontiguousarray(
        inp["L0_ffn_Wv_up"].transpose(1, 0, 2).reshape(L, H * DH)))
    g["wo_f"] = _lay(inp["L0_ffn_Wo"])
    g["wr"] = _lay(inp["L0_ffn_Wr"])
    g["c_rb"] = inp["L0_ffn_r_bias"].reshape(1, E).astype(np.float32)
    g["sw1"] = _layF(inp["L0_ffn_sw1"])
    g["sw3"] = _layF(inp["L0_ffn_sw3"])
    g["sw2"] = _layF2(inp["L0_ffn_sw2"])
    g["ew1"] = np.stack([_layF(inp["L0_ffn_ew1"][e]) for e in range(E)])
    g["ew3"] = np.stack([_layF(inp["L0_ffn_ew3"][e]) for e in range(E)])
    g["ew2"] = np.stack([_layF2(inp["L0_ffn_ew2"][e]) for e in range(E)])

    g["c_eye"] = np.eye(128, dtype=np.float32)
    m = np.zeros((128, 128), dtype=np.float32)
    m[np.triu_indices(128, 1)] = NEG
    g["c_mask"] = m
    inv = 1.0 / (100000.0 ** (np.arange(0, DH, 2, dtype=np.float64) / DH))
    tpos = np.arange(T, dtype=np.float64)[:, None] * inv[None, :]
    g["c_cos"] = np.ascontiguousarray(np.cos(tpos).astype(np.float32)
                                      .reshape(NT, 128, 32).transpose(1, 0, 2)
                                      .reshape(128, NT * 32))
    g["c_sin"] = np.ascontiguousarray(np.sin(tpos).astype(np.float32)
                                      .reshape(NT, 128, 32).transpose(1, 0, 2)
                                      .reshape(128, NT * 32))
    gain = np.zeros((128, 24), dtype=np.float32)
    gain[:, 0:8] = inp["L0_attn_norm_w"].reshape(8, 128).T
    gain[:, 8:16] = inp["L0_ffn_ln1_w"].reshape(8, 128).T
    gain[:, 16:24] = inp["L0_ffn_ln2_w"].reshape(8, 128).T
    g["c_gain"] = gain
    g["c_io8"] = np.tile(np.arange(8, dtype=np.float32), (128, 1))
    ind2 = np.zeros((128, 2), dtype=np.float32)
    ind2[0:64, 0] = 1.0
    ind2[64:128, 1] = 1.0
    g["c_ind2"] = ind2
    row2 = np.zeros((2, 256), dtype=np.float32)
    row2[0, 0:64] = 1.0
    row2[1, 64:128] = 1.0
    row2[0, 128:256] = 1.0
    g["c_row2"] = row2
    exp9 = np.zeros((16, 9 * 128), dtype=np.float32)
    for e in range(9):
        exp9[e, e * 128:(e + 1) * 128] = 1.0
    g["c_exp9"] = exp9
    g["c_i128"] = (8 * 1024 + (np.arange(T) % 128)).astype(np.float32).reshape(1, T)
    r16 = np.zeros((16, 32), dtype=np.float32)
    r16[:16, :16] = np.eye(16)
    r16[0:8, 16] = 1.0
    g["c_r16"] = r16
    return g


def kernel(**inputs):
    inputs = {k: np.asarray(v, dtype=np.float32) for k, v in inputs.items()}
    g = _prep_inputs(inputs)
    toks = np.ascontiguousarray(inputs["L0_attn_tokens"].reshape(16 * T, D))
    nc = build_nc()
    in_maps = []
    for c in range(N_CORES):
        m = dict(g)
        m["x0"] = np.ascontiguousarray(
            toks[c * B_PER_CORE * T:(c + 1) * B_PER_CORE * T, :])
        in_maps.append(m)
    res = run_bass_kernel_spmd(nc, in_maps, core_ids=list(range(N_CORES)))
    out = np.concatenate([r["out"].reshape(B_PER_CORE, T, D) for r in res.results],
                         axis=0)
    return out


# version 15
# speedup vs baseline: 1.0096x; 1.0096x over previous
import numpy as np
import concourse.bass as bass
import concourse.mybir as mybir
from concourse.bass_utils import run_bass_kernel_spmd

F32 = mybir.dt.float32
BF16 = mybir.dt.bfloat16
AF = mybir.ActivationFunctionType
ALU = mybir.AluOpType
AX = mybir.AxisListType

N_CORES = 8
B_PER_CORE = 2
T = 2048
D = 1024
H = 16
DH = 64
NCH = D // 128
NT = T // 128
L = 256
E = 8
EPS_A = float(np.finfo(np.float32).eps)
EPS_F = 1e-6
EPS_LN = 1e-5
NEG = -1.0e30

STAGES = ("A", "B", "C", "F")
DBG = ()

KEYBLK = 2048


class Sched:
    def __init__(self):
        self.ops = []
        self.last_w = {}
        self.readers = {}

    def add(self, ch, emit, r=(), w=(), dma=None):
        i = len(self.ops)
        raw = set()
        other = set()
        for k in r:
            j = self.last_w.get(k)
            if j is not None:
                raw.add(j)
        for k in w:
            j = self.last_w.get(k)
            if j is not None:
                other.add(j)
            for j2 in self.readers.get(k, ()):
                other.add(j2)
        for k in r:
            self.readers.setdefault(k, []).append(i)
        for k in w:
            self.last_w[k] = i
            self.readers[k] = []
        self.ops.append({"ch": ch, "emit": emit, "dma": dma, "raw": raw,
                         "other": other})
        return i

    def finalize(self):
        ops = self.ops
        n = len(ops)
        deps = [None] * n
        for i, op in enumerate(ops):
            d = set()
            for j in op["raw"]:
                pj = ops[j]
                if (pj["ch"] == op["ch"] == "PE"
                        and pj["dma"] is None and op["dma"] is None):
                    continue
                d.add(j)
            for j in op["other"]:
                pj = ops[j]
                if (pj["ch"] == op["ch"] == "PE"
                        and pj["dma"] is None and op["dma"] is None):
                    continue
                d.add(j)
            d.discard(i)
            deps[i] = d
        need = [False] * n
        for i in range(n):
            for j in deps[i]:
                need[j] = True
        NSUB = {"Pq": 36, "Sq": 36, "POq": 1, "SOq": 1}
        cnt = {}
        inc_val = [None] * n
        sub_of = [None] * n
        dma_cum = [None] * n
        cur_dma = {}
        rr = {}
        for i, op in enumerate(ops):
            dma_cum[i] = dict(cur_dma)
            if need[i]:
                if op["dma"]:
                    q = op["dma"]
                    rr[q] = rr.get(q, -1) + 1
                    sem = f"{q}{rr[q] % NSUB[q]}"
                    sub_of[i] = sem
                    cnt[sem] = cnt.get(sem, 0) + 16
                    inc_val[i] = cnt[sem]
                    cur_dma[sem] = cnt[sem]
                else:
                    sem = op["ch"]
                    cnt[sem] = cnt.get(sem, 0) + 1
                    inc_val[i] = cnt[sem]
        per_ch = {}
        clock = {}
        for i, op in enumerate(ops):
            waits = []
            for j in sorted(deps[i]):
                pj = ops[j]
                if pj["dma"]:
                    sem = sub_of[j]
                    v = dma_cum[i].get(sem, 0)
                else:
                    sem = pj["ch"]
                    v = inc_val[j]
                key = (op["ch"], sem)
                if clock.get(key, 0) >= v:
                    continue
                clock[key] = v
                waits.append((sem, v))
            inc = None
            if need[i]:
                sem = sub_of[i] if op["dma"] else op["ch"]
                inc = (sem, 16 if op["dma"] else 1)
            per_ch.setdefault(op["ch"], []).append(
                {"emit": op["emit"], "waits": waits, "inc": inc})
        self.sem_totals = cnt
        return per_ch


class Arena:
    def __init__(self, nbytes):
        self.free_l = [(0, nbytes)]
        self.used = {}
        self.hw = 0

    def alloc(self, name, nb, hi=False):
        nb = (nb + 31) & ~31
        idxs = range(len(self.free_l))
        if hi:
            idxs = reversed(list(idxs))
        for idx in idxs:
            off, sz = self.free_l[idx]
            if sz >= nb:
                if hi:
                    self.used[name] = (off + sz - nb, nb)
                    if sz == nb:
                        self.free_l.pop(idx)
                    else:
                        self.free_l[idx] = (off, sz - nb)
                    return off + sz - nb
                self.used[name] = (off, nb)
                if sz == nb:
                    self.free_l.pop(idx)
                else:
                    self.free_l[idx] = (off + nb, sz - nb)
                return off
        raise MemoryError(f"arena: cannot alloc {name} ({nb}B); free={self.free_l}")

    def release(self, name):
        off, nb = self.used.pop(name)
        self.free_l.append((off, nb))
        self.free_l.sort()
        merged = []
        for o, s in self.free_l:
            if merged and merged[-1][0] + merged[-1][1] == o:
                merged[-1] = (merged[-1][0], merged[-1][1] + s)
            else:
                merged.append((o, s))
        self.free_l = merged


class Buf:
    def __init__(self, bld, name, dtype, cols, hi=True):
        self.bld = bld
        self.name = name
        self.dtype = dtype
        self.isz = 2 if dtype == BF16 else 4
        self.cols = cols
        self.off_b = bld.arena.alloc(name, cols * self.isz, hi=hi)

    def ap(self, c0=0, c1=None, p0=0, p1=128):
        c1 = self.cols if c1 is None else c1
        h, W = self.bld.handle(self.dtype)
        off = p0 * W + self.off_b // self.isz + c0
        return bass.AP(h, off, [[W, p1 - p0], [1, c1 - c0]])

    def ap3(self, c0, dims, p0=0, p1=128):
        h, W = self.bld.handle(self.dtype)
        off = p0 * W + self.off_b // self.isz + c0
        return bass.AP(h, off, [[W, p1 - p0]] + [[s, c] for s, c in dims])

    def k(self, c0=0, c1=None):
        c1 = self.cols if c1 is None else c1
        b0 = self.off_b + c0 * self.isz
        b1 = self.off_b + c1 * self.isz
        return [("S", x) for x in range(b0 // KEYBLK, (b1 - 1) // KEYBLK + 1)]


class CBuf:

    def __init__(self, bld, name, dtype, nch, cols_per, hi=True):
        self.name = name
        self.cols_per = cols_per
        self.chunks = [Buf(bld, f"{name}.{i}", dtype, cols_per, hi=hi)
                       for i in range(nch)]

    def ap(self, c0=0, c1=None, p0=0, p1=128):
        ch = c0 // self.cols_per
        assert c1 is not None and c1 <= (ch + 1) * self.cols_per, (c0, c1)
        base = ch * self.cols_per
        return self.chunks[ch].ap(c0 - base, c1 - base, p0, p1)

    def ap3(self, c0, dims, p0=0, p1=128):
        ch = c0 // self.cols_per
        base = ch * self.cols_per
        ext = c0 - base + sum(s_ * (n_ - 1) for s_, n_ in dims) + 1
        assert ext <= self.cols_per, (c0, dims)
        return self.chunks[ch].ap3(c0 - base, dims, p0, p1)

    def k(self, c0=0, c1=None):
        if c1 is None:
            c1 = len(self.chunks) * self.cols_per
        out = []
        ch0 = c0 // self.cols_per
        ch1 = (c1 - 1) // self.cols_per
        for ch in range(ch0, ch1 + 1):
            base = ch * self.cols_per
            a = max(c0, base) - base
            bcl = min(c1, base + self.cols_per) - base
            out += self.chunks[ch].k(a, bcl)
        return out

    def release_all(self, arena):
        for c in self.chunks:
            arena.release(c.name)


class Builder:
    def __init__(self, nc, sched, arena_t, arena_cols, psums, dram):
        self.nc = nc
        self.s = sched
        self.arena = Arena(arena_cols * 2)
        self._hb = arena_t.ap()
        self._hf = arena_t.ap().bitcast(F32)
        self._hu = arena_t.ap().bitcast(mybir.dt.uint32)
        self.psf_t = [p.ap().bitcast(F32) for p in psums]
        self.psb_t = [p.ap() for p in psums]
        self.dram = dram
        self.bufs = {}
        self.dbg_keys = []

    def handle(self, dtype):
        ap = (self._hb if dtype == BF16 else
              self._hu if dtype == mybir.dt.uint32 else self._hf)
        return ap.tensor, ap.ap[0][0]

    LOW = {"RES", "Q", "K", "V", "OFMB", "X2B", "EYEF", "EYEB", "MASK",
           "GAIN", "QG", "KG", "IO8", "IND2", "ROW2", "RB8", "ONESC", "RBROW"}

    def _is_low(self, name):
        return name.split(".")[0] in self.LOW

    def buf(self, name, dtype, cols):
        bb = Buf(self, name, dtype, cols, hi=not self._is_low(name))
        self.bufs[name] = bb
        return bb

    def cbuf(self, name, dtype, nch, cols_per):
        bb = CBuf(self, name, dtype, nch, cols_per, hi=not self._is_low(name))
        self.bufs[name] = bb
        return bb

    def freebuf(self, *names):
        for nm in names:
            bb = self.bufs.pop(nm)
            if isinstance(bb, CBuf):
                bb.release_all(self.arena)
            else:
                self.arena.release(bb.name)

    def pf(self, i, c0=0, c1=512, p0=0, p1=128):
        ap = self.psf_t[i]
        return bass.AP(ap.tensor, p0 * 512 + c0, [[512, p1 - p0], [1, c1 - c0]])

    def pb(self, i, c0=0, c1=1024, p0=0, p1=128):
        ap = self.psb_t[i]
        return bass.AP(ap.tensor, p0 * 1024 + c0, [[1024, p1 - p0], [1, c1 - c0]])

    def pk(self, i):
        return [("P", i)]

    def dma(self, q, dst_ap, src_ap, r=(), w=(), transpose=False):
        eng = "POOL" if q in ("Pq", "POq") else "SP"
        self.s.add(eng, lambda e, d_=dst_ap, s_=src_ap, t_=transpose:
                   e.dma_start(d_, s_, transpose=t_), r=r, w=w, dma=q)

    def idma(self, dst_ap, dst_off, src_ap, src_off, r=(), w=()):
        from bass_rust import IndirectOffsetOnAxis
        oo = IndirectOffsetOnAxis(ap=dst_off, axis=0) if dst_off is not None else None
        io = IndirectOffsetOnAxis(ap=src_off, axis=0) if src_off is not None else None
        self.s.add("POOL", lambda e, d_=dst_ap, s_=src_ap, o_=oo, i_=io:
                   e.indirect_dma_start(d_, o_, s_, i_), r=r, w=w, dma="Pq")

    def mm(self, bank, lhsT, rhs, start, stop, r=(), out=None):
        o = out if out is not None else self.pf(bank)
        self.s.add("PE", lambda e, o_=o, l_=lhsT, r_=rhs, st=start, sp=stop:
                   e.matmul(o_, l_, r_, start=st, stop=sp),
                   r=list(r) + self.pk(bank), w=self.pk(bank))

    def transp(self, bank, in_ap, ident_ap, out_ap, r=()):
        self.s.add("PE", lambda e, o_=out_ap, i_=in_ap, d_=ident_ap:
                   e.transpose(o_, i_, d_), r=list(r) + self.pk(bank),
                   w=self.pk(bank))

    def op(self, ch, fn, *args, r=(), w=(), **kw):
        self.s.add(ch, lambda e, a=args, k=kw: getattr(e, fn)(*a, **k), r=r, w=w)


def build_nc():
    nc = bass.Bass("TRN2", target_bir_lowering=False)
    dram = {}

    def din(name, shape):
        dram[name] = nc.dram_tensor(name, list(shape), F32, kind="ExternalInput").ap()

    din("x0", (B_PER_CORE * T, D))
    din("wq_a", (128, 8 * D)); din("wk_a", (128, 8 * D)); din("wv_a", (128, 8 * D))
    din("wo_a", (128, 8 * D))
    din("c_qg", (128, 8)); din("c_kg", (128, 8))
    din("wq_f", (128, 8 * D)); din("wkv_f", (128, 8 * L))
    din("wk_up", (128, 2 * D)); din("wv_up", (128, 2 * D))
    din("wo_f", (128, 8 * D)); din("wr", (128, 8 * E)); din("c_rb", (1, E))
    din("sw1", (2, 128, 8 * 1024)); din("sw3", (2, 128, 8 * 1024))
    din("sw2", (2, 128, 8 * 1024))
    din("ew1", (E, 2, 128, 8 * 1024)); din("ew3", (E, 2, 128, 8 * 1024))
    din("ew2", (E, 2, 128, 8 * 1024))
    din("c_eye", (128, 128)); din("c_mask", (128, 128))
    din("c_cos", (128, 512)); din("c_sin", (128, 512))
    din("c_gain", (128, 24)); din("c_io8", (128, 8))
    din("c_ind2", (128, 2)); din("c_row2", (2, 256)); din("c_exp9", (16, 9 * 128))
    din("c_i128", (1, T)); din("c_r16", (16, 32))
    dram["out"] = nc.dram_tensor("out", [B_PER_CORE * T, D], F32,
                                 kind="ExternalOutput").ap()
    dram["RG"] = nc.dram_tensor("RG", [8 * 1024 + 128, D], BF16).ap()
    dram["OGT"] = nc.dram_tensor("OGT", [8 * 1024 + 128, D], BF16).ap()
    for nm in DBG:
        dram["dbg_" + nm] = nc.dram_tensor(
            "dbg_" + nm, [B_PER_CORE * NT * 128, T], F32, kind="ExternalOutput").ap()

    ARENA_COLS = 106352
    with (
        nc.sbuf_tensor("arena", [128, ARENA_COLS], BF16) as arena_t,
        nc.psum_tensor("ps0", [128, 1024], BF16) as ps0,
        nc.psum_tensor("ps1", [128, 1024], BF16) as ps1,
        nc.psum_tensor("ps2", [128, 1024], BF16) as ps2,
        nc.psum_tensor("ps3", [128, 1024], BF16) as ps3,
        nc.psum_tensor("ps4", [128, 1024], BF16) as ps4,
        nc.psum_tensor("ps5", [128, 1024], BF16) as ps5,
        nc.psum_tensor("ps6", [128, 1024], BF16) as ps6,
        nc.psum_tensor("ps7", [128, 1024], BF16) as ps7,
        nc.semaphore("sPE") as sPE,
        nc.semaphore("sDVE") as sDVE,
        nc.semaphore("sACT") as sACT,
        nc.semaphore("sPOOL") as sPOOL,
        nc.semaphore("sSP") as sSP,
        nc.Block() as block,
    ):
        sched = Sched()
        bld = Builder(nc, sched, arena_t, ARENA_COLS,
                      [ps0, ps1, ps2, ps3, ps4, ps5, ps6, ps7], dram)
        build_program(bld)
        per_ch = sched.finalize()
        sems = {"PE": sPE, "DVE": sDVE, "ACT": sACT, "POOL": sPOOL, "SP": sSP}
        import contextlib
        _stack = contextlib.ExitStack()
        for q, nsub in [("Pq", 36), ("Sq", 36), ("POq", 1), ("SOq", 1)]:
            for si in range(nsub):
                sems[f"{q}{si}"] = _stack.enter_context(
                    nc.semaphore(f"s{q}{si}"))
        for sem, tot in sched.sem_totals.items():
            assert tot < 60000, f"sem {sem} overflow risk: {tot}"

        def emit_ch(section, ch):
            items = per_ch.get(ch, [])

            @section
            def _(eng):
                for it in items:
                    for (sm, v) in it["waits"]:
                        eng.wait_ge(sems[sm], v)
                    inst = it["emit"](eng)
                    if it["inc"]:
                        inst.then_inc(sems[it["inc"][0]], it["inc"][1])

        emit_ch(block.tensor, "PE")
        emit_ch(block.vector, "DVE")
        emit_ch(block.scalar, "ACT")
        emit_ch(block.gpsimd, "POOL")
        emit_ch(block.sync, "SP")
    return nc


def build_program(b):
    d = b.dram
    EYEF = b.buf("EYEF", F32, 128)
    EYEB = b.buf("EYEB", BF16, 128)
    MASK = b.buf("MASK", F32, 128)
    GAIN = b.buf("GAIN", F32, 24)
    QG = b.buf("QG", F32, 8)
    KG = b.buf("KG", F32, 8)
    IO8 = b.buf("IO8", F32, 8)
    IND2 = b.buf("IND2", F32, 2)
    ROW2 = b.buf("ROW2", F32, 256)
    RB8 = b.buf("RB8", F32, 8)
    ONESC = b.buf("ONESC", F32, 16)
    RBROW = b.buf("RBROW", F32, 8)
    EPSC = b.buf("EPSC", F32, 4)

    b.dma("Sq", EYEF.ap(), d["c_eye"], w=EYEF.k())
    b.dma("Sq", MASK.ap(), d["c_mask"], w=MASK.k())
    b.dma("Sq", GAIN.ap(), d["c_gain"], w=GAIN.k())
    b.dma("Sq", QG.ap(), d["c_qg"], w=QG.k())
    b.dma("Sq", KG.ap(), d["c_kg"], w=KG.k())
    b.dma("Sq", IO8.ap(), d["c_io8"], w=IO8.k())
    b.dma("Sq", IND2.ap(), d["c_ind2"], w=IND2.k())
    b.dma("Sq", ROW2.ap(0, 256, 0, 2), d["c_row2"], w=ROW2.k())
    b.dma("Sq", RBROW.ap(0, 8, 0, 1), d["c_rb"], w=RBROW.k())
    b.op("DVE", "tensor_copy", EYEB.ap(), EYEF.ap(), r=EYEF.k(), w=EYEB.k())
    b.op("DVE", "memset", ONESC.ap(0, 1), 1.0, w=ONESC.k())
    b.op("DVE", "memset", EPSC.ap(0, 1), EPS_A, w=EPSC.k())
    b.op("DVE", "memset", EPSC.ap(1, 2), EPS_F, w=EPSC.k())
    b.op("DVE", "memset", EPSC.ap(2, 3), EPS_LN, w=EPSC.k())
    b.op("DVE", "tensor_scalar", QG.ap(), QG.ap(), 1.0, 8.0, ALU.add, ALU.mult,
         r=QG.k(), w=QG.k())
    b.op("DVE", "tensor_scalar", KG.ap(), KG.ap(), 1.0, 8.0, ALU.add, ALU.mult,
         r=KG.k(), w=KG.k())
    b.mm(7, ROW2.ap(128, 256, 0, 1), RBROW.ap(0, 8, 0, 1), True, True,
         r=ROW2.k() + RBROW.k(), out=b.pf(7, 0, 8))
    b.op("DVE", "tensor_copy", RB8.ap(), b.pf(7, 0, 8), r=RB8.k(), w=RB8.k() + b.pk(7))

    cst = dict(EYEF=EYEF, EYEB=EYEB, MASK=MASK, GAIN=GAIN,
               QG=QG, KG=KG, IO8=IO8, IND2=IND2, ROW2=ROW2, RB8=RB8,
               ONESC=ONESC, EPSC=EPSC)

    for bi in range(B_PER_CORE):
        if "A" in STAGES:
            stage_a(b, cst, bi)
        if "B" in STAGES:
            stage_b(b, cst, bi)
        if "C" in STAGES:
            stage_c(b, cst, bi)
        if "F" in STAGES:
            stage_f(b, cst, bi)
    outkeys = [("OUT", bb, tt) for bb in range(B_PER_CORE) for tt in range(NT)]
    outkeys += b.dbg_keys
    b.s.add("SP", lambda e: e.nop(), r=outkeys)


def rinv(b, cst, ap, src_ap, wk, rk):
    b.op("ACT", "activation", ap, src_ap, AF.Ln, r=list(rk) + list(wk), w=wk)
    b.op("ACT", "activation", ap, ap, AF.Exp, scale=-1.0, r=wk, w=wk)


def rsqrt_rows(b, cst, dst_ap, src_ap, epsi, scale, wk, rk):
    b.op("ACT", "activation", dst_ap, src_ap, AF.Ln,
         bias=cst["EPSC"].ap(epsi, epsi + 1, 0, 1), scale=scale,
         r=list(rk) + cst["EPSC"].k() + list(wk), w=wk)
    b.op("ACT", "activation", dst_ap, dst_ap, AF.Exp, scale=-0.5, r=wk, w=wk)

def rmsnorm_rows(b, cst, SRC, SQ, ROWR, bank, epsi):
    for tc in range(4):
        c0 = tc * 512
        for ch in range(NCH):
            b.op("ACT", "activation", SQ.ap(0, 512),
                 SRC.ap(ch * T + c0, ch * T + c0 + 512), AF.Square,
                 r=SRC.k(ch * T + c0, ch * T + c0 + 512), w=SQ.k())
            b.mm(bank, cst["ONESC"].ap(0, 1), SQ.ap(0, 512), ch == 0, ch == NCH - 1,
                 r=cst["ONESC"].k() + SQ.k(), out=b.pf(bank, 0, 512, 0, 1))
        b.op("ACT", "activation", ROWR.ap(c0, c0 + 512, 0, 1),
             b.pf(bank, 0, 512, 0, 1), AF.Ln,
             bias=cst["EPSC"].ap(epsi, epsi + 1, 0, 1), scale=1.0 / D,
             r=cst["EPSC"].k() + b.pk(bank),
             w=ROWR.k(c0, c0 + 512) + b.pk(bank))
        b.op("ACT", "activation", ROWR.ap(c0, c0 + 512, 0, 1),
             ROWR.ap(c0, c0 + 512, 0, 1), AF.Exp, scale=-0.5,
             r=ROWR.k(c0, c0 + 512), w=ROWR.k(c0, c0 + 512))


def bcast_row(b, cst, ROWR, DSTB, bank):
    for tc in range(4):
        c0 = tc * 512
        b.mm(bank, cst["ROW2"].ap(128, 256, 0, 1), ROWR.ap(c0, c0 + 512, 0, 1),
             True, True, r=cst["ROW2"].k() + ROWR.k(c0, c0 + 512))
        b.op("DVE", "tensor_copy", DSTB.ap(c0, c0 + 512), b.pf(bank, 0, 512),
             r=b.pk(bank), w=DSTB.k(c0, c0 + 512) + b.pk(bank))


def layernorm_tm(b, cst, ZTM, SQA, COLS):
    b.op("DVE", "tensor_reduce", COLS.ap(0, 1), ZTM.ap(), axis=AX.X, op=ALU.add,
         r=ZTM.k(), w=COLS.k(0, 1))
    b.op("DVE", "tensor_scalar", COLS.ap(0, 1), COLS.ap(0, 1), 1.0 / D, None,
         ALU.mult, r=COLS.k(0, 1), w=COLS.k(0, 1))
    b.op("ACT", "activation", SQA.ap(), ZTM.ap(), AF.Square,
         r=ZTM.k(), w=SQA.k())
    b.op("DVE", "tensor_reduce", COLS.ap(1, 2), SQA.ap(), axis=AX.X, op=ALU.add,
         r=SQA.k(), w=COLS.k(1, 2))
    b.op("DVE", "tensor_scalar", COLS.ap(1, 2), COLS.ap(1, 2), 1.0 / D, None,
         ALU.mult, r=COLS.k(1, 2), w=COLS.k(1, 2))
    b.op("DVE", "tensor_tensor", COLS.ap(2, 3), COLS.ap(0, 1), COLS.ap(0, 1),
         ALU.mult, r=COLS.k(0, 1), w=COLS.k(2, 3))
    b.op("DVE", "tensor_tensor", COLS.ap(2, 3), COLS.ap(1, 2), COLS.ap(2, 3),
         ALU.subtract, r=COLS.k(1, 2) + COLS.k(2, 3), w=COLS.k(2, 3))
    b.op("ACT", "activation", COLS.ap(2, 3), COLS.ap(2, 3), AF.Ln,
         bias=cst["EPSC"].ap(2, 3), r=COLS.k(2, 3) + cst["EPSC"].k(),
         w=COLS.k(2, 3))
    b.op("ACT", "activation", COLS.ap(2, 3), COLS.ap(2, 3), AF.Exp, scale=-0.5,
         r=COLS.k(2, 3), w=COLS.k(2, 3))
    b.op("DVE", "tensor_scalar", ZTM.ap(), ZTM.ap(), COLS.ap(0, 1), COLS.ap(2, 3),
         ALU.subtract, ALU.mult,
         r=ZTM.k() + COLS.k(0, 1) + COLS.k(2, 3), w=ZTM.k())


def fm_transpose_in(b, cst, SRCTM, sc0, DSTFM, tt):
    for g in range(2):
        bank = 6 + g
        for c in range(4):
            ch = g * 4 + c
            b.transp(bank, SRCTM.ap(sc0 + ch * 128, sc0 + (ch + 1) * 128),
                     cst["EYEF"].ap(),
                     b.pf(bank, c * 128, (c + 1) * 128),
                     r=SRCTM.k(sc0, sc0 + D) + cst["EYEF"].k())
        for c in range(4):
            ch = g * 4 + c
            c0 = ch * T + tt * 128
            b.op("DVE", "tensor_copy", DSTFM.ap(c0, c0 + 128),
                 b.pf(bank, c * 128, (c + 1) * 128), r=b.pk(bank),
                 w=DSTFM.k(c0, c0 + 128) + b.pk(bank))


def dump_bf(b, nm, bi, SRC, ncols=T, nch=NCH):
    if nm not in DBG:
        return
    dst = b.dram["dbg_" + nm]
    r0 = bi * NT * 128
    for ch in range(nch):
        key = ("DBG", nm, bi, ch)
        b.dbg_keys.append(key)
        step = min(ncols, 512)
        for c0 in range(0, ncols, step):
            b.dma("POq", dst[r0 + ch * 128: r0 + (ch + 1) * 128, c0:c0 + step],
                  SRC.ap(ch * ncols + c0, ch * ncols + c0 + step),
                  r=SRC.k(ch * ncols + c0, ch * ncols + c0 + step), w=[key])


def dump_fm(b, nm, bi, SRC):
    if nm not in DBG:
        return
    dst = b.dram["dbg_" + nm]
    r0 = bi * NT * 128
    for ch in range(NCH):
        key = ("DBG", nm, bi, ch)
        b.dbg_keys.append(key)
        b.dma("SOq", dst[r0 + ch * 128: r0 + (ch + 1) * 128, :],
              SRC.ap(ch * T, (ch + 1) * T), r=SRC.k(ch * T, (ch + 1) * T),
              w=[key])


def stage_a(b, cst, bi):
    d = b.dram
    t_off = bi * T

    XTM = b.buf("XTM", F32, D)
    SQA = b.buf("SQA", BF16, D)
    RCOL = b.buf("RCOL", F32, 16)
    RROW1 = b.buf("RROW1", F32, 128)
    RSTDT = b.buf("RSTDT", F32, 128)
    RN = b.cbuf("RN", BF16, NCH, T)
    for tt in range(NT):
        b.dma("Sq", XTM.ap(), d["x0"][t_off + tt * 128: t_off + (tt + 1) * 128, :],
              w=XTM.k())
        b.op("ACT", "activation", SQA.ap(), XTM.ap(), AF.Square,
             r=XTM.k(), w=SQA.k())
        b.op("DVE", "tensor_reduce", RCOL.ap(0, 1), SQA.ap(), axis=AX.X,
             op=ALU.add, r=SQA.k(), w=RCOL.k())
        b.op("ACT", "activation", RCOL.ap(0, 1), RCOL.ap(0, 1), AF.Ln,
             bias=cst["EPSC"].ap(0, 1), scale=1.0 / D,
             r=RCOL.k() + cst["EPSC"].k(), w=RCOL.k())
        b.op("ACT", "activation", RCOL.ap(0, 1), RCOL.ap(0, 1), AF.Exp,
             scale=-0.5, r=RCOL.k(), w=RCOL.k())
        b.transp(2, RCOL.ap(0, 1), cst["EYEF"].ap(), b.pf(2, 0, 128, 0, 1),
                 r=RCOL.k() + cst["EYEF"].k())
        b.op("DVE", "tensor_copy", RROW1.ap(0, 128, 0, 1), b.pf(2, 0, 128, 0, 1),
             r=b.pk(2), w=RROW1.k() + b.pk(2))
        b.mm(3, cst["ROW2"].ap(128, 256, 0, 1), RROW1.ap(0, 128, 0, 1), True, True,
             r=cst["ROW2"].k() + RROW1.k(), out=b.pf(3, 0, 128))
        b.op("DVE", "tensor_copy", RSTDT.ap(), b.pf(3, 0, 128),
             r=b.pk(3), w=RSTDT.k() + b.pk(3))
        for g in range(2):
            bank = 6 + g
            for c in range(4):
                ch = g * 4 + c
                b.transp(bank, XTM.ap(ch * 128, (ch + 1) * 128), cst["EYEF"].ap(),
                         b.pf(bank, c * 128, (c + 1) * 128),
                         r=XTM.k() + cst["EYEF"].k())
            for c in range(4):
                ch = g * 4 + c
                c0 = ch * T + tt * 128
                b.op("DVE", "scalar_tensor_tensor", RN.ap(c0, c0 + 128),
                     b.pf(bank, c * 128, (c + 1) * 128),
                     cst["GAIN"].ap(ch, ch + 1), RSTDT.ap(),
                     op0=ALU.mult, op1=ALU.mult,
                     r=cst["GAIN"].k() + RSTDT.k() + b.pk(bank),
                     w=RN.k(c0, c0 + 128) + b.pk(bank))
    b.freebuf("SQA", "RCOL", "RROW1", "RSTDT", "XTM")
    dump_bf(b, "rn", bi, RN)

    WS0 = b.buf("WSA0", BF16, 8 * D)
    WS1 = b.buf("WSA1", BF16, 8 * D)
    Q = b.cbuf("Q", BF16, NCH, T)
    K = b.cbuf("K", BF16, NCH, T)
    V = b.cbuf("V", BF16, NT, H * 65)
    SQH = b.buf("SQH", F32, 512)
    RNB = b.buf("RNB", F32, 512)
    HROW = b.buf("HROW", F32, 512)
    b.dma("Pq", WS0.ap(), d["wq_a"], w=WS0.k())
    b.dma("Pq", WS1.ap(), d["wk_a"], w=WS1.k())

    def headnorm(DST, GCOL, fc, tc):
        c0 = fc * T + tc * 512
        dk = DST.k(c0, c0 + 512)
        b.op("ACT", "activation", SQH.ap(0, 512), DST.ap(c0, c0 + 512), AF.Square,
             r=dk, w=SQH.k())
        b.mm(2, cst["IND2"].ap(0, 2), SQH.ap(0, 512), True, True,
             r=cst["IND2"].k() + SQH.k(), out=b.pf(2, 0, 512, 0, 2))
        b.op("ACT", "activation", HROW.ap(0, 512, 0, 2), b.pf(2, 0, 512, 0, 2),
             AF.Ln, r=b.pk(2), w=HROW.k() + b.pk(2))
        b.op("ACT", "activation", HROW.ap(0, 512, 0, 2), HROW.ap(0, 512, 0, 2),
             AF.Exp, scale=-0.5, r=HROW.k(), w=HROW.k())
        b.op("DVE", "tensor_scalar", HROW.ap(0, 512, 0, 2), HROW.ap(0, 512, 0, 2),
             1e12, None, ALU.min, r=HROW.k(), w=HROW.k())
        b.mm(3, cst["ROW2"].ap(0, 128, 0, 2), HROW.ap(0, 512, 0, 2), True, True,
             r=cst["ROW2"].k() + HROW.k())
        b.op("DVE", "tensor_copy", RNB.ap(0, 512), b.pf(3, 0, 512), r=b.pk(3),
             w=RNB.k() + b.pk(3))
        b.op("DVE", "scalar_tensor_tensor", DST.ap(c0, c0 + 512),
             DST.ap(c0, c0 + 512), GCOL.ap(fc, fc + 1), RNB.ap(0, 512),
             op0=ALU.mult, op1=ALU.mult, r=dk + GCOL.k() + RNB.k(), w=dk)

    for which, (DST, GCOL, ws) in enumerate(
            [(Q, cst["QG"], WS0), (K, cst["KG"], WS1)]):
        wk = ws.k()
        for fc in range(NCH):
            for tc in range(4):
                bank = tc % 2
                for dc in range(NCH):
                    b.mm(bank, ws.ap(dc * D + fc * 128, dc * D + (fc + 1) * 128),
                         RN.ap(dc * T + tc * 512, dc * T + tc * 512 + 512),
                         dc == 0, dc == NCH - 1,
                         r=wk + RN.k(dc * T + tc * 512, dc * T + tc * 512 + 512))
                c0 = fc * T + tc * 512
                b.op("DVE", "tensor_copy", DST.ap(c0, c0 + 512), b.pf(bank, 0, 512),
                     r=b.pk(bank), w=DST.k(c0, c0 + 512) + b.pk(bank))
                headnorm(DST, GCOL, fc, tc)
        if which == 0:
            b.dma("Pq", WS0.ap(), d["wv_a"], w=WS0.k())

    for tt in range(NT):
        b.op("DVE", "memset", V.ap3(tt * H * 65 + 64, [(65, 16), (1, 1)]), 1.0,
             w=V.k(tt * H * 65, (tt + 1) * H * 65))
        for fcol in range(2):
            bank = fcol % 2
            for dc in range(NCH):
                b.mm(bank, RN.ap(dc * T + tt * 128, dc * T + (tt + 1) * 128),
                     WS0.ap(dc * D + fcol * 512, dc * D + fcol * 512 + 512),
                     dc == 0, dc == NCH - 1,
                     r=WS0.k() + RN.k(dc * T + tt * 128, dc * T + (tt + 1) * 128))
            dst = V.ap3(tt * H * 65 + fcol * 8 * 65, [(65, 8), (1, 64)])
            b.op("DVE", "tensor_copy", dst, b.pf(bank, 0, 512), r=b.pk(bank),
                 w=V.k(tt * H * 65, (tt + 1) * H * 65) + b.pk(bank))
    dump_bf(b, "qh", bi, Q)
    dump_bf(b, "kh", bi, K)
    dump_bf(b, "vv", bi, V, ncols=H * 65, nch=NT)
    b.freebuf("SQH", "RNB", "HROW", "RN", "WSA0")
    b.dma("Pq", WS1.ap(), d["wo_a"], w=WS1.k())

    RES = b.cbuf("RES", F32, NCH, T)
    ICW = 256
    PT = b.buf("PT", BF16, NT * ICW)
    OFM = b.buf("OFM", BF16, NCH * ICW)
    OSC = b.buf("OSC", F32, 256)
    RROW = b.buf("RROW", F32, 256)
    ZTM = b.buf("ZTM", F32, D)
    SQA2 = b.buf("SQA2", BF16, D)
    COLS = b.buf("COLS", F32, 8)
    XT2 = b.buf("XT2", F32, D)
    wo_k = WS1.k()

    TSA = b.cbuf("TSA", BF16, 4, 512)
    for ic in range(T // ICW):
        i0 = ic * ICW

        def norm_part(h):
            qc, prow = h // 2, (h % 2) * 64
            obank = 4 + (h % 2)
            b.op("DVE", "reciprocal", RROW.ap(0, ICW, 0, 1),
                 b.pf(obank, 0, ICW, 64, 65), r=b.pk(obank),
                 w=RROW.k() + b.pk(obank))
            b.mm(6, cst["ROW2"].ap(128, 192, 0, 1), RROW.ap(0, ICW, 0, 1),
                 True, True, r=cst["ROW2"].k() + RROW.k(),
                 out=b.pf(6, 0, ICW, 0, 64))
            b.op("DVE", "tensor_copy", OSC.ap(0, ICW, 0, 64),
                 b.pf(6, 0, ICW, 0, 64), r=b.pk(6), w=OSC.k() + b.pk(6))
            b.op("DVE", "tensor_tensor",
                 OFM.ap(qc * ICW, (qc + 1) * ICW, prow, prow + 64),
                 b.pf(obank, 0, ICW, 0, 64), OSC.ap(0, ICW, 0, 64), ALU.mult,
                 r=OSC.k() + b.pk(obank),
                 w=OFM.k(qc * ICW, (qc + 1) * ICW) + b.pk(obank))

        def av_wave(h, jts):
            obank = 4 + (h % 2)
            for jt in jts:
                b.mm(obank, V.ap(jt * H * 65 + h * 65, jt * H * 65 + h * 65 + 65),
                     PT.ap(jt * ICW, (jt + 1) * ICW), jt == 0, jt == NT - 1,
                     r=V.k(jt * H * 65 + h * 65, jt * H * 65 + (h + 1) * 65)
                       + PT.k(jt * ICW, (jt + 1) * ICW),
                     out=b.pf(obank, 0, ICW, 0, 65))

        def s_wave(h, prs):
            qc, prow = h // 2, (h % 2) * 64
            for p in prs:
                for half in range(2):
                    jt = 2 * p + half
                    sb = p % 4
                    b.mm(sb, K.ap(qc * T + jt * 128, qc * T + (jt + 1) * 128,
                                  prow, prow + 64),
                         Q.ap(qc * T + i0, qc * T + i0 + ICW, prow, prow + 64),
                         True, True,
                         r=K.k(qc * T + jt * 128, qc * T + (jt + 1) * 128)
                           + Q.k(qc * T + i0, qc * T + i0 + ICW),
                         out=b.pf(sb, half * ICW, (half + 1) * ICW))

        def act_wave(prs):
            for p in prs:
                sb = p % 4
                b.op("ACT", "activation", TSA.ap((p % 4) * 512, (p % 4 + 1) * 512),
                     b.pf(sb, 0, 2 * ICW), AF.Tanh, scale=1.0 / 50.0,
                     r=b.pk(sb), w=TSA.k((p % 4) * 512, (p % 4 + 1) * 512) + b.pk(sb))
            for p in prs:
                b.op("ACT", "activation", PT.ap(2 * p * ICW, 2 * (p + 1) * ICW),
                     TSA.ap((p % 4) * 512, (p % 4 + 1) * 512), AF.Exp, scale=6.25,
                     r=TSA.k((p % 4) * 512, (p % 4 + 1) * 512),
                     w=PT.k(2 * p * ICW, 2 * (p + 1) * ICW))

        for h in range(H):
            s_wave(h, range(0, 4))
            if h > 0:
                av_wave(h - 1, range(0, 8))
            act_wave(range(0, 4))
            s_wave(h, range(4, 8))
            if h > 0:
                av_wave(h - 1, range(8, 16))
            act_wave(range(4, 8))
            if h > 0:
                norm_part(h - 1)
        av_wave(H - 1, range(0, 16))
        norm_part(H - 1)
        for t2 in range(2):
            tt = ic * 2 + t2
            b.dma("Sq", XT2.ap(),
                  d["x0"][t_off + tt * 128: t_off + (tt + 1) * 128, :],
                  w=XT2.k())
            for dcol in range(2):
                bank = 2 + dcol
                for c in range(NCH):
                    b.mm(bank, OFM.ap(c * ICW + t2 * 128, c * ICW + t2 * 128 + 128),
                         WS1.ap(c * D + dcol * 512, c * D + dcol * 512 + 512),
                         c == 0, c == NCH - 1,
                         r=OFM.k(c * ICW, (c + 1) * ICW) + wo_k)
                b.op("DVE", "tensor_tensor", ZTM.ap(dcol * 512, dcol * 512 + 512),
                     b.pf(bank, 0, 512),
                     XT2.ap(dcol * 512, dcol * 512 + 512), ALU.add,
                     r=XT2.k() + b.pk(bank),
                     w=ZTM.k(dcol * 512, dcol * 512 + 512) + b.pk(bank))
            if "z0" in DBG:
                key = ("DBG", "z0", bi, tt)
                b.dbg_keys.append(key)
                b.dma("SOq", b.dram["dbg_z0"][bi * NT * 128 + tt * 128:
                                             bi * NT * 128 + (tt + 1) * 128, 0:D],
                      ZTM.ap(), r=ZTM.k(), w=[key])
            layernorm_tm(b, cst, ZTM, SQA2, COLS)
            fm_transpose_in(b, cst, ZTM, 0, RES, tt)
    b.freebuf("PT", "OFM", "OSC", "RROW", "ZTM", "SQA2", "COLS", "XT2", "TSA",
              "Q", "K", "V", "WSA1")
    dump_fm(b, "x1", bi, RES)


def stage_b(b, cst, bi):
    d = b.dram
    RES = b.bufs["RES"]

    COS = b.buf("COS", F32, 512)
    SIN = b.buf("SIN", F32, 512)
    b.dma("Sq", COS.ap(), d["c_cos"], w=COS.k())
    b.dma("Sq", SIN.ap(), d["c_sin"], w=SIN.k())
    cst = dict(cst, COS=COS, SIN=SIN)
    SQ = b.buf("SQB0", F32, 512)
    ROWR = b.buf("ROWRB", F32, T)
    RSTDB = b.buf("RSTDBB", F32, T)
    RN1 = b.cbuf("RN1", BF16, NCH, T)
    rmsnorm_rows(b, cst, RES, SQ, ROWR, 2, 1)
    bcast_row(b, cst, ROWR, RSTDB, 3)
    for ch in range(NCH):
        for tc in range(4):
            c0 = ch * T + tc * 512
            b.op("DVE", "scalar_tensor_tensor",
                 RN1.ap(c0, c0 + 512), RES.ap(c0, c0 + 512),
                 cst["GAIN"].ap(8 + ch, 9 + ch), RSTDB.ap(tc * 512, tc * 512 + 512),
                 op0=ALU.mult, op1=ALU.mult,
                 r=RES.k(c0, c0 + 512) + cst["GAIN"].k()
                   + RSTDB.k(tc * 512, tc * 512 + 512),
                 w=RN1.k(c0, c0 + 512))
    b.freebuf("SQB0", "ROWRB", "RSTDBB")

    WKV = b.buf("WKV", BF16, 8 * L)
    KV = b.buf("KV", BF16, 2 * T)
    b.dma("Pq", WKV.ap(), d["wkv_f"], w=WKV.k())
    for lc in range(2):
        for tc in range(4):
            bank = tc % 2
            for dc in range(NCH):
                b.mm(bank, WKV.ap(dc * L + lc * 128, dc * L + (lc + 1) * 128),
                     RN1.ap(dc * T + tc * 512, dc * T + tc * 512 + 512),
                     dc == 0, dc == NCH - 1,
                     r=WKV.k() + RN1.k(dc * T + tc * 512, dc * T + tc * 512 + 512))
            c0 = lc * T + tc * 512
            b.op("DVE", "tensor_scalar", KV.ap(c0, c0 + 512), b.pf(bank, 0, 512),
                 0.25, None, ALU.mult, r=b.pk(bank), w=KV.k(c0, c0 + 512) + b.pk(bank))
    b.freebuf("WKV")

    WQF = b.buf("WQF", BF16, 8 * D)
    WKU = b.buf("WKU", BF16, 2 * D)
    WVU = b.buf("WVU", BF16, 2 * D)
    Q = b.cbuf("Q", BF16, NCH, T)
    QST = b.buf("QST", BF16, 2 * D)
    TMP = b.buf("TMPR", F32, 4 * 512)
    b.dma("Pq", WQF.ap(), d["wq_f"], w=WQF.k())
    b.dma("Pq", WKU.ap(), d["wk_up"], w=WKU.k())
    b.dma("Pq", WVU.ap(), d["wv_up"], w=WVU.k())

    def rope_tile(tt, sl):
        base = sl * D
        kst = QST.k(base, base + D)
        x1v = QST.ap3(base, [(64, 16), (1, 32)])
        x2v = QST.ap3(base + 32, [(64, 16), (1, 32)])
        cosb = cst["COS"].ap3(tt * 32, [(0, 16), (1, 32)])
        sinb = cst["SIN"].ap3(tt * 32, [(0, 16), (1, 32)])
        tvs = [TMP.ap3(i * 512, [(32, 16), (1, 32)]) for i in range(4)]
        tks = [TMP.k(i * 512, (i + 1) * 512) for i in range(4)]
        b.op("DVE", "tensor_tensor", tvs[0], x1v, cosb, ALU.mult,
             r=kst + cst["COS"].k(), w=tks[0])
        b.op("DVE", "tensor_tensor", tvs[1], x2v, sinb, ALU.mult,
             r=kst + cst["SIN"].k(), w=tks[1])
        b.op("DVE", "tensor_tensor", tvs[2], x1v, sinb, ALU.mult,
             r=kst + cst["SIN"].k(), w=tks[2])
        b.op("DVE", "tensor_tensor", tvs[3], x2v, cosb, ALU.mult,
             r=kst + cst["COS"].k(), w=tks[3])
        b.op("DVE", "tensor_tensor", x1v, tvs[0], tvs[1], ALU.subtract,
             r=tks[0] + tks[1], w=kst)
        b.op("DVE", "tensor_tensor", x2v, tvs[2], tvs[3], ALU.add,
             r=tks[2] + tks[3], w=kst)

    def fm_transpose_bf16(SRC, sc0, DSTFM, tt):
        for g in range(2):
            bank = 6 + g
            for c in range(4):
                ch = g * 4 + c
                b.transp(bank, SRC.ap(sc0 + ch * 128, sc0 + (ch + 1) * 128),
                         cst["EYEB"].ap(), b.pb(bank, c * 128, (c + 1) * 128),
                         r=SRC.k(sc0, sc0 + D) + cst["EYEB"].k())
            for c in range(4):
                ch = g * 4 + c
                c0 = ch * T + tt * 128
                b.op("DVE", "tensor_copy", DSTFM.ap(c0, c0 + 128),
                     b.pb(bank, c * 128, (c + 1) * 128), r=b.pk(bank),
                     w=DSTFM.k(c0, c0 + 128) + b.pk(bank))

    for tt in range(NT):
        sl = tt % 2
        for fcol in range(2):
            bank = fcol
            for dc in range(NCH):
                b.mm(bank, RN1.ap(dc * T + tt * 128, dc * T + (tt + 1) * 128),
                     WQF.ap(dc * D + fcol * 512, dc * D + fcol * 512 + 512),
                     dc == 0, dc == NCH - 1,
                     r=WQF.k() + RN1.k(dc * T + tt * 128, dc * T + (tt + 1) * 128))
            b.op("DVE", "tensor_copy",
                 QST.ap(sl * D + fcol * 512, sl * D + fcol * 512 + 512),
                 b.pf(bank, 0, 512), r=b.pk(bank),
                 w=QST.k(sl * D + fcol * 512, sl * D + fcol * 512 + 512) + b.pk(bank))
        rope_tile(tt, sl)
        fm_transpose_bf16(QST, sl * D, Q, tt)
    b.freebuf("RN1", "WQF")
    K = b.cbuf("K", BF16, NCH, T)
    for tt in range(NT):
        sl = tt % 2
        for fcol in range(2):
            bank = fcol
            for lc in range(2):
                b.mm(bank, KV.ap(lc * T + tt * 128, lc * T + (tt + 1) * 128),
                     WKU.ap(lc * D + fcol * 512, lc * D + fcol * 512 + 512),
                     lc == 0, lc == 1,
                     r=WKU.k() + KV.k(lc * T + tt * 128, lc * T + (tt + 1) * 128))
            b.op("DVE", "tensor_copy",
                 QST.ap(sl * D + fcol * 512, sl * D + fcol * 512 + 512),
                 b.pf(bank, 0, 512), r=b.pk(bank),
                 w=QST.k(sl * D + fcol * 512, sl * D + fcol * 512 + 512) + b.pk(bank))
        rope_tile(tt, sl)
        fm_transpose_bf16(QST, sl * D, K, tt)
    b.freebuf("WKU")
    V = b.cbuf("V", BF16, NT, H * 65)
    for tt in range(NT):
        b.op("DVE", "memset", V.ap3(tt * H * 65 + 64, [(65, 16), (1, 1)]), 1.0,
             w=V.k(tt * H * 65, (tt + 1) * H * 65))
        for fcol in range(2):
            bank = fcol
            for lc in range(2):
                b.mm(bank, KV.ap(lc * T + tt * 128, lc * T + (tt + 1) * 128),
                     WVU.ap(lc * D + fcol * 512, lc * D + fcol * 512 + 512),
                     lc == 0, lc == 1,
                     r=WVU.k() + KV.k(lc * T + tt * 128, lc * T + (tt + 1) * 128))
            dst = V.ap3(tt * H * 65 + fcol * 8 * 65, [(65, 8), (1, 64)])
            b.op("DVE", "tensor_copy", dst, b.pf(bank, 0, 512), r=b.pk(bank),
                 w=V.k(tt * H * 65, (tt + 1) * H * 65) + b.pk(bank))
    b.freebuf("QST", "TMPR", "WVU", "KV", "COS", "SIN")

    P = b.buf("P", BF16, T)
    PT = b.buf("PTB", BF16, T)
    OFMB = b.cbuf("OFMB", BF16, NCH * 4, 512)
    RROW = b.buf("RROWB", F32, 128)
    OSC = b.buf("OSCB", F32, 128)
    CB = b.buf("CB", F32, 8)

    RROW = b.bufs["RROWB"]
    OSC = b.bufs["OSCB"]

    def stg_s(it, h):
        qc, prow = h // 2, (h % 2) * 64
        jmax = (it + 1) * 128
        njc = (jmax + 511) // 512
        for jc in range(njc):
            jw = min(512, jmax - jc * 512)
            b.mm(jc, Q.ap(qc * T + it * 128, qc * T + (it + 1) * 128,
                          prow, prow + 64),
                 K.ap(qc * T + jc * 512, qc * T + jc * 512 + jw, prow, prow + 64),
                 True, True,
                 r=Q.k(qc * T + it * 128, qc * T + (it + 1) * 128)
                   + K.k(qc * T + jc * 512, qc * T + jc * 512 + jw),
                 out=b.pf(jc, 0, jw))
        jcd = (it * 128) // 512
        coff = it * 128 - jcd * 512
        b.op("DVE", "tensor_tensor", b.pf(jcd, coff, coff + 128),
             b.pf(jcd, coff, coff + 128), cst["MASK"].ap(), ALU.add,
             r=cst["MASK"].k() + b.pk(jcd), w=b.pk(jcd))
        for jc in range(njc):
            jw = min(512, jmax - jc * 512)
            b.op("DVE", "tensor_reduce", CB.ap(jc, jc + 1), b.pf(jc, 0, jw),
                 axis=AX.X, op=ALU.max, r=b.pk(jc), w=CB.k() + b.pk(jc))
        for step in range(1, njc):
            b.op("DVE", "tensor_tensor", CB.ap(0, 1), CB.ap(0, 1),
                 CB.ap(step, step + 1), ALU.max, r=CB.k(), w=CB.k())
        b.op("DVE", "tensor_scalar", CB.ap(4, 5), CB.ap(0, 1), -0.125, None,
             ALU.mult, r=CB.k(), w=CB.k())

    def stg_exp(it, h):
        jmax = (it + 1) * 128
        njc = (jmax + 511) // 512
        for jc in range(njc):
            jw = min(512, jmax - jc * 512)
            b.op("ACT", "activation", P.ap(jc * 512, jc * 512 + jw),
                 b.pf(jc, 0, jw), AF.Exp, scale=0.125, bias=CB.ap(4, 5),
                 r=CB.k() + b.pk(jc), w=P.k(jc * 512, jc * 512 + jw) + b.pk(jc))

    def stg_transp(it, h):
        for g in range((it + 1 + 3) // 4):
            gw = min(4, it + 1 - g * 4)
            for c in range(gw):
                jt = g * 4 + c
                b.transp(6, P.ap(jt * 128, (jt + 1) * 128), cst["EYEB"].ap(),
                         b.pb(6, c * 128, (c + 1) * 128),
                         r=P.k(jt * 128, (jt + 1) * 128) + cst["EYEB"].k())
            b.op("DVE", "tensor_copy", PT.ap(g * 512, g * 512 + gw * 128),
                 b.pb(6, 0, gw * 128), r=b.pk(6),
                 w=PT.k(g * 512, g * 512 + gw * 128) + b.pk(6))

    def stg_av(it, h, ob):
        qc, prow = h // 2, (h % 2) * 64
        for jt in range(it + 1):
            b.mm(ob, V.ap(jt * H * 65 + h * 65, jt * H * 65 + h * 65 + 65),
                 PT.ap(jt * 128, (jt + 1) * 128), jt == 0, jt == it,
                 r=V.k(jt * H * 65, (jt + 1) * H * 65)
                   + PT.k(jt * 128, (jt + 1) * 128),
                 out=b.pf(ob, 0, 128, 0, 65))
        rinv(b, cst, RROW.ap(0, 128, 0, 1), b.pf(ob, 0, 128, 64, 65),
             RROW.k(), b.pk(ob))
        b.mm(4, cst["ROW2"].ap(128, 192, 0, 1), RROW.ap(0, 128, 0, 1),
             True, True, r=cst["ROW2"].k() + RROW.k(),
             out=b.pf(4, 0, 128, 0, 64))
        b.op("ACT", "copy", OSC.ap(0, 128, 0, 64), b.pf(4, 0, 128, 0, 64),
             r=b.pk(4), w=OSC.k() + b.pk(4))
        b.op("DVE", "tensor_tensor",
             OFMB.ap(qc * T + it * 128, qc * T + (it + 1) * 128, prow, prow + 64),
             b.pf(ob, 0, 128, 0, 64), OSC.ap(0, 128, 0, 64), ALU.mult,
             r=OSC.k() + b.pk(ob),
             w=OFMB.k(qc * T + it * 128, qc * T + (it + 1) * 128) + b.pk(ob))

    jobs = [(it, h) for it in range(NT) for h in range(H)]
    for k in range(len(jobs) + 3):
        if k >= 2 and k - 2 < len(jobs):
            stg_transp(*jobs[k - 2])
        if k >= 1 and k - 1 < len(jobs):
            stg_exp(*jobs[k - 1])
        if k >= 3:
            stg_av(*jobs[k - 3], 5 if (k - 3) % 2 == 0 else 7)
        if k < len(jobs):
            stg_s(*jobs[k])
    b.freebuf("P", "PTB", "RROWB", "OSCB", "CB", "Q", "K", "V")

    WOF = b.buf("WOF", BF16, 8 * D)
    X2B = b.cbuf("X2B", BF16, NCH * 4, 512)
    X2SC = b.buf("X2SC", F32, 512)
    SQB = b.buf("SQB", F32, 512)
    ROWR2 = b.buf("ROWR2", F32, T)
    b.dma("Pq", WOF.ap(), d["wo_f"], w=WOF.k())
    for grp in range(4):
        g0 = grp * 512
        for dch in range(NCH):
            bank = 2 + (dch % 2)
            for c in range(NCH):
                b.mm(bank, WOF.ap(c * D + dch * 128, c * D + (dch + 1) * 128),
                     OFMB.ap(c * T + g0, c * T + g0 + 512),
                     c == 0, c == NCH - 1,
                     r=WOF.k() + OFMB.k(c * T + g0, c * T + g0 + 512))
            c0 = dch * T + g0
            b.op("DVE", "tensor_tensor", X2SC.ap(0, 512), b.pf(bank, 0, 512),
                 RES.ap(c0, c0 + 512), ALU.add,
                 r=RES.k(c0, c0 + 512) + b.pk(bank), w=X2SC.k() + b.pk(bank))
            b.op("DVE", "tensor_copy", X2B.ap(c0, c0 + 512), X2SC.ap(0, 512),
                 r=X2SC.k(), w=X2B.k(c0, c0 + 512))
            b.op("ACT", "activation", SQB.ap(0, 512), X2SC.ap(0, 512), AF.Square,
                 r=X2SC.k(), w=SQB.k())
            b.mm(7, cst["ONESC"].ap(0, 1), SQB.ap(0, 512), dch == 0,
                 dch == NCH - 1, r=cst["ONESC"].k() + SQB.k(),
                 out=b.pf(7, 0, 512, 0, 1))
            b.op("DVE", "tensor_tensor", RES.ap(c0, c0 + 512), RES.ap(c0, c0 + 512),
                 X2SC.ap(0, 512), ALU.add,
                 r=RES.k(c0, c0 + 512) + X2SC.k(), w=RES.k(c0, c0 + 512))
        b.op("ACT", "activation", ROWR2.ap(g0, g0 + 512, 0, 1),
             b.pf(7, 0, 512, 0, 1), AF.Ln,
             bias=cst["EPSC"].ap(1, 2, 0, 1), scale=1.0 / D,
             r=cst["EPSC"].k() + b.pk(7), w=ROWR2.k(g0, g0 + 512) + b.pk(7))
        b.op("ACT", "activation", ROWR2.ap(g0, g0 + 512, 0, 1),
             ROWR2.ap(g0, g0 + 512, 0, 1), AF.Exp, scale=-0.5,
             r=ROWR2.k(g0, g0 + 512), w=ROWR2.k(g0, g0 + 512))
    b.freebuf("WOF", "OFMB", "X2SC", "SQB")
    RSTDB2 = b.buf("RSTDB2", F32, T)
    bcast_row(b, cst, ROWR2, RSTDB2, 3)
    for ch in range(NCH):
        for tc in range(4):
            c0 = ch * T + tc * 512
            b.op("DVE", "scalar_tensor_tensor", X2B.ap(c0, c0 + 512),
                 X2B.ap(c0, c0 + 512), cst["GAIN"].ap(16 + ch, 17 + ch),
                 RSTDB2.ap(tc * 512, tc * 512 + 512), op0=ALU.mult, op1=ALU.mult,
                 r=X2B.k(c0, c0 + 512) + cst["GAIN"].k()
                   + RSTDB2.k(tc * 512, tc * 512 + 512),
                 w=X2B.k(c0, c0 + 512))
    b.freebuf("ROWR2", "RSTDB2")
    dump_fm(b, "x2s", bi, RES)
    dump_bf(b, "rn2", bi, X2B)


def stage_c(b, cst, bi):
    d = b.dram
    RES = b.bufs["RES"]
    RN2 = b.bufs["X2B"]
    C = 1024
    ZBASE = 8 * C

    WRS = b.buf("WRS", BF16, 8 * E)
    I128 = b.buf("I128", F32, T)
    R16 = b.buf("R16", F32, 32)
    SROWS = b.buf("SROWS", F32, T)
    MR = b.buf("MR", F32, T)
    RNK = b.buf("RNK", F32, T)
    ROWS4 = b.buf("ROWS4", F32, 4 * T)
    ESC = b.buf("ESC", F32, 16)
    DCOL = b.buf("DCOL", mybir.dt.uint32, 16)
    PRB = b.buf("PRB", F32, 8)
    EQ = b.buf("EQ", F32, 8)
    CC = b.buf("CC", F32, 8)
    SMAT = b.buf("SMAT", F32, 16)
    SM2 = b.buf("SM2", F32, 16)
    b.dma("Pq", WRS.ap(), d["wr"], w=WRS.k())
    b.dma("Pq", I128.ap(0, T, 0, 1), d["c_i128"], w=I128.k())
    b.dma("Pq", R16.ap(0, 32, 0, 16), d["c_r16"], w=R16.k())
    PROBS = b.buf("PROBS", F32, 16 * E)
    PCOL = b.buf("PCOL", F32, 16)
    RTW = b.buf("RTW", F32, 16)
    b.op("DVE", "memset", SMAT.ap(), 0.0, w=SMAT.k())
    b.op("DVE", "memset", SM2.ap(), 0.0, w=SM2.k())
    for tt in range(NT):
        bank = 6 + tt % 2
        for dc in range(NCH):
            b.mm(bank, RN2.ap(dc * T + tt * 128, dc * T + (tt + 1) * 128),
                 WRS.ap(dc * E, (dc + 1) * E), dc == 0, dc == NCH - 1,
                 r=WRS.k() + RN2.k(dc * T + tt * 128, dc * T + (tt + 1) * 128),
                 out=b.pf(bank, 0, E))
        b.op("DVE", "tensor_tensor", PROBS.ap(tt * E, (tt + 1) * E),
             b.pf(bank, 0, E), cst["RB8"].ap(), ALU.add,
             r=cst["RB8"].k() + b.pk(bank),
             w=PROBS.k(tt * E, (tt + 1) * E) + b.pk(bank))
    b.op("DVE", "tensor_scalar", PROBS.ap(), PROBS.ap(), -50.0, 50.0,
         ALU.max, ALU.min, r=PROBS.k(), w=PROBS.k())
    b.op("ACT", "activation", PROBS.ap(), PROBS.ap(), AF.Sigmoid,
         r=PROBS.k(), w=PROBS.k())
    for tt in range(NT):
        b.op("DVE", "tensor_reduce", PCOL.ap(tt, tt + 1),
             PROBS.ap(tt * E, (tt + 1) * E), axis=AX.X, op=ALU.max,
             r=PROBS.k(), w=PCOL.k())
    b.op("DVE", "tensor_scalar", PCOL.ap(), PCOL.ap(), 1e-8, 1.0,
         ALU.max, ALU.min, r=PCOL.k(), w=PCOL.k())
    b.op("DVE", "tensor_scalar", RTW.ap(), PCOL.ap(), 0.5, 1e-8,
         ALU.add, ALU.add, r=PCOL.k(), w=RTW.k())
    rinv(b, cst, RTW.ap(), RTW.ap(), RTW.k(), RTW.k())
    b.op("DVE", "tensor_tensor", ESC.ap(), PCOL.ap(), RTW.ap(), ALU.mult,
         r=PCOL.k() + RTW.k(), w=ESC.k())
    for tt in range(NT):
        b.op("DVE", "tensor_scalar", EQ.ap(), PROBS.ap(tt * E, (tt + 1) * E),
             PCOL.ap(tt, tt + 1), None, ALU.is_equal,
             r=PROBS.k() + PCOL.k(), w=EQ.k())
        b.op("DVE", "tensor_scalar", EQ.ap(), EQ.ap(), -1e9, 1e9,
             ALU.mult, ALU.add, r=EQ.k(), w=EQ.k())
        b.op("DVE", "tensor_tensor", EQ.ap(), EQ.ap(), cst["IO8"].ap(), ALU.add,
             r=EQ.k() + cst["IO8"].k(), w=EQ.k())
        b.op("DVE", "tensor_reduce", CC.ap(1, 2), EQ.ap(), axis=AX.X, op=ALU.min,
             r=EQ.k(), w=CC.k())
        b.op("DVE", "tensor_scalar", SMAT.ap(8, 9), RTW.ap(tt, tt + 1), 0.5, None,
             ALU.mult, r=RTW.k(), w=SMAT.k())
        for e in range(E):
            b.op("DVE", "tensor_scalar", SM2.ap(e, e + 1), CC.ap(1, 2),
                 float(e), None, ALU.is_equal, r=CC.k(), w=SM2.k())
        b.op("DVE", "tensor_copy", SM2.ap(8, 9), CC.ap(1, 2),
             r=CC.k(), w=SM2.k())
        b.transp(7, SMAT.ap(), cst["EYEF"].ap(), b.pf(7, 0, 128, 0, 16),
                 r=SMAT.k() + cst["EYEF"].k())
        b.op("DVE", "tensor_copy", SROWS.ap(tt * 128, (tt + 1) * 128, 0, 16),
             b.pf(7, 0, 128, 0, 16), r=b.pk(7),
             w=SROWS.k(tt * 128, (tt + 1) * 128) + b.pk(7))
        b.transp(6, SM2.ap(), cst["EYEF"].ap(), b.pf(6, 0, 128, 0, 16),
                 r=SM2.k() + cst["EYEF"].k())
        b.op("DVE", "tensor_copy", MR.ap(tt * 128, (tt + 1) * 128, 0, 16),
             b.pf(6, 0, 128, 0, 16), r=b.pk(6),
             w=MR.k(tt * 128, (tt + 1) * 128) + b.pk(6))
    b.freebuf("WRS", "PRB", "EQ", "SMAT", "SM2", "PROBS", "PCOL", "RTW")

    ones16 = bass.AP(cst["ONESC"].ap().tensor, cst["ONESC"].ap().offset,
                     [list(cst["ONESC"].ap().ap[0])[:1] + [16], [0, T]])
    b.op("DVE", "tensor_tensor_scan", RNK.ap(0, T, 0, 16), ones16,
         MR.ap(0, T, 0, 16), 0.0, op0=ALU.mult, op1=ALU.add,
         r=cst["ONESC"].k() + MR.k(0, T), w=RNK.k(0, T))
    b.op("DVE", "tensor_tensor", RNK.ap(0, T, 0, 16), RNK.ap(0, T, 0, 16),
         MR.ap(0, T, 0, 16), ALU.subtract, r=RNK.k(0, T) + MR.k(0, T),
         w=RNK.k(0, T))
    for tc in range(4):
        b.mm(7, R16.ap(8, 9, 0, 16), MR.ap(tc * 512, tc * 512 + 512, 0, 16),
             True, True, r=R16.k() + MR.k(tc * 512, tc * 512 + 512),
             out=b.pf(7, 0, 512, 0, 1))
        b.op("DVE", "tensor_copy", ROWS4.ap(tc * 512, tc * 512 + 512, 0, 1),
             b.pf(7, 0, 512, 0, 1), r=b.pk(7),
             w=ROWS4.k(tc * 512, tc * 512 + 512) + b.pk(7))
    b.op("DVE", "tensor_tensor", MR.ap(0, T, 0, 16), MR.ap(0, T, 0, 16),
         RNK.ap(0, T, 0, 16), ALU.mult, r=MR.k(0, T) + RNK.k(0, T),
         w=MR.k(0, T))
    for tc in range(4):
        b.mm(7, R16.ap(16, 17, 0, 16), MR.ap(tc * 512, tc * 512 + 512, 0, 16),
             True, True, r=R16.k() + MR.k(tc * 512, tc * 512 + 512),
             out=b.pf(7, 0, 512, 0, 1))
        b.op("DVE", "tensor_copy", ROWS4.ap(T + tc * 512, T + tc * 512 + 512, 0, 1),
             b.pf(7, 0, 512, 0, 1), r=b.pk(7),
             w=ROWS4.k(T + tc * 512, T + tc * 512 + 512) + b.pk(7))
    r_sel = ROWS4.ap(0, T, 0, 1)
    r_rnk = ROWS4.ap(T, 2 * T, 0, 1)
    r_d = ROWS4.ap(2 * T, 3 * T, 0, 1)
    r_t = ROWS4.ap(3 * T, 4 * T, 0, 1)
    kS, kR = ROWS4.k(0, T), ROWS4.k(T, 2 * T)
    kD, kT = ROWS4.k(2 * T, 3 * T), ROWS4.k(3 * T, 4 * T)
    b.op("DVE", "scalar_tensor_tensor", r_d, r_sel, float(C), r_rnk,
         op0=ALU.mult, op1=ALU.add, r=kS + kR, w=kD)
    b.op("DVE", "tensor_scalar", r_t, r_rnk, float(C), None, ALU.is_lt,
         r=kR, w=kT)
    b.op("DVE", "tensor_tensor", r_d, r_d, r_t, ALU.mult, r=kD + kT, w=kD)
    b.op("DVE", "tensor_scalar", r_t, r_t, -1.0, 1.0, ALU.mult, ALU.add,
         r=kT, w=kT)
    b.op("DVE", "tensor_tensor", r_t, r_t, I128.ap(0, T, 0, 1), ALU.mult,
         r=kT + I128.k(), w=kT)
    b.op("DVE", "tensor_tensor", r_d, r_d, r_t, ALU.add, r=kD + kT, w=kD)
    for tt in range(NT):
        b.transp(7, ROWS4.ap(2 * T + tt * 128, 2 * T + (tt + 1) * 128, 0, 1),
                 cst["EYEF"].ap(0, 1, 0, 1), b.pf(7, 0, 1, 0, 128),
                 r=kD + cst["EYEF"].k())
        b.op("DVE", "tensor_copy", DCOL.ap(tt, tt + 1), b.pf(7, 0, 1, 0, 128),
             r=b.pk(7), w=DCOL.k() + b.pk(7))
    if "mrd" in DBG:
        key = ("DBG", "mrd", bi, 0)
        b.dbg_keys.append(key)
        b.dma("SOq", b.dram["dbg_mrd"][bi * NT * 128: bi * NT * 128 + 16, 0:T],
              MR.ap(0, T, 0, 16), r=MR.k(0, T), w=[key])
        key2 = ("DBG", "mrd", bi, 1)
        b.dbg_keys.append(key2)
        b.dma("SOq", b.dram["dbg_mrd"][bi * NT * 128 + 16: bi * NT * 128 + 20, 0:T],
              ROWS4.ap3(0, [(T, 4), (1, T)], 0, 1), r=ROWS4.k(0, 4 * T), w=[key2])
    b.freebuf("MR", "RNK", "ROWS4", "I128", "R16")

    EXP9 = b.buf("EXP9", F32, 9 * 128)
    b.dma("Sq", EXP9.ap(0, 9 * 128, 0, 16), d["c_exp9"], w=EXP9.k())
    SBC = b.buf("SBC", BF16, T)
    for tc in range(4):
        b.mm(7, EXP9.ap(8 * 128, 9 * 128, 0, 16),
             SROWS.ap(tc * 512, tc * 512 + 512, 0, 16), True, True,
             r=EXP9.k() + SROWS.k(tc * 512, tc * 512 + 512))
        b.op("DVE", "tensor_copy", SBC.ap(tc * 512, tc * 512 + 512),
             b.pf(7, 0, 512), r=b.pk(7),
             w=SBC.k(tc * 512, tc * 512 + 512) + b.pk(7))
    b.freebuf("SROWS", "EXP9")

    if bi == 0:
        ZT = b.buf("ZT", BF16, D)
        b.op("DVE", "memset", ZT.ap(), 0.0, w=ZT.k())
        for rr in range(65):
            b.dma("SOq", d["RG"][rr * 128:(rr + 1) * 128, :], ZT.ap(),
                  r=ZT.k(), w=[("RG",)])
        b.dma("SOq", d["OGT"][ZBASE:ZBASE + 128, :], ZT.ap(), r=ZT.k(),
              w=[("OGTZ",)])
        b.freebuf("ZT")
    TMT = b.buf("TMT", BF16, 2 * D)
    for tt in range(NT):
        sl = tt % 2
        for g in range(2):
            bank = 6 + g
            for c in range(4):
                ch = g * 4 + c
                b.transp(bank, RN2.ap(ch * T + tt * 128, ch * T + (tt + 1) * 128),
                         cst["EYEB"].ap(), b.pb(bank, c * 128, (c + 1) * 128),
                         r=RN2.k(ch * T + tt * 128, ch * T + (tt + 1) * 128)
                           + cst["EYEB"].k())
            b.op("DVE", "tensor_copy",
                 TMT.ap(sl * D + g * 512, sl * D + (g + 1) * 512),
                 b.pb(bank, 0, 512), r=b.pk(bank),
                 w=TMT.k(sl * D + g * 512, sl * D + (g + 1) * 512) + b.pk(bank))
        b.idma(d["RG"], DCOL.ap(tt, tt + 1), TMT.ap(sl * D, (sl + 1) * D), None,
               r=TMT.k(sl * D, (sl + 1) * D) + DCOL.k(), w=[("RG",)])
    b.freebuf("TMT")

    G0 = b.cbuf("G0", BF16, 8, 512)
    G1 = b.cbuf("G1", BF16, 8, 512)
    USC = b.buf("USC", BF16, 512)
    WSL = [b.buf(f"WC{i}", BF16, 8 * 1024) for i in range(4)]
    XG = XGT = OUTC = None
    gbufs = [G0, G1]

    steps = [(8, 0), (8, 1)] + [(e, F_) for e in range(E) for F_ in range(2)]

    def w_src(e, F_, which):
        nm = ["ew1", "ew3", "ew2"][which] if e < E else ["sw1", "sw3", "sw2"][which]
        ap = d[nm]
        if e < E:
            return ap[e:e + 1, F_:F_ + 1]
        return ap[F_:F_ + 1]

    def load_step(si):
        e, F_ = steps[si]
        for j in range(3):
            sl = WSL[(si * 3 + j) % 4]
            b.dma("Pq", sl.ap(), w_src(e, F_, j), w=sl.k())

    load_step(0)
    pend_w2 = []

    def emit_w2_shared(tc, gsl, w2s):
        g = gbufs[gsl]
        for dch in range(NCH):
            bank = 4 + (dch % 2)
            for fc in range(NCH):
                b.mm(bank, w2s.ap(fc * 1024 + dch * 128, fc * 1024 + (dch + 1) * 128),
                     g.ap(fc * 512, (fc + 1) * 512), fc == 0, fc == NCH - 1,
                     r=w2s.k() + g.k(fc * 512, (fc + 1) * 512))
            c0 = dch * T + tc * 512
            b.op("DVE", "tensor_tensor", RES.ap(c0, c0 + 512), b.pf(bank, 0, 512),
                 RES.ap(c0, c0 + 512), ALU.add,
                 r=RES.k(c0, c0 + 512) + b.pk(bank),
                 w=RES.k(c0, c0 + 512) + b.pk(bank))

    def emit_w2_routed(e, F_, tc, gsl, w2s):
        g = gbufs[gsl]
        for st in range(4):
            slot0 = tc * 512 + st * 128
            sti = slot0 // 128
            for dcol in range(2):
                bank = 4 + (st % 2)
                for fc in range(NCH):
                    b.mm(bank, g.ap(fc * 512 + st * 128, fc * 512 + (st + 1) * 128),
                         w2s.ap(fc * 1024 + dcol * 512, fc * 1024 + dcol * 512 + 512),
                         fc == 0, fc == NCH - 1,
                         r=w2s.k(fc * 512, (fc + 1) * 512)
                           + g.k(fc * 512, (fc + 1) * 512))
                oc0 = sti * D + dcol * 512
                if F_ == 0:
                    b.op("DVE", "tensor_copy", OUTC.ap(oc0, oc0 + 512),
                         b.pf(bank, 0, 512), r=b.pk(bank),
                         w=OUTC.k(oc0, oc0 + 512) + b.pk(bank))
                else:
                    b.op("DVE", "tensor_tensor", OUTC.ap(oc0, oc0 + 512),
                         b.pf(bank, 0, 512), OUTC.ap(oc0, oc0 + 512), ALU.add,
                         r=OUTC.k(oc0, oc0 + 512) + b.pk(bank),
                         w=OUTC.k(oc0, oc0 + 512) + b.pk(bank))
            if F_ == 1:
                row0 = e * C + slot0
                b.dma("Pq", d["OGT"][row0:row0 + 128, :], OUTC.ap(sti * D, (sti + 1) * D),
                      r=OUTC.k(sti * D, (sti + 1) * D), w=[("OGT", e, slot0)])

    for si, (e, F_) in enumerate(steps):
        w1s = WSL[(si * 3 + 0) % 4]
        w3s = WSL[(si * 3 + 1) % 4]
        w2s = WSL[(si * 3 + 2) % 4]
        shared = e == 8
        if e == 0 and F_ == 0:
            b.freebuf("X2B")
            XG = [b.cbuf("XGa", BF16, 8, C), b.cbuf("XGb", BF16, 8, C)]
            XGT = b.buf("XGT", BF16, 2 * D)
            OUTC = b.cbuf("OUTC", BF16, 8, D)
        def xg_fill(ee):
            xg = XG[ee % 2]
            for st in range(8):
                sl2 = st % 2
                b.dma("Sq", XGT.ap(sl2 * D, (sl2 + 1) * D),
                      d["RG"][ee * C + st * 128: ee * C + (st + 1) * 128, :],
                      r=[("RG",)], w=XGT.k(sl2 * D, (sl2 + 1) * D))
                for g2 in range(2):
                    bank = 6 + g2
                    for c2 in range(4):
                        ch2 = g2 * 4 + c2
                        b.transp(bank,
                                 XGT.ap(sl2 * D + ch2 * 128, sl2 * D + (ch2 + 1) * 128),
                                 cst["EYEB"].ap(), b.pb(bank, c2 * 128, (c2 + 1) * 128),
                                 r=XGT.k(sl2 * D, (sl2 + 1) * D) + cst["EYEB"].k())
                    for c2 in range(4):
                        ch2 = g2 * 4 + c2
                        c0x = ch2 * C + st * 128
                        b.op("DVE", "tensor_copy", xg.ap(c0x, c0x + 128),
                             b.pb(bank, c2 * 128, (c2 + 1) * 128), r=b.pk(bank),
                             w=xg.k(c0x, c0x + 128) + b.pk(bank))

        if e == 0 and F_ == 0:
            xg_fill(0)
        if not shared and F_ == 1 and e + 1 < E:
            xg_fill(e + 1)
        SRC = RN2 if shared else XG[e % 2]
        ntc = 4 if shared else 2
        srcw = T if shared else C
        for tc in range(ntc):
            gsl = tc % 2
            g = gbufs[gsl]
            for fc in range(NCH):
                hb = fc % 2
                ub = 2 + (fc % 2)
                for dc in range(NCH):
                    b.mm(hb, w1s.ap(dc * 1024 + fc * 128, dc * 1024 + (fc + 1) * 128),
                         SRC.ap(dc * srcw + tc * 512, dc * srcw + tc * 512 + 512),
                         dc == 0, dc == NCH - 1,
                         r=w1s.k() + SRC.k(dc * srcw + tc * 512,
                                           dc * srcw + tc * 512 + 512))
                b.op("ACT", "activation", g.ap(fc * 512, (fc + 1) * 512),
                     b.pf(hb, 0, 512), AF.Silu, r=b.pk(hb),
                     w=g.k(fc * 512, (fc + 1) * 512) + b.pk(hb))
                for dc in range(NCH):
                    b.mm(ub, w3s.ap(dc * 1024 + fc * 128, dc * 1024 + (fc + 1) * 128),
                         SRC.ap(dc * srcw + tc * 512, dc * srcw + tc * 512 + 512),
                         dc == 0, dc == NCH - 1,
                         r=w3s.k() + SRC.k(dc * srcw + tc * 512,
                                           dc * srcw + tc * 512 + 512))
                if shared:
                    b.op("DVE", "tensor_tensor", USC.ap(0, 512), b.pf(ub, 0, 512),
                         SBC.ap(tc * 512, tc * 512 + 512), ALU.mult,
                         r=SBC.k(tc * 512, tc * 512 + 512) + b.pk(ub),
                         w=USC.k() + b.pk(ub))
                    b.op("DVE", "tensor_tensor", g.ap(fc * 512, (fc + 1) * 512),
                         g.ap(fc * 512, (fc + 1) * 512), USC.ap(0, 512), ALU.mult,
                         r=g.k(fc * 512, (fc + 1) * 512) + USC.k(),
                         w=g.k(fc * 512, (fc + 1) * 512))
                else:
                    b.op("DVE", "tensor_tensor", g.ap(fc * 512, (fc + 1) * 512),
                         g.ap(fc * 512, (fc + 1) * 512), b.pf(ub, 0, 512), ALU.mult,
                         r=g.k(fc * 512, (fc + 1) * 512) + b.pk(ub),
                         w=g.k(fc * 512, (fc + 1) * 512) + b.pk(ub))
            pend_w2.append((e, F_, tc, gsl, w2s))
            if tc == ntc - 1 and si + 1 < len(steps):
                load_step(si + 1)
            if len(pend_w2) >= 2:
                ee, ff, tcc, gg, ww = pend_w2.pop(0)
                if ee == 8:
                    emit_w2_shared(tcc, gg, ww)
                else:
                    emit_w2_routed(ee, ff, tcc, gg, ww)
    while pend_w2:
        ee, ff, tcc, gg, ww = pend_w2.pop(0)
        if ee == 8:
            emit_w2_shared(tcc, gg, ww)
        else:
            emit_w2_routed(ee, ff, tcc, gg, ww)
    b.freebuf("SBC", "G0", "G1", "USC", "XGa", "XGb", "XGT", "OUTC",
              "WC0", "WC1", "WC2", "WC3")


def stage_f(b, cst, bi):
    d = b.dram
    RES = b.bufs["RES"]
    ESC = b.bufs["ESC"]
    DCOL = b.bufs["DCOL"]
    C = 1024
    ZTM = b.buf("ZTMF", F32, D)
    SQA = b.buf("SQAF", BF16, D)
    COLS = b.buf("COLSF", F32, 8)
    MOET = b.buf("MOET", BF16, 2 * D)
    ogt_keys = [("OGT", e, s0) for e in range(E) for s0 in range(0, C, 128)]
    ogt_keys += [("OGTZ",)]
    for tt in range(NT):
        sl = tt % 2
        b.idma(MOET.ap(sl * D, (sl + 1) * D), None, d["OGT"],
               DCOL.ap(tt, tt + 1),
               r=ogt_keys + DCOL.k(), w=MOET.k(sl * D, (sl + 1) * D))
        for g in range(2):
            bank = 6 + g
            for c in range(4):
                ch = g * 4 + c
                b.transp(bank, RES.ap(ch * T + tt * 128, ch * T + (tt + 1) * 128),
                         cst["EYEF"].ap(), b.pf(bank, c * 128, (c + 1) * 128),
                         r=RES.k(ch * T + tt * 128, ch * T + (tt + 1) * 128)
                           + cst["EYEF"].k())
            b.op("DVE", "tensor_copy", ZTM.ap(g * 512, g * 512 + 512),
                 b.pf(bank, 0, 512), r=b.pk(bank),
                 w=ZTM.k(g * 512, g * 512 + 512) + b.pk(bank))
        b.op("DVE", "scalar_tensor_tensor", ZTM.ap(), MOET.ap(sl * D, (sl + 1) * D),
             ESC.ap(tt, tt + 1), ZTM.ap(), op0=ALU.mult, op1=ALU.add,
             r=MOET.k(sl * D, (sl + 1) * D) + ESC.k() + ZTM.k(), w=ZTM.k())
        layernorm_tm(b, cst, ZTM, SQA, COLS)
        b.dma("SOq", d["out"][bi * T + tt * 128: bi * T + (tt + 1) * 128, :],
              ZTM.ap(), r=ZTM.k(), w=[("OUT", bi, tt)])
    b.freebuf("ZTMF", "SQAF", "COLSF", "MOET", "RES", "ESC", "DCOL")


def _lay(w):
    din, dout = w.shape
    return np.ascontiguousarray(
        w.reshape(din // 128, 128, dout).transpose(1, 0, 2).reshape(128, -1)
    ).astype(np.float32)


def _layF(w):
    return np.ascontiguousarray(
        w.reshape(8, 128, 2, 1024).transpose(2, 1, 0, 3).reshape(2, 128, 8 * 1024)
    ).astype(np.float32)


def _layF2(w):
    return np.ascontiguousarray(
        w.reshape(2, 8, 128, 1024).transpose(0, 2, 1, 3).reshape(2, 128, 8 * 1024)
    ).astype(np.float32)


def _prep_inputs(inp):
    g = {}
    g["wq_a"] = _lay(inp["L0_attn_Wq"])
    g["wk_a"] = _lay(inp["L0_attn_Wkv"][:, :D])
    g["wv_a"] = _lay(inp["L0_attn_Wkv"][:, D:])
    g["wo_a"] = _lay(inp["L0_attn_Wo"])
    g["c_qg"] = np.ascontiguousarray(
        inp["L0_attn_q_gamma"].reshape(-1).reshape(8, 128).T).astype(np.float32)
    g["c_kg"] = np.ascontiguousarray(
        inp["L0_attn_k_gamma"].reshape(-1).reshape(8, 128).T).astype(np.float32)
    g["wq_f"] = _lay(inp["L0_ffn_Wq"])
    g["wkv_f"] = _lay(inp["L0_ffn_Wkv"])
    g["wk_up"] = _lay(np.ascontiguousarray(
        inp["L0_ffn_Wk_up"].transpose(1, 0, 2).reshape(L, H * DH)))
    g["wv_up"] = _lay(np.ascontiguousarray(
        inp["L0_ffn_Wv_up"].transpose(1, 0, 2).reshape(L, H * DH)))
    g["wo_f"] = _lay(inp["L0_ffn_Wo"])
    g["wr"] = _lay(inp["L0_ffn_Wr"])
    g["c_rb"] = inp["L0_ffn_r_bias"].reshape(1, E).astype(np.float32)
    g["sw1"] = _layF(inp["L0_ffn_sw1"])
    g["sw3"] = _layF(inp["L0_ffn_sw3"])
    g["sw2"] = _layF2(inp["L0_ffn_sw2"])
    g["ew1"] = np.stack([_layF(inp["L0_ffn_ew1"][e]) for e in range(E)])
    g["ew3"] = np.stack([_layF(inp["L0_ffn_ew3"][e]) for e in range(E)])
    g["ew2"] = np.stack([_layF2(inp["L0_ffn_ew2"][e]) for e in range(E)])

    g["c_eye"] = np.eye(128, dtype=np.float32)
    m = np.zeros((128, 128), dtype=np.float32)
    m[np.triu_indices(128, 1)] = NEG
    g["c_mask"] = m
    inv = 1.0 / (100000.0 ** (np.arange(0, DH, 2, dtype=np.float64) / DH))
    tpos = np.arange(T, dtype=np.float64)[:, None] * inv[None, :]
    g["c_cos"] = np.ascontiguousarray(np.cos(tpos).astype(np.float32)
                                      .reshape(NT, 128, 32).transpose(1, 0, 2)
                                      .reshape(128, NT * 32))
    g["c_sin"] = np.ascontiguousarray(np.sin(tpos).astype(np.float32)
                                      .reshape(NT, 128, 32).transpose(1, 0, 2)
                                      .reshape(128, NT * 32))
    gain = np.zeros((128, 24), dtype=np.float32)
    gain[:, 0:8] = inp["L0_attn_norm_w"].reshape(8, 128).T
    gain[:, 8:16] = inp["L0_ffn_ln1_w"].reshape(8, 128).T
    gain[:, 16:24] = inp["L0_ffn_ln2_w"].reshape(8, 128).T
    g["c_gain"] = gain
    g["c_io8"] = np.tile(np.arange(8, dtype=np.float32), (128, 1))
    ind2 = np.zeros((128, 2), dtype=np.float32)
    ind2[0:64, 0] = 1.0
    ind2[64:128, 1] = 1.0
    g["c_ind2"] = ind2
    row2 = np.zeros((2, 256), dtype=np.float32)
    row2[0, 0:64] = 1.0
    row2[1, 64:128] = 1.0
    row2[0, 128:256] = 1.0
    g["c_row2"] = row2
    exp9 = np.zeros((16, 9 * 128), dtype=np.float32)
    for e in range(9):
        exp9[e, e * 128:(e + 1) * 128] = 1.0
    g["c_exp9"] = exp9
    g["c_i128"] = (8 * 1024 + (np.arange(T) % 128)).astype(np.float32).reshape(1, T)
    r16 = np.zeros((16, 32), dtype=np.float32)
    r16[:16, :16] = np.eye(16)
    r16[0:8, 16] = 1.0
    g["c_r16"] = r16
    return g


def kernel(**inputs):
    inputs = {k: np.asarray(v, dtype=np.float32) for k, v in inputs.items()}
    g = _prep_inputs(inputs)
    toks = np.ascontiguousarray(inputs["L0_attn_tokens"].reshape(16 * T, D))
    nc = build_nc()
    in_maps = []
    for c in range(N_CORES):
        m = dict(g)
        m["x0"] = np.ascontiguousarray(
            toks[c * B_PER_CORE * T:(c + 1) * B_PER_CORE * T, :])
        in_maps.append(m)
    res = run_bass_kernel_spmd(nc, in_maps, core_ids=list(range(N_CORES)))
    out = np.concatenate([r["out"].reshape(B_PER_CORE, T, D) for r in res.results],
                         axis=0)
    return out


# version 18
# speedup vs baseline: 1.0109x; 1.0013x over previous
import numpy as np
import concourse.bass as bass
import concourse.mybir as mybir
from concourse.bass_utils import run_bass_kernel_spmd

F32 = mybir.dt.float32
BF16 = mybir.dt.bfloat16
AF = mybir.ActivationFunctionType
ALU = mybir.AluOpType
AX = mybir.AxisListType

N_CORES = 8
B_PER_CORE = 2
T = 2048
D = 1024
H = 16
DH = 64
NCH = D // 128
NT = T // 128
L = 256
E = 8
EPS_A = float(np.finfo(np.float32).eps)
EPS_F = 1e-6
EPS_LN = 1e-5
NEG = -1.0e30

STAGES = ("A", "B", "C", "F")
DBG = ()

KEYBLK = 2048


class Sched:
    def __init__(self):
        self.ops = []
        self.last_w = {}
        self.readers = {}

    def add(self, ch, emit, r=(), w=(), dma=None):
        i = len(self.ops)
        raw = set()
        other = set()
        for k in r:
            j = self.last_w.get(k)
            if j is not None:
                raw.add(j)
        for k in w:
            j = self.last_w.get(k)
            if j is not None:
                other.add(j)
            for j2 in self.readers.get(k, ()):
                other.add(j2)
        for k in r:
            self.readers.setdefault(k, []).append(i)
        for k in w:
            self.last_w[k] = i
            self.readers[k] = []
        self.ops.append({"ch": ch, "emit": emit, "dma": dma, "raw": raw,
                         "other": other})
        return i

    def finalize(self):
        ops = self.ops
        n = len(ops)
        deps = [None] * n
        for i, op in enumerate(ops):
            d = set()
            for j in op["raw"]:
                pj = ops[j]
                if (pj["ch"] == op["ch"] == "PE"
                        and pj["dma"] is None and op["dma"] is None):
                    continue
                d.add(j)
            for j in op["other"]:
                pj = ops[j]
                if (pj["ch"] == op["ch"] == "PE"
                        and pj["dma"] is None and op["dma"] is None):
                    continue
                d.add(j)
            d.discard(i)
            deps[i] = d
        need = [False] * n
        for i in range(n):
            for j in deps[i]:
                need[j] = True
        NSUB = {"Pq": 36, "Sq": 36, "POq": 1, "SOq": 1}
        cnt = {}
        inc_val = [None] * n
        sub_of = [None] * n
        dma_cum = [None] * n
        cur_dma = {}
        rr = {}
        for i, op in enumerate(ops):
            dma_cum[i] = dict(cur_dma)
            if need[i]:
                if op["dma"]:
                    q = op["dma"]
                    rr[q] = rr.get(q, -1) + 1
                    sem = f"{q}{rr[q] % NSUB[q]}"
                    sub_of[i] = sem
                    cnt[sem] = cnt.get(sem, 0) + 16
                    inc_val[i] = cnt[sem]
                    cur_dma[sem] = cnt[sem]
                else:
                    sem = op["ch"]
                    cnt[sem] = cnt.get(sem, 0) + 1
                    inc_val[i] = cnt[sem]
        per_ch = {}
        clock = {}
        for i, op in enumerate(ops):
            waits = []
            for j in sorted(deps[i]):
                pj = ops[j]
                if pj["dma"]:
                    sem = sub_of[j]
                    v = dma_cum[i].get(sem, 0)
                else:
                    sem = pj["ch"]
                    v = inc_val[j]
                key = (op["ch"], sem)
                if clock.get(key, 0) >= v:
                    continue
                clock[key] = v
                waits.append((sem, v))
            inc = None
            if need[i]:
                sem = sub_of[i] if op["dma"] else op["ch"]
                inc = (sem, 16 if op["dma"] else 1)
            per_ch.setdefault(op["ch"], []).append(
                {"emit": op["emit"], "waits": waits, "inc": inc})
        self.sem_totals = cnt
        return per_ch


class Arena:
    def __init__(self, nbytes):
        self.free_l = [(0, nbytes)]
        self.used = {}
        self.hw = 0

    def alloc(self, name, nb, hi=False):
        nb = (nb + 31) & ~31
        idxs = range(len(self.free_l))
        if hi:
            idxs = reversed(list(idxs))
        for idx in idxs:
            off, sz = self.free_l[idx]
            if sz >= nb:
                if hi:
                    self.used[name] = (off + sz - nb, nb)
                    if sz == nb:
                        self.free_l.pop(idx)
                    else:
                        self.free_l[idx] = (off, sz - nb)
                    return off + sz - nb
                self.used[name] = (off, nb)
                if sz == nb:
                    self.free_l.pop(idx)
                else:
                    self.free_l[idx] = (off + nb, sz - nb)
                return off
        raise MemoryError(f"arena: cannot alloc {name} ({nb}B); free={self.free_l}")

    def release(self, name):
        off, nb = self.used.pop(name)
        self.free_l.append((off, nb))
        self.free_l.sort()
        merged = []
        for o, s in self.free_l:
            if merged and merged[-1][0] + merged[-1][1] == o:
                merged[-1] = (merged[-1][0], merged[-1][1] + s)
            else:
                merged.append((o, s))
        self.free_l = merged


class Buf:
    def __init__(self, bld, name, dtype, cols, hi=True):
        self.bld = bld
        self.name = name
        self.dtype = dtype
        self.isz = 2 if dtype == BF16 else 4
        self.cols = cols
        self.off_b = bld.arena.alloc(name, cols * self.isz, hi=hi)

    def ap(self, c0=0, c1=None, p0=0, p1=128):
        c1 = self.cols if c1 is None else c1
        h, W = self.bld.handle(self.dtype)
        off = p0 * W + self.off_b // self.isz + c0
        return bass.AP(h, off, [[W, p1 - p0], [1, c1 - c0]])

    def ap3(self, c0, dims, p0=0, p1=128):
        h, W = self.bld.handle(self.dtype)
        off = p0 * W + self.off_b // self.isz + c0
        return bass.AP(h, off, [[W, p1 - p0]] + [[s, c] for s, c in dims])

    def k(self, c0=0, c1=None):
        c1 = self.cols if c1 is None else c1
        b0 = self.off_b + c0 * self.isz
        b1 = self.off_b + c1 * self.isz
        return [("S", x) for x in range(b0 // KEYBLK, (b1 - 1) // KEYBLK + 1)]


class CBuf:

    def __init__(self, bld, name, dtype, nch, cols_per, hi=True):
        self.name = name
        self.cols_per = cols_per
        self.chunks = [Buf(bld, f"{name}.{i}", dtype, cols_per, hi=hi)
                       for i in range(nch)]

    def ap(self, c0=0, c1=None, p0=0, p1=128):
        ch = c0 // self.cols_per
        assert c1 is not None and c1 <= (ch + 1) * self.cols_per, (c0, c1)
        base = ch * self.cols_per
        return self.chunks[ch].ap(c0 - base, c1 - base, p0, p1)

    def ap3(self, c0, dims, p0=0, p1=128):
        ch = c0 // self.cols_per
        base = ch * self.cols_per
        ext = c0 - base + sum(s_ * (n_ - 1) for s_, n_ in dims) + 1
        assert ext <= self.cols_per, (c0, dims)
        return self.chunks[ch].ap3(c0 - base, dims, p0, p1)

    def k(self, c0=0, c1=None):
        if c1 is None:
            c1 = len(self.chunks) * self.cols_per
        out = []
        ch0 = c0 // self.cols_per
        ch1 = (c1 - 1) // self.cols_per
        for ch in range(ch0, ch1 + 1):
            base = ch * self.cols_per
            a = max(c0, base) - base
            bcl = min(c1, base + self.cols_per) - base
            out += self.chunks[ch].k(a, bcl)
        return out

    def release_all(self, arena):
        for c in self.chunks:
            arena.release(c.name)


class Builder:
    def __init__(self, nc, sched, arena_t, arena_cols, psums, dram):
        self.nc = nc
        self.s = sched
        self.arena = Arena(arena_cols * 2)
        self._hb = arena_t.ap()
        self._hf = arena_t.ap().bitcast(F32)
        self._hu = arena_t.ap().bitcast(mybir.dt.uint32)
        self.psf_t = [p.ap().bitcast(F32) for p in psums]
        self.psb_t = [p.ap() for p in psums]
        self.dram = dram
        self.bufs = {}
        self.dbg_keys = []

    def handle(self, dtype):
        ap = (self._hb if dtype == BF16 else
              self._hu if dtype == mybir.dt.uint32 else self._hf)
        return ap.tensor, ap.ap[0][0]

    LOW = {"RES", "Q", "K", "V", "OFMB", "X2B", "EYEF", "EYEB", "MASK",
           "GAIN", "QG", "KG", "IO8", "IND2", "ROW2", "RB8", "ONESC", "RBROW"}

    def _is_low(self, name):
        return name.split(".")[0] in self.LOW

    def buf(self, name, dtype, cols):
        bb = Buf(self, name, dtype, cols, hi=not self._is_low(name))
        self.bufs[name] = bb
        return bb

    def cbuf(self, name, dtype, nch, cols_per):
        bb = CBuf(self, name, dtype, nch, cols_per, hi=not self._is_low(name))
        self.bufs[name] = bb
        return bb

    def freebuf(self, *names):
        for nm in names:
            bb = self.bufs.pop(nm)
            if isinstance(bb, CBuf):
                bb.release_all(self.arena)
            else:
                self.arena.release(bb.name)

    def pf(self, i, c0=0, c1=512, p0=0, p1=128):
        ap = self.psf_t[i]
        return bass.AP(ap.tensor, p0 * 512 + c0, [[512, p1 - p0], [1, c1 - c0]])

    def pb(self, i, c0=0, c1=1024, p0=0, p1=128):
        ap = self.psb_t[i]
        return bass.AP(ap.tensor, p0 * 1024 + c0, [[1024, p1 - p0], [1, c1 - c0]])

    def pk(self, i):
        return [("P", i)]

    def dma(self, q, dst_ap, src_ap, r=(), w=(), transpose=False):
        eng = "POOL" if q in ("Pq", "POq") else "SP"
        self.s.add(eng, lambda e, d_=dst_ap, s_=src_ap, t_=transpose:
                   e.dma_start(d_, s_, transpose=t_), r=r, w=w, dma=q)

    def idma(self, dst_ap, dst_off, src_ap, src_off, r=(), w=()):
        from bass_rust import IndirectOffsetOnAxis
        oo = IndirectOffsetOnAxis(ap=dst_off, axis=0) if dst_off is not None else None
        io = IndirectOffsetOnAxis(ap=src_off, axis=0) if src_off is not None else None
        self.s.add("POOL", lambda e, d_=dst_ap, s_=src_ap, o_=oo, i_=io:
                   e.indirect_dma_start(d_, o_, s_, i_), r=r, w=w, dma="Pq")

    def mm(self, bank, lhsT, rhs, start, stop, r=(), out=None):
        o = out if out is not None else self.pf(bank)
        self.s.add("PE", lambda e, o_=o, l_=lhsT, r_=rhs, st=start, sp=stop:
                   e.matmul(o_, l_, r_, start=st, stop=sp),
                   r=list(r) + self.pk(bank), w=self.pk(bank))

    def transp(self, bank, in_ap, ident_ap, out_ap, r=()):
        self.s.add("PE", lambda e, o_=out_ap, i_=in_ap, d_=ident_ap:
                   e.transpose(o_, i_, d_), r=list(r) + self.pk(bank),
                   w=self.pk(bank))

    def op(self, ch, fn, *args, r=(), w=(), **kw):
        self.s.add(ch, lambda e, a=args, k=kw: getattr(e, fn)(*a, **k), r=r, w=w)


def build_nc():
    nc = bass.Bass("TRN2", target_bir_lowering=False)
    dram = {}

    def din(name, shape):
        dram[name] = nc.dram_tensor(name, list(shape), F32, kind="ExternalInput").ap()

    din("x0", (B_PER_CORE * T, D))
    din("wq_a", (128, 8 * D)); din("wk_a", (128, 8 * D)); din("wv_a", (128, 8 * D))
    din("wo_a", (128, 8 * D))
    din("c_qg", (128, 8)); din("c_kg", (128, 8))
    din("wq_f", (128, 8 * D)); din("wkv_f", (128, 8 * L))
    din("wk_up", (128, 2 * D)); din("wv_up", (128, 2 * D))
    din("wo_f", (128, 8 * D)); din("wr", (128, 8 * E)); din("c_rb", (1, E))
    din("sw1", (2, 128, 8 * 1024)); din("sw3", (2, 128, 8 * 1024))
    din("sw2", (2, 128, 8 * 1024))
    din("ew1", (E, 2, 128, 8 * 1024)); din("ew3", (E, 2, 128, 8 * 1024))
    din("ew2", (E, 2, 128, 8 * 1024))
    din("c_eye", (128, 128)); din("c_mask", (128, 128))
    din("c_cos", (128, 512)); din("c_sin", (128, 512))
    din("c_gain", (128, 24)); din("c_io8", (128, 8))
    din("c_ind2", (128, 2)); din("c_row2", (2, 256)); din("c_exp9", (16, 9 * 128))
    din("c_i128", (1, T)); din("c_r16", (16, 32))
    dram["out"] = nc.dram_tensor("out", [B_PER_CORE * T, D], F32,
                                 kind="ExternalOutput").ap()
    dram["RG"] = nc.dram_tensor("RG", [8 * 1024 + 128, D], BF16).ap()
    dram["OGT"] = nc.dram_tensor("OGT", [8 * 1024 + 128, D], BF16).ap()
    for nm in DBG:
        dram["dbg_" + nm] = nc.dram_tensor(
            "dbg_" + nm, [B_PER_CORE * NT * 128, T], F32, kind="ExternalOutput").ap()

    ARENA_COLS = 106352
    with (
        nc.sbuf_tensor("arena", [128, ARENA_COLS], BF16) as arena_t,
        nc.psum_tensor("ps0", [128, 1024], BF16) as ps0,
        nc.psum_tensor("ps1", [128, 1024], BF16) as ps1,
        nc.psum_tensor("ps2", [128, 1024], BF16) as ps2,
        nc.psum_tensor("ps3", [128, 1024], BF16) as ps3,
        nc.psum_tensor("ps4", [128, 1024], BF16) as ps4,
        nc.psum_tensor("ps5", [128, 1024], BF16) as ps5,
        nc.psum_tensor("ps6", [128, 1024], BF16) as ps6,
        nc.psum_tensor("ps7", [128, 1024], BF16) as ps7,
        nc.semaphore("sPE") as sPE,
        nc.semaphore("sDVE") as sDVE,
        nc.semaphore("sACT") as sACT,
        nc.semaphore("sPOOL") as sPOOL,
        nc.semaphore("sSP") as sSP,
        nc.Block() as block,
    ):
        sched = Sched()
        bld = Builder(nc, sched, arena_t, ARENA_COLS,
                      [ps0, ps1, ps2, ps3, ps4, ps5, ps6, ps7], dram)
        build_program(bld)
        per_ch = sched.finalize()
        sems = {"PE": sPE, "DVE": sDVE, "ACT": sACT, "POOL": sPOOL, "SP": sSP}
        import contextlib
        _stack = contextlib.ExitStack()
        for q, nsub in [("Pq", 36), ("Sq", 36), ("POq", 1), ("SOq", 1)]:
            for si in range(nsub):
                sems[f"{q}{si}"] = _stack.enter_context(
                    nc.semaphore(f"s{q}{si}"))
        for sem, tot in sched.sem_totals.items():
            assert tot < 60000, f"sem {sem} overflow risk: {tot}"

        def emit_ch(section, ch):
            items = per_ch.get(ch, [])

            @section
            def _(eng):
                for it in items:
                    for (sm, v) in it["waits"]:
                        eng.wait_ge(sems[sm], v)
                    inst = it["emit"](eng)
                    if it["inc"]:
                        inst.then_inc(sems[it["inc"][0]], it["inc"][1])

        emit_ch(block.tensor, "PE")
        emit_ch(block.vector, "DVE")
        emit_ch(block.scalar, "ACT")
        emit_ch(block.gpsimd, "POOL")
        emit_ch(block.sync, "SP")
    return nc


def build_program(b):
    d = b.dram
    EYEF = b.buf("EYEF", F32, 128)
    EYEB = b.buf("EYEB", BF16, 128)
    MASK = b.buf("MASK", F32, 128)
    GAIN = b.buf("GAIN", F32, 24)
    QG = b.buf("QG", F32, 8)
    KG = b.buf("KG", F32, 8)
    IO8 = b.buf("IO8", F32, 8)
    IND2 = b.buf("IND2", F32, 2)
    ROW2 = b.buf("ROW2", F32, 256)
    RB8 = b.buf("RB8", F32, 8)
    ONESC = b.buf("ONESC", F32, 16)
    RBROW = b.buf("RBROW", F32, 8)
    EPSC = b.buf("EPSC", F32, 4)

    b.dma("Sq", EYEF.ap(), d["c_eye"], w=EYEF.k())
    b.dma("Sq", MASK.ap(), d["c_mask"], w=MASK.k())
    b.dma("Sq", GAIN.ap(), d["c_gain"], w=GAIN.k())
    b.dma("Sq", QG.ap(), d["c_qg"], w=QG.k())
    b.dma("Sq", KG.ap(), d["c_kg"], w=KG.k())
    b.dma("Sq", IO8.ap(), d["c_io8"], w=IO8.k())
    b.dma("Sq", IND2.ap(), d["c_ind2"], w=IND2.k())
    b.dma("Sq", ROW2.ap(0, 256, 0, 2), d["c_row2"], w=ROW2.k())
    b.dma("Sq", RBROW.ap(0, 8, 0, 1), d["c_rb"], w=RBROW.k())
    b.op("DVE", "tensor_copy", EYEB.ap(), EYEF.ap(), r=EYEF.k(), w=EYEB.k())
    b.op("DVE", "memset", ONESC.ap(0, 1), 1.0, w=ONESC.k())
    b.op("DVE", "memset", EPSC.ap(0, 1), EPS_A, w=EPSC.k())
    b.op("DVE", "memset", EPSC.ap(1, 2), EPS_F, w=EPSC.k())
    b.op("DVE", "memset", EPSC.ap(2, 3), EPS_LN, w=EPSC.k())
    b.op("DVE", "tensor_scalar", QG.ap(), QG.ap(), 1.0, 8.0, ALU.add, ALU.mult,
         r=QG.k(), w=QG.k())
    b.op("DVE", "tensor_scalar", KG.ap(), KG.ap(), 1.0, 8.0, ALU.add, ALU.mult,
         r=KG.k(), w=KG.k())
    b.mm(7, ROW2.ap(128, 256, 0, 1), RBROW.ap(0, 8, 0, 1), True, True,
         r=ROW2.k() + RBROW.k(), out=b.pf(7, 0, 8))
    b.op("DVE", "tensor_copy", RB8.ap(), b.pf(7, 0, 8), r=RB8.k(), w=RB8.k() + b.pk(7))

    cst = dict(EYEF=EYEF, EYEB=EYEB, MASK=MASK, GAIN=GAIN,
               QG=QG, KG=KG, IO8=IO8, IND2=IND2, ROW2=ROW2, RB8=RB8,
               ONESC=ONESC, EPSC=EPSC)

    for bi in range(B_PER_CORE):
        if "A" in STAGES:
            stage_a(b, cst, bi)
        if "B" in STAGES:
            stage_b(b, cst, bi)
        if "C" in STAGES:
            stage_c(b, cst, bi)
        if "F" in STAGES:
            stage_f(b, cst, bi)
    outkeys = [("OUT", bb, tt) for bb in range(B_PER_CORE) for tt in range(NT)]
    outkeys += b.dbg_keys
    b.s.add("SP", lambda e: e.nop(), r=outkeys)


def rinv(b, cst, ap, src_ap, wk, rk):
    b.op("ACT", "activation", ap, src_ap, AF.Ln, r=list(rk) + list(wk), w=wk)
    b.op("ACT", "activation", ap, ap, AF.Exp, scale=-1.0, r=wk, w=wk)


def rsqrt_rows(b, cst, dst_ap, src_ap, epsi, scale, wk, rk):
    b.op("ACT", "activation", dst_ap, src_ap, AF.Ln,
         bias=cst["EPSC"].ap(epsi, epsi + 1, 0, 1), scale=scale,
         r=list(rk) + cst["EPSC"].k() + list(wk), w=wk)
    b.op("ACT", "activation", dst_ap, dst_ap, AF.Exp, scale=-0.5, r=wk, w=wk)

def rmsnorm_rows(b, cst, SRC, SQ, ROWR, bank, epsi):
    for tc in range(4):
        c0 = tc * 512
        for ch in range(NCH):
            b.op("ACT", "activation", SQ.ap(0, 512),
                 SRC.ap(ch * T + c0, ch * T + c0 + 512), AF.Square,
                 r=SRC.k(ch * T + c0, ch * T + c0 + 512), w=SQ.k())
            b.mm(bank, cst["ONESC"].ap(0, 1), SQ.ap(0, 512), ch == 0, ch == NCH - 1,
                 r=cst["ONESC"].k() + SQ.k(), out=b.pf(bank, 0, 512, 0, 1))
        b.op("ACT", "activation", ROWR.ap(c0, c0 + 512, 0, 1),
             b.pf(bank, 0, 512, 0, 1), AF.Ln,
             bias=cst["EPSC"].ap(epsi, epsi + 1, 0, 1), scale=1.0 / D,
             r=cst["EPSC"].k() + b.pk(bank),
             w=ROWR.k(c0, c0 + 512) + b.pk(bank))
        b.op("ACT", "activation", ROWR.ap(c0, c0 + 512, 0, 1),
             ROWR.ap(c0, c0 + 512, 0, 1), AF.Exp, scale=-0.5,
             r=ROWR.k(c0, c0 + 512), w=ROWR.k(c0, c0 + 512))


def bcast_row(b, cst, ROWR, DSTB, bank):
    for tc in range(4):
        c0 = tc * 512
        b.mm(bank, cst["ROW2"].ap(128, 256, 0, 1), ROWR.ap(c0, c0 + 512, 0, 1),
             True, True, r=cst["ROW2"].k() + ROWR.k(c0, c0 + 512))
        b.op("DVE", "tensor_copy", DSTB.ap(c0, c0 + 512), b.pf(bank, 0, 512),
             r=b.pk(bank), w=DSTB.k(c0, c0 + 512) + b.pk(bank))


def layernorm_tm(b, cst, ZTM, SQA, COLS):
    b.op("DVE", "tensor_reduce", COLS.ap(0, 1), ZTM.ap(), axis=AX.X, op=ALU.add,
         r=ZTM.k(), w=COLS.k(0, 1))
    b.op("DVE", "tensor_scalar", COLS.ap(0, 1), COLS.ap(0, 1), 1.0 / D, None,
         ALU.mult, r=COLS.k(0, 1), w=COLS.k(0, 1))
    b.op("ACT", "activation", SQA.ap(), ZTM.ap(), AF.Square,
         r=ZTM.k(), w=SQA.k())
    b.op("DVE", "tensor_reduce", COLS.ap(1, 2), SQA.ap(), axis=AX.X, op=ALU.add,
         r=SQA.k(), w=COLS.k(1, 2))
    b.op("DVE", "tensor_scalar", COLS.ap(1, 2), COLS.ap(1, 2), 1.0 / D, None,
         ALU.mult, r=COLS.k(1, 2), w=COLS.k(1, 2))
    b.op("DVE", "tensor_tensor", COLS.ap(2, 3), COLS.ap(0, 1), COLS.ap(0, 1),
         ALU.mult, r=COLS.k(0, 1), w=COLS.k(2, 3))
    b.op("DVE", "tensor_tensor", COLS.ap(2, 3), COLS.ap(1, 2), COLS.ap(2, 3),
         ALU.subtract, r=COLS.k(1, 2) + COLS.k(2, 3), w=COLS.k(2, 3))
    b.op("ACT", "activation", COLS.ap(2, 3), COLS.ap(2, 3), AF.Ln,
         bias=cst["EPSC"].ap(2, 3), r=COLS.k(2, 3) + cst["EPSC"].k(),
         w=COLS.k(2, 3))
    b.op("ACT", "activation", COLS.ap(2, 3), COLS.ap(2, 3), AF.Exp, scale=-0.5,
         r=COLS.k(2, 3), w=COLS.k(2, 3))
    b.op("DVE", "tensor_scalar", ZTM.ap(), ZTM.ap(), COLS.ap(0, 1), COLS.ap(2, 3),
         ALU.subtract, ALU.mult,
         r=ZTM.k() + COLS.k(0, 1) + COLS.k(2, 3), w=ZTM.k())


def fm_transpose_in(b, cst, SRCTM, sc0, DSTFM, tt):
    for g in range(2):
        bank = 6 + g
        for c in range(4):
            ch = g * 4 + c
            b.transp(bank, SRCTM.ap(sc0 + ch * 128, sc0 + (ch + 1) * 128),
                     cst["EYEF"].ap(),
                     b.pf(bank, c * 128, (c + 1) * 128),
                     r=SRCTM.k(sc0, sc0 + D) + cst["EYEF"].k())
        for c in range(4):
            ch = g * 4 + c
            c0 = ch * T + tt * 128
            b.op("DVE", "tensor_copy", DSTFM.ap(c0, c0 + 128),
                 b.pf(bank, c * 128, (c + 1) * 128), r=b.pk(bank),
                 w=DSTFM.k(c0, c0 + 128) + b.pk(bank))


def dump_bf(b, nm, bi, SRC, ncols=T, nch=NCH):
    if nm not in DBG:
        return
    dst = b.dram["dbg_" + nm]
    r0 = bi * NT * 128
    for ch in range(nch):
        key = ("DBG", nm, bi, ch)
        b.dbg_keys.append(key)
        step = min(ncols, 512)
        for c0 in range(0, ncols, step):
            b.dma("POq", dst[r0 + ch * 128: r0 + (ch + 1) * 128, c0:c0 + step],
                  SRC.ap(ch * ncols + c0, ch * ncols + c0 + step),
                  r=SRC.k(ch * ncols + c0, ch * ncols + c0 + step), w=[key])


def dump_fm(b, nm, bi, SRC):
    if nm not in DBG:
        return
    dst = b.dram["dbg_" + nm]
    r0 = bi * NT * 128
    for ch in range(NCH):
        key = ("DBG", nm, bi, ch)
        b.dbg_keys.append(key)
        b.dma("SOq", dst[r0 + ch * 128: r0 + (ch + 1) * 128, :],
              SRC.ap(ch * T, (ch + 1) * T), r=SRC.k(ch * T, (ch + 1) * T),
              w=[key])


def pe_fill(b, cst, bank, n, gate=(), c0=128):
    for i in range(n):
        b.mm(bank, cst["EYEF"].ap(), cst["EYEF"].ap(), True, True,
             r=(list(gate) if i == 0 else []) + cst["EYEF"].k(),
             out=b.pf(bank, c0, c0 + 128))


def stage_a(b, cst, bi):
    d = b.dram
    t_off = bi * T

    XTM = b.buf("XTM", F32, D)
    SQA = b.buf("SQA", BF16, D)
    RCOL = b.buf("RCOL", F32, 16)
    RROW1 = b.buf("RROW1", F32, 128)
    RSTDT = b.buf("RSTDT", F32, 128)
    RN = b.cbuf("RN", BF16, NCH, T)
    for tt in range(NT):
        b.dma("Sq", XTM.ap(), d["x0"][t_off + tt * 128: t_off + (tt + 1) * 128, :],
              w=XTM.k())
        b.op("ACT", "activation", SQA.ap(), XTM.ap(), AF.Square,
             r=XTM.k(), w=SQA.k())
        b.op("DVE", "tensor_reduce", RCOL.ap(0, 1), SQA.ap(), axis=AX.X,
             op=ALU.add, r=SQA.k(), w=RCOL.k())
        b.op("ACT", "activation", RCOL.ap(0, 1), RCOL.ap(0, 1), AF.Ln,
             bias=cst["EPSC"].ap(0, 1), scale=1.0 / D,
             r=RCOL.k() + cst["EPSC"].k(), w=RCOL.k())
        b.op("ACT", "activation", RCOL.ap(0, 1), RCOL.ap(0, 1), AF.Exp,
             scale=-0.5, r=RCOL.k(), w=RCOL.k())
        b.transp(2, RCOL.ap(0, 1), cst["EYEF"].ap(), b.pf(2, 0, 128, 0, 1),
                 r=RCOL.k() + cst["EYEF"].k())
        b.op("DVE", "tensor_copy", RROW1.ap(0, 128, 0, 1), b.pf(2, 0, 128, 0, 1),
             r=b.pk(2), w=RROW1.k() + b.pk(2))
        b.mm(3, cst["ROW2"].ap(128, 256, 0, 1), RROW1.ap(0, 128, 0, 1), True, True,
             r=cst["ROW2"].k() + RROW1.k(), out=b.pf(3, 0, 128))
        b.op("DVE", "tensor_copy", RSTDT.ap(), b.pf(3, 0, 128),
             r=b.pk(3), w=RSTDT.k() + b.pk(3))
        for g in range(2):
            bank = 6 + g
            for c in range(4):
                ch = g * 4 + c
                b.transp(bank, XTM.ap(ch * 128, (ch + 1) * 128), cst["EYEF"].ap(),
                         b.pf(bank, c * 128, (c + 1) * 128),
                         r=XTM.k() + cst["EYEF"].k())
            for c in range(4):
                ch = g * 4 + c
                c0 = ch * T + tt * 128
                b.op("DVE", "scalar_tensor_tensor", RN.ap(c0, c0 + 128),
                     b.pf(bank, c * 128, (c + 1) * 128),
                     cst["GAIN"].ap(ch, ch + 1), RSTDT.ap(),
                     op0=ALU.mult, op1=ALU.mult,
                     r=cst["GAIN"].k() + RSTDT.k() + b.pk(bank),
                     w=RN.k(c0, c0 + 128) + b.pk(bank))
    b.freebuf("SQA", "RCOL", "RROW1", "RSTDT", "XTM")
    dump_bf(b, "rn", bi, RN)

    WS0 = b.buf("WSA0", BF16, 8 * D)
    WS1 = b.buf("WSA1", BF16, 8 * D)
    Q = b.cbuf("Q", BF16, NCH, T)
    K = b.cbuf("K", BF16, NCH, T)
    V = b.cbuf("V", BF16, NT, H * 65)
    SQH = b.buf("SQH", F32, 512)
    RNB = b.buf("RNB", F32, 512)
    HROW = b.buf("HROW", F32, 512)
    b.dma("Pq", WS0.ap(), d["wq_a"], w=WS0.k())
    b.dma("Pq", WS1.ap(), d["wk_a"], w=WS1.k())

    def headnorm(DST, GCOL, fc, tc):
        c0 = fc * T + tc * 512
        dk = DST.k(c0, c0 + 512)
        b.op("ACT", "activation", SQH.ap(0, 512), DST.ap(c0, c0 + 512), AF.Square,
             r=dk, w=SQH.k())
        b.mm(2, cst["IND2"].ap(0, 2), SQH.ap(0, 512), True, True,
             r=cst["IND2"].k() + SQH.k(), out=b.pf(2, 0, 512, 0, 2))
        b.op("ACT", "activation", HROW.ap(0, 512, 0, 2), b.pf(2, 0, 512, 0, 2),
             AF.Ln, r=b.pk(2), w=HROW.k() + b.pk(2))
        b.op("ACT", "activation", HROW.ap(0, 512, 0, 2), HROW.ap(0, 512, 0, 2),
             AF.Exp, scale=-0.5, r=HROW.k(), w=HROW.k())
        b.op("DVE", "tensor_scalar", HROW.ap(0, 512, 0, 2), HROW.ap(0, 512, 0, 2),
             1e12, None, ALU.min, r=HROW.k(), w=HROW.k())
        b.mm(3, cst["ROW2"].ap(0, 128, 0, 2), HROW.ap(0, 512, 0, 2), True, True,
             r=cst["ROW2"].k() + HROW.k())
        b.op("DVE", "tensor_copy", RNB.ap(0, 512), b.pf(3, 0, 512), r=b.pk(3),
             w=RNB.k() + b.pk(3))
        b.op("DVE", "scalar_tensor_tensor", DST.ap(c0, c0 + 512),
             DST.ap(c0, c0 + 512), GCOL.ap(fc, fc + 1), RNB.ap(0, 512),
             op0=ALU.mult, op1=ALU.mult, r=dk + GCOL.k() + RNB.k(), w=dk)

    for which, (DST, GCOL, ws) in enumerate(
            [(Q, cst["QG"], WS0), (K, cst["KG"], WS1)]):
        wk = ws.k()
        for fc in range(NCH):
            for tc in range(4):
                bank = tc % 2
                for dc in range(NCH):
                    b.mm(bank, ws.ap(dc * D + fc * 128, dc * D + (fc + 1) * 128),
                         RN.ap(dc * T + tc * 512, dc * T + tc * 512 + 512),
                         dc == 0, dc == NCH - 1,
                         r=wk + RN.k(dc * T + tc * 512, dc * T + tc * 512 + 512))
                c0 = fc * T + tc * 512
                b.op("DVE", "tensor_copy", DST.ap(c0, c0 + 512), b.pf(bank, 0, 512),
                     r=b.pk(bank), w=DST.k(c0, c0 + 512) + b.pk(bank))
                headnorm(DST, GCOL, fc, tc)
        if which == 0:
            b.dma("Pq", WS0.ap(), d["wv_a"], w=WS0.k())

    for tt in range(NT):
        b.op("DVE", "memset", V.ap3(tt * H * 65 + 64, [(65, 16), (1, 1)]), 1.0,
             w=V.k(tt * H * 65, (tt + 1) * H * 65))
        for fcol in range(2):
            bank = fcol % 2
            for dc in range(NCH):
                b.mm(bank, RN.ap(dc * T + tt * 128, dc * T + (tt + 1) * 128),
                     WS0.ap(dc * D + fcol * 512, dc * D + fcol * 512 + 512),
                     dc == 0, dc == NCH - 1,
                     r=WS0.k() + RN.k(dc * T + tt * 128, dc * T + (tt + 1) * 128))
            dst = V.ap3(tt * H * 65 + fcol * 8 * 65, [(65, 8), (1, 64)])
            b.op("DVE", "tensor_copy", dst, b.pf(bank, 0, 512), r=b.pk(bank),
                 w=V.k(tt * H * 65, (tt + 1) * H * 65) + b.pk(bank))
    dump_bf(b, "qh", bi, Q)
    dump_bf(b, "kh", bi, K)
    dump_bf(b, "vv", bi, V, ncols=H * 65, nch=NT)
    b.freebuf("SQH", "RNB", "HROW", "RN", "WSA0")
    b.dma("Pq", WS1.ap(), d["wo_a"], w=WS1.k())

    RES = b.cbuf("RES", F32, NCH, T)
    ICW = 256
    PT = b.buf("PT", BF16, NT * ICW)
    OFM = b.buf("OFM", BF16, NCH * ICW)
    OSC = b.buf("OSC", F32, 256)
    RROW = b.buf("RROW", F32, 256)
    ZTM = b.buf("ZTM", F32, D)
    SQA2 = b.buf("SQA2", BF16, D)
    COLS = b.buf("COLS", F32, 8)
    XT2 = b.buf("XT2", F32, D)
    wo_k = WS1.k()

    TSA = b.cbuf("TSA", BF16, 4, 512)
    for ic in range(T // ICW):
        i0 = ic * ICW

        def norm_part(h):
            qc, prow = h // 2, (h % 2) * 64
            obank = 4 + (h % 2)
            b.op("DVE", "reciprocal", RROW.ap(0, ICW, 0, 1),
                 b.pf(obank, 0, ICW, 64, 65), r=b.pk(obank),
                 w=RROW.k() + b.pk(obank))
            b.mm(6, cst["ROW2"].ap(128, 192, 0, 1), RROW.ap(0, ICW, 0, 1),
                 True, True, r=cst["ROW2"].k() + RROW.k(),
                 out=b.pf(6, 0, ICW, 0, 64))
            b.op("DVE", "tensor_copy", OSC.ap(0, ICW, 0, 64),
                 b.pf(6, 0, ICW, 0, 64), r=b.pk(6), w=OSC.k() + b.pk(6))
            b.op("DVE", "tensor_tensor",
                 OFM.ap(qc * ICW, (qc + 1) * ICW, prow, prow + 64),
                 b.pf(obank, 0, ICW, 0, 64), OSC.ap(0, ICW, 0, 64), ALU.mult,
                 r=OSC.k() + b.pk(obank),
                 w=OFM.k(qc * ICW, (qc + 1) * ICW) + b.pk(obank))

        def av_wave(h, jts):
            obank = 4 + (h % 2)
            for jt in jts:
                b.mm(obank, V.ap(jt * H * 65 + h * 65, jt * H * 65 + h * 65 + 65),
                     PT.ap(jt * ICW, (jt + 1) * ICW), jt == 0, jt == NT - 1,
                     r=V.k(jt * H * 65 + h * 65, jt * H * 65 + (h + 1) * 65)
                       + PT.k(jt * ICW, (jt + 1) * ICW),
                     out=b.pf(obank, 0, ICW, 0, 65))

        def s_wave(h, prs):
            qc, prow = h // 2, (h % 2) * 64
            for p in prs:
                for half in range(2):
                    jt = 2 * p + half
                    sb = p % 4
                    b.mm(sb, K.ap(qc * T + jt * 128, qc * T + (jt + 1) * 128,
                                  prow, prow + 64),
                         Q.ap(qc * T + i0, qc * T + i0 + ICW, prow, prow + 64),
                         True, True,
                         r=K.k(qc * T + jt * 128, qc * T + (jt + 1) * 128)
                           + Q.k(qc * T + i0, qc * T + i0 + ICW),
                         out=b.pf(sb, half * ICW, (half + 1) * ICW))

        def act_wave(prs):
            for p in prs:
                sb = p % 4
                b.op("ACT", "activation", TSA.ap((p % 4) * 512, (p % 4 + 1) * 512),
                     b.pf(sb, 0, 2 * ICW), AF.Tanh, scale=1.0 / 50.0,
                     r=b.pk(sb), w=TSA.k((p % 4) * 512, (p % 4 + 1) * 512) + b.pk(sb))
            for p in prs:
                b.op("ACT", "activation", PT.ap(2 * p * ICW, 2 * (p + 1) * ICW),
                     TSA.ap((p % 4) * 512, (p % 4 + 1) * 512), AF.Exp, scale=6.25,
                     r=TSA.k((p % 4) * 512, (p % 4 + 1) * 512),
                     w=PT.k(2 * p * ICW, 2 * (p + 1) * ICW))

        for h in range(H):
            s_wave(h, range(0, 4))
            pe_fill(b, cst, 7, 3)
            if h > 0:
                av_wave(h - 1, range(0, 8))
            act_wave(range(0, 4))
            s_wave(h, range(4, 8))
            pe_fill(b, cst, 7, 3)
            if h > 0:
                av_wave(h - 1, range(8, 16))
            act_wave(range(4, 8))
            if h > 0:
                norm_part(h - 1)
        av_wave(H - 1, range(0, 16))
        norm_part(H - 1)
        for t2 in range(2):
            tt = ic * 2 + t2
            b.dma("Sq", XT2.ap(),
                  d["x0"][t_off + tt * 128: t_off + (tt + 1) * 128, :],
                  w=XT2.k())
            for dcol in range(2):
                bank = 2 + dcol
                for c in range(NCH):
                    b.mm(bank, OFM.ap(c * ICW + t2 * 128, c * ICW + t2 * 128 + 128),
                         WS1.ap(c * D + dcol * 512, c * D + dcol * 512 + 512),
                         c == 0, c == NCH - 1,
                         r=OFM.k(c * ICW, (c + 1) * ICW) + wo_k)
                b.op("DVE", "tensor_tensor", ZTM.ap(dcol * 512, dcol * 512 + 512),
                     b.pf(bank, 0, 512),
                     XT2.ap(dcol * 512, dcol * 512 + 512), ALU.add,
                     r=XT2.k() + b.pk(bank),
                     w=ZTM.k(dcol * 512, dcol * 512 + 512) + b.pk(bank))
            if "z0" in DBG:
                key = ("DBG", "z0", bi, tt)
                b.dbg_keys.append(key)
                b.dma("SOq", b.dram["dbg_z0"][bi * NT * 128 + tt * 128:
                                             bi * NT * 128 + (tt + 1) * 128, 0:D],
                      ZTM.ap(), r=ZTM.k(), w=[key])
            layernorm_tm(b, cst, ZTM, SQA2, COLS)
            fm_transpose_in(b, cst, ZTM, 0, RES, tt)
    b.freebuf("PT", "OFM", "OSC", "RROW", "ZTM", "SQA2", "COLS", "XT2", "TSA",
              "Q", "K", "V", "WSA1")
    dump_fm(b, "x1", bi, RES)


def stage_b(b, cst, bi):
    d = b.dram
    RES = b.bufs["RES"]

    COS = b.buf("COS", F32, 512)
    SIN = b.buf("SIN", F32, 512)
    b.dma("Sq", COS.ap(), d["c_cos"], w=COS.k())
    b.dma("Sq", SIN.ap(), d["c_sin"], w=SIN.k())
    cst = dict(cst, COS=COS, SIN=SIN)
    SQ = b.buf("SQB0", F32, 512)
    ROWR = b.buf("ROWRB", F32, T)
    RSTDB = b.buf("RSTDBB", F32, T)
    RN1 = b.cbuf("RN1", BF16, NCH, T)
    rmsnorm_rows(b, cst, RES, SQ, ROWR, 2, 1)
    bcast_row(b, cst, ROWR, RSTDB, 3)
    for ch in range(NCH):
        for tc in range(4):
            c0 = ch * T + tc * 512
            b.op("DVE", "scalar_tensor_tensor",
                 RN1.ap(c0, c0 + 512), RES.ap(c0, c0 + 512),
                 cst["GAIN"].ap(8 + ch, 9 + ch), RSTDB.ap(tc * 512, tc * 512 + 512),
                 op0=ALU.mult, op1=ALU.mult,
                 r=RES.k(c0, c0 + 512) + cst["GAIN"].k()
                   + RSTDB.k(tc * 512, tc * 512 + 512),
                 w=RN1.k(c0, c0 + 512))
    b.freebuf("SQB0", "ROWRB", "RSTDBB")

    WKV = b.buf("WKV", BF16, 8 * L)
    KV = b.buf("KV", BF16, 2 * T)
    b.dma("Pq", WKV.ap(), d["wkv_f"], w=WKV.k())
    for lc in range(2):
        for tc in range(4):
            bank = tc % 2
            for dc in range(NCH):
                b.mm(bank, WKV.ap(dc * L + lc * 128, dc * L + (lc + 1) * 128),
                     RN1.ap(dc * T + tc * 512, dc * T + tc * 512 + 512),
                     dc == 0, dc == NCH - 1,
                     r=WKV.k() + RN1.k(dc * T + tc * 512, dc * T + tc * 512 + 512))
            c0 = lc * T + tc * 512
            b.op("DVE", "tensor_scalar", KV.ap(c0, c0 + 512), b.pf(bank, 0, 512),
                 0.25, None, ALU.mult, r=b.pk(bank), w=KV.k(c0, c0 + 512) + b.pk(bank))
    b.freebuf("WKV")

    WQF = b.buf("WQF", BF16, 8 * D)
    WKU = b.buf("WKU", BF16, 2 * D)
    WVU = b.buf("WVU", BF16, 2 * D)
    Q = b.cbuf("Q", BF16, NCH, T)
    QST = b.buf("QST", BF16, 2 * D)
    TMP = b.buf("TMPR", F32, 4 * 512)
    b.dma("Pq", WQF.ap(), d["wq_f"], w=WQF.k())
    b.dma("Pq", WKU.ap(), d["wk_up"], w=WKU.k())
    b.dma("Pq", WVU.ap(), d["wv_up"], w=WVU.k())

    def rope_tile(tt, sl):
        base = sl * D
        kst = QST.k(base, base + D)
        x1v = QST.ap3(base, [(64, 16), (1, 32)])
        x2v = QST.ap3(base + 32, [(64, 16), (1, 32)])
        cosb = cst["COS"].ap3(tt * 32, [(0, 16), (1, 32)])
        sinb = cst["SIN"].ap3(tt * 32, [(0, 16), (1, 32)])
        tvs = [TMP.ap3(i * 512, [(32, 16), (1, 32)]) for i in range(4)]
        tks = [TMP.k(i * 512, (i + 1) * 512) for i in range(4)]
        b.op("DVE", "tensor_tensor", tvs[0], x1v, cosb, ALU.mult,
             r=kst + cst["COS"].k(), w=tks[0])
        b.op("DVE", "tensor_tensor", tvs[1], x2v, sinb, ALU.mult,
             r=kst + cst["SIN"].k(), w=tks[1])
        b.op("DVE", "tensor_tensor", tvs[2], x1v, sinb, ALU.mult,
             r=kst + cst["SIN"].k(), w=tks[2])
        b.op("DVE", "tensor_tensor", tvs[3], x2v, cosb, ALU.mult,
             r=kst + cst["COS"].k(), w=tks[3])
        b.op("DVE", "tensor_tensor", x1v, tvs[0], tvs[1], ALU.subtract,
             r=tks[0] + tks[1], w=kst)
        b.op("DVE", "tensor_tensor", x2v, tvs[2], tvs[3], ALU.add,
             r=tks[2] + tks[3], w=kst)

    def fm_transpose_bf16(SRC, sc0, DSTFM, tt):
        for g in range(2):
            bank = 6 + g
            for c in range(4):
                ch = g * 4 + c
                b.transp(bank, SRC.ap(sc0 + ch * 128, sc0 + (ch + 1) * 128),
                         cst["EYEB"].ap(), b.pb(bank, c * 128, (c + 1) * 128),
                         r=SRC.k(sc0, sc0 + D) + cst["EYEB"].k())
            for c in range(4):
                ch = g * 4 + c
                c0 = ch * T + tt * 128
                b.op("DVE", "tensor_copy", DSTFM.ap(c0, c0 + 128),
                     b.pb(bank, c * 128, (c + 1) * 128), r=b.pk(bank),
                     w=DSTFM.k(c0, c0 + 128) + b.pk(bank))

    for tt in range(NT):
        sl = tt % 2
        for fcol in range(2):
            bank = fcol
            for dc in range(NCH):
                b.mm(bank, RN1.ap(dc * T + tt * 128, dc * T + (tt + 1) * 128),
                     WQF.ap(dc * D + fcol * 512, dc * D + fcol * 512 + 512),
                     dc == 0, dc == NCH - 1,
                     r=WQF.k() + RN1.k(dc * T + tt * 128, dc * T + (tt + 1) * 128))
            b.op("DVE", "tensor_copy",
                 QST.ap(sl * D + fcol * 512, sl * D + fcol * 512 + 512),
                 b.pf(bank, 0, 512), r=b.pk(bank),
                 w=QST.k(sl * D + fcol * 512, sl * D + fcol * 512 + 512) + b.pk(bank))
        rope_tile(tt, sl)
        fm_transpose_bf16(QST, sl * D, Q, tt)
    b.freebuf("RN1", "WQF")
    K = b.cbuf("K", BF16, NCH, T)
    for tt in range(NT):
        sl = tt % 2
        for fcol in range(2):
            bank = fcol
            for lc in range(2):
                b.mm(bank, KV.ap(lc * T + tt * 128, lc * T + (tt + 1) * 128),
                     WKU.ap(lc * D + fcol * 512, lc * D + fcol * 512 + 512),
                     lc == 0, lc == 1,
                     r=WKU.k() + KV.k(lc * T + tt * 128, lc * T + (tt + 1) * 128))
            b.op("DVE", "tensor_copy",
                 QST.ap(sl * D + fcol * 512, sl * D + fcol * 512 + 512),
                 b.pf(bank, 0, 512), r=b.pk(bank),
                 w=QST.k(sl * D + fcol * 512, sl * D + fcol * 512 + 512) + b.pk(bank))
        rope_tile(tt, sl)
        fm_transpose_bf16(QST, sl * D, K, tt)
    b.freebuf("WKU")
    V = b.cbuf("V", BF16, NT, H * 65)
    for tt in range(NT):
        b.op("DVE", "memset", V.ap3(tt * H * 65 + 64, [(65, 16), (1, 1)]), 1.0,
             w=V.k(tt * H * 65, (tt + 1) * H * 65))
        for fcol in range(2):
            bank = fcol
            for lc in range(2):
                b.mm(bank, KV.ap(lc * T + tt * 128, lc * T + (tt + 1) * 128),
                     WVU.ap(lc * D + fcol * 512, lc * D + fcol * 512 + 512),
                     lc == 0, lc == 1,
                     r=WVU.k() + KV.k(lc * T + tt * 128, lc * T + (tt + 1) * 128))
            dst = V.ap3(tt * H * 65 + fcol * 8 * 65, [(65, 8), (1, 64)])
            b.op("DVE", "tensor_copy", dst, b.pf(bank, 0, 512), r=b.pk(bank),
                 w=V.k(tt * H * 65, (tt + 1) * H * 65) + b.pk(bank))
    b.freebuf("QST", "TMPR", "WVU", "KV", "COS", "SIN")

    P = b.buf("P", BF16, T)
    PT = b.buf("PTB", BF16, T)
    OFMB = b.cbuf("OFMB", BF16, NCH * 4, 512)
    RROW = b.buf("RROWB", F32, 128)
    OSC = b.buf("OSCB", F32, 128)
    CB = b.buf("CB", F32, 8)

    RROW = b.bufs["RROWB"]
    OSC = b.bufs["OSCB"]

    def stg_s(it, h):
        qc, prow = h // 2, (h % 2) * 64
        jmax = (it + 1) * 128
        njc = (jmax + 511) // 512
        for jc in range(njc):
            jw = min(512, jmax - jc * 512)
            b.mm(jc, Q.ap(qc * T + it * 128, qc * T + (it + 1) * 128,
                          prow, prow + 64),
                 K.ap(qc * T + jc * 512, qc * T + jc * 512 + jw, prow, prow + 64),
                 True, True,
                 r=Q.k(qc * T + it * 128, qc * T + (it + 1) * 128)
                   + K.k(qc * T + jc * 512, qc * T + jc * 512 + jw),
                 out=b.pf(jc, 0, jw))
        jcd = (it * 128) // 512
        coff = it * 128 - jcd * 512
        b.op("DVE", "tensor_tensor", b.pf(jcd, coff, coff + 128),
             b.pf(jcd, coff, coff + 128), cst["MASK"].ap(), ALU.add,
             r=cst["MASK"].k() + b.pk(jcd), w=b.pk(jcd))
        for jc in range(njc):
            jw = min(512, jmax - jc * 512)
            b.op("DVE", "tensor_reduce", CB.ap(jc, jc + 1), b.pf(jc, 0, jw),
                 axis=AX.X, op=ALU.max, r=b.pk(jc), w=CB.k() + b.pk(jc))
        for step in range(1, njc):
            b.op("DVE", "tensor_tensor", CB.ap(0, 1), CB.ap(0, 1),
                 CB.ap(step, step + 1), ALU.max, r=CB.k(), w=CB.k())
        b.op("DVE", "tensor_scalar", CB.ap(4, 5), CB.ap(0, 1), -0.125, None,
             ALU.mult, r=CB.k(), w=CB.k())

    def stg_exp(it, h):
        jmax = (it + 1) * 128
        njc = (jmax + 511) // 512
        for jc in range(njc):
            jw = min(512, jmax - jc * 512)
            b.op("ACT", "activation", P.ap(jc * 512, jc * 512 + jw),
                 b.pf(jc, 0, jw), AF.Exp, scale=0.125, bias=CB.ap(4, 5),
                 r=CB.k() + b.pk(jc), w=P.k(jc * 512, jc * 512 + jw) + b.pk(jc))

    def stg_transp(it, h):
        for g in range((it + 1 + 3) // 4):
            gw = min(4, it + 1 - g * 4)
            for c in range(gw):
                jt = g * 4 + c
                b.transp(6, P.ap(jt * 128, (jt + 1) * 128), cst["EYEB"].ap(),
                         b.pb(6, c * 128, (c + 1) * 128),
                         r=P.k(jt * 128, (jt + 1) * 128) + cst["EYEB"].k())
            b.op("DVE", "tensor_copy", PT.ap(g * 512, g * 512 + gw * 128),
                 b.pb(6, 0, gw * 128), r=b.pk(6),
                 w=PT.k(g * 512, g * 512 + gw * 128) + b.pk(6))

    def stg_av(it, h, ob):
        qc, prow = h // 2, (h % 2) * 64
        for jt in range(it + 1):
            b.mm(ob, V.ap(jt * H * 65 + h * 65, jt * H * 65 + h * 65 + 65),
                 PT.ap(jt * 128, (jt + 1) * 128), jt == 0, jt == it,
                 r=V.k(jt * H * 65, (jt + 1) * H * 65)
                   + PT.k(jt * 128, (jt + 1) * 128),
                 out=b.pf(ob, 0, 128, 0, 65))
        rinv(b, cst, RROW.ap(0, 128, 0, 1), b.pf(ob, 0, 128, 64, 65),
             RROW.k(), b.pk(ob))
        b.mm(4, cst["ROW2"].ap(128, 192, 0, 1), RROW.ap(0, 128, 0, 1),
             True, True, r=cst["ROW2"].k() + RROW.k(),
             out=b.pf(4, 0, 128, 0, 64))
        b.op("ACT", "copy", OSC.ap(0, 128, 0, 64), b.pf(4, 0, 128, 0, 64),
             r=b.pk(4), w=OSC.k() + b.pk(4))
        b.op("DVE", "tensor_tensor",
             OFMB.ap(qc * T + it * 128, qc * T + (it + 1) * 128, prow, prow + 64),
             b.pf(ob, 0, 128, 0, 64), OSC.ap(0, 128, 0, 64), ALU.mult,
             r=OSC.k() + b.pk(ob),
             w=OFMB.k(qc * T + it * 128, qc * T + (it + 1) * 128) + b.pk(ob))

    jobs = [(it, h) for it in range(NT) for h in range(H)]
    for k in range(len(jobs) + 3):
        if k >= 2 and k - 2 < len(jobs):
            stg_transp(*jobs[k - 2])
        if k >= 1 and k - 1 < len(jobs):
            stg_exp(*jobs[k - 1])
        if k >= 3:
            stg_av(*jobs[k - 3], 5 if (k - 3) % 2 == 0 else 7)
        if k < len(jobs):
            stg_s(*jobs[k])
    b.freebuf("P", "PTB", "RROWB", "OSCB", "CB", "Q", "K", "V")

    WOF = b.buf("WOF", BF16, 8 * D)
    X2B = b.cbuf("X2B", BF16, NCH * 4, 512)
    X2SC = b.buf("X2SC", F32, 512)
    SQB = b.buf("SQB", F32, 512)
    ROWR2 = b.buf("ROWR2", F32, T)
    b.dma("Pq", WOF.ap(), d["wo_f"], w=WOF.k())
    for grp in range(4):
        g0 = grp * 512
        for dch in range(NCH):
            bank = 2 + (dch % 2)
            for c in range(NCH):
                b.mm(bank, WOF.ap(c * D + dch * 128, c * D + (dch + 1) * 128),
                     OFMB.ap(c * T + g0, c * T + g0 + 512),
                     c == 0, c == NCH - 1,
                     r=WOF.k() + OFMB.k(c * T + g0, c * T + g0 + 512))
            c0 = dch * T + g0
            b.op("DVE", "tensor_tensor", X2SC.ap(0, 512), b.pf(bank, 0, 512),
                 RES.ap(c0, c0 + 512), ALU.add,
                 r=RES.k(c0, c0 + 512) + b.pk(bank), w=X2SC.k() + b.pk(bank))
            b.op("DVE", "tensor_copy", X2B.ap(c0, c0 + 512), X2SC.ap(0, 512),
                 r=X2SC.k(), w=X2B.k(c0, c0 + 512))
            b.op("ACT", "activation", SQB.ap(0, 512), X2SC.ap(0, 512), AF.Square,
                 r=X2SC.k(), w=SQB.k())
            b.mm(7, cst["ONESC"].ap(0, 1), SQB.ap(0, 512), dch == 0,
                 dch == NCH - 1, r=cst["ONESC"].k() + SQB.k(),
                 out=b.pf(7, 0, 512, 0, 1))
            b.op("DVE", "tensor_tensor", RES.ap(c0, c0 + 512), RES.ap(c0, c0 + 512),
                 X2SC.ap(0, 512), ALU.add,
                 r=RES.k(c0, c0 + 512) + X2SC.k(), w=RES.k(c0, c0 + 512))
        b.op("ACT", "activation", ROWR2.ap(g0, g0 + 512, 0, 1),
             b.pf(7, 0, 512, 0, 1), AF.Ln,
             bias=cst["EPSC"].ap(1, 2, 0, 1), scale=1.0 / D,
             r=cst["EPSC"].k() + b.pk(7), w=ROWR2.k(g0, g0 + 512) + b.pk(7))
        b.op("ACT", "activation", ROWR2.ap(g0, g0 + 512, 0, 1),
             ROWR2.ap(g0, g0 + 512, 0, 1), AF.Exp, scale=-0.5,
             r=ROWR2.k(g0, g0 + 512), w=ROWR2.k(g0, g0 + 512))
    b.freebuf("WOF", "OFMB", "X2SC", "SQB")
    RSTDB2 = b.buf("RSTDB2", F32, T)
    bcast_row(b, cst, ROWR2, RSTDB2, 3)
    for ch in range(NCH):
        for tc in range(4):
            c0 = ch * T + tc * 512
            b.op("DVE", "scalar_tensor_tensor", X2B.ap(c0, c0 + 512),
                 X2B.ap(c0, c0 + 512), cst["GAIN"].ap(16 + ch, 17 + ch),
                 RSTDB2.ap(tc * 512, tc * 512 + 512), op0=ALU.mult, op1=ALU.mult,
                 r=X2B.k(c0, c0 + 512) + cst["GAIN"].k()
                   + RSTDB2.k(tc * 512, tc * 512 + 512),
                 w=X2B.k(c0, c0 + 512))
    b.freebuf("ROWR2", "RSTDB2")
    dump_fm(b, "x2s", bi, RES)
    dump_bf(b, "rn2", bi, X2B)


def stage_c(b, cst, bi):
    d = b.dram
    RES = b.bufs["RES"]
    RN2 = b.bufs["X2B"]
    C = 1024
    ZBASE = 8 * C

    WRS = b.buf("WRS", BF16, 8 * E)
    I128 = b.buf("I128", F32, T)
    R16 = b.buf("R16", F32, 32)
    SROWS = b.buf("SROWS", F32, T)
    MR = b.buf("MR", F32, T)
    RNK = b.buf("RNK", F32, T)
    ROWS4 = b.buf("ROWS4", F32, 4 * T)
    ESC = b.buf("ESC", F32, 16)
    DCOL = b.buf("DCOL", mybir.dt.uint32, 16)
    PRB = b.buf("PRB", F32, 8)
    EQ = b.buf("EQ", F32, 8)
    CC = b.buf("CC", F32, 8)
    SMAT = b.buf("SMAT", F32, 16)
    SM2 = b.buf("SM2", F32, 16)
    b.dma("Pq", WRS.ap(), d["wr"], w=WRS.k())
    b.dma("Pq", I128.ap(0, T, 0, 1), d["c_i128"], w=I128.k())
    b.dma("Pq", R16.ap(0, 32, 0, 16), d["c_r16"], w=R16.k())
    PROBS = b.buf("PROBS", F32, 16 * E)
    PCOL = b.buf("PCOL", F32, 16)
    RTW = b.buf("RTW", F32, 16)
    b.op("DVE", "memset", SMAT.ap(), 0.0, w=SMAT.k())
    b.op("DVE", "memset", SM2.ap(), 0.0, w=SM2.k())
    for tt in range(NT):
        bank = 6 + tt % 2
        for dc in range(NCH):
            b.mm(bank, RN2.ap(dc * T + tt * 128, dc * T + (tt + 1) * 128),
                 WRS.ap(dc * E, (dc + 1) * E), dc == 0, dc == NCH - 1,
                 r=WRS.k() + RN2.k(dc * T + tt * 128, dc * T + (tt + 1) * 128),
                 out=b.pf(bank, 0, E))
        b.op("DVE", "tensor_tensor", PROBS.ap(tt * E, (tt + 1) * E),
             b.pf(bank, 0, E), cst["RB8"].ap(), ALU.add,
             r=cst["RB8"].k() + b.pk(bank),
             w=PROBS.k(tt * E, (tt + 1) * E) + b.pk(bank))
    b.op("DVE", "tensor_scalar", PROBS.ap(), PROBS.ap(), -50.0, 50.0,
         ALU.max, ALU.min, r=PROBS.k(), w=PROBS.k())
    b.op("ACT", "activation", PROBS.ap(), PROBS.ap(), AF.Sigmoid,
         r=PROBS.k(), w=PROBS.k())
    for tt in range(NT):
        b.op("DVE", "tensor_reduce", PCOL.ap(tt, tt + 1),
             PROBS.ap(tt * E, (tt + 1) * E), axis=AX.X, op=ALU.max,
             r=PROBS.k(), w=PCOL.k())
    b.op("DVE", "tensor_scalar", PCOL.ap(), PCOL.ap(), 1e-8, 1.0,
         ALU.max, ALU.min, r=PCOL.k(), w=PCOL.k())
    b.op("DVE", "tensor_scalar", RTW.ap(), PCOL.ap(), 0.5, 1e-8,
         ALU.add, ALU.add, r=PCOL.k(), w=RTW.k())
    rinv(b, cst, RTW.ap(), RTW.ap(), RTW.k(), RTW.k())
    b.op("DVE", "tensor_tensor", ESC.ap(), PCOL.ap(), RTW.ap(), ALU.mult,
         r=PCOL.k() + RTW.k(), w=ESC.k())
    for tt in range(NT):
        b.op("DVE", "tensor_scalar", EQ.ap(), PROBS.ap(tt * E, (tt + 1) * E),
             PCOL.ap(tt, tt + 1), None, ALU.is_equal,
             r=PROBS.k() + PCOL.k(), w=EQ.k())
        b.op("DVE", "tensor_scalar", EQ.ap(), EQ.ap(), -1e9, 1e9,
             ALU.mult, ALU.add, r=EQ.k(), w=EQ.k())
        b.op("DVE", "tensor_tensor", EQ.ap(), EQ.ap(), cst["IO8"].ap(), ALU.add,
             r=EQ.k() + cst["IO8"].k(), w=EQ.k())
        b.op("DVE", "tensor_reduce", CC.ap(1, 2), EQ.ap(), axis=AX.X, op=ALU.min,
             r=EQ.k(), w=CC.k())
        b.op("DVE", "tensor_scalar", SMAT.ap(8, 9), RTW.ap(tt, tt + 1), 0.5, None,
             ALU.mult, r=RTW.k(), w=SMAT.k())
        for e in range(E):
            b.op("DVE", "tensor_scalar", SM2.ap(e, e + 1), CC.ap(1, 2),
                 float(e), None, ALU.is_equal, r=CC.k(), w=SM2.k())
        b.op("DVE", "tensor_copy", SM2.ap(8, 9), CC.ap(1, 2),
             r=CC.k(), w=SM2.k())
        b.transp(7, SMAT.ap(), cst["EYEF"].ap(), b.pf(7, 0, 128, 0, 16),
                 r=SMAT.k() + cst["EYEF"].k())
        b.op("DVE", "tensor_copy", SROWS.ap(tt * 128, (tt + 1) * 128, 0, 16),
             b.pf(7, 0, 128, 0, 16), r=b.pk(7),
             w=SROWS.k(tt * 128, (tt + 1) * 128) + b.pk(7))
        b.transp(6, SM2.ap(), cst["EYEF"].ap(), b.pf(6, 0, 128, 0, 16),
                 r=SM2.k() + cst["EYEF"].k())
        b.op("DVE", "tensor_copy", MR.ap(tt * 128, (tt + 1) * 128, 0, 16),
             b.pf(6, 0, 128, 0, 16), r=b.pk(6),
             w=MR.k(tt * 128, (tt + 1) * 128) + b.pk(6))
    b.freebuf("WRS", "PRB", "EQ", "SMAT", "SM2", "PROBS", "PCOL", "RTW")

    ones16 = bass.AP(cst["ONESC"].ap().tensor, cst["ONESC"].ap().offset,
                     [list(cst["ONESC"].ap().ap[0])[:1] + [16], [0, T]])
    b.op("DVE", "tensor_tensor_scan", RNK.ap(0, T, 0, 16), ones16,
         MR.ap(0, T, 0, 16), 0.0, op0=ALU.mult, op1=ALU.add,
         r=cst["ONESC"].k() + MR.k(0, T), w=RNK.k(0, T))
    b.op("DVE", "tensor_tensor", RNK.ap(0, T, 0, 16), RNK.ap(0, T, 0, 16),
         MR.ap(0, T, 0, 16), ALU.subtract, r=RNK.k(0, T) + MR.k(0, T),
         w=RNK.k(0, T))
    for tc in range(4):
        b.mm(7, R16.ap(8, 9, 0, 16), MR.ap(tc * 512, tc * 512 + 512, 0, 16),
             True, True, r=R16.k() + MR.k(tc * 512, tc * 512 + 512),
             out=b.pf(7, 0, 512, 0, 1))
        b.op("DVE", "tensor_copy", ROWS4.ap(tc * 512, tc * 512 + 512, 0, 1),
             b.pf(7, 0, 512, 0, 1), r=b.pk(7),
             w=ROWS4.k(tc * 512, tc * 512 + 512) + b.pk(7))
    b.op("DVE", "tensor_tensor", MR.ap(0, T, 0, 16), MR.ap(0, T, 0, 16),
         RNK.ap(0, T, 0, 16), ALU.mult, r=MR.k(0, T) + RNK.k(0, T),
         w=MR.k(0, T))
    for tc in range(4):
        b.mm(7, R16.ap(16, 17, 0, 16), MR.ap(tc * 512, tc * 512 + 512, 0, 16),
             True, True, r=R16.k() + MR.k(tc * 512, tc * 512 + 512),
             out=b.pf(7, 0, 512, 0, 1))
        b.op("DVE", "tensor_copy", ROWS4.ap(T + tc * 512, T + tc * 512 + 512, 0, 1),
             b.pf(7, 0, 512, 0, 1), r=b.pk(7),
             w=ROWS4.k(T + tc * 512, T + tc * 512 + 512) + b.pk(7))
    r_sel = ROWS4.ap(0, T, 0, 1)
    r_rnk = ROWS4.ap(T, 2 * T, 0, 1)
    r_d = ROWS4.ap(2 * T, 3 * T, 0, 1)
    r_t = ROWS4.ap(3 * T, 4 * T, 0, 1)
    kS, kR = ROWS4.k(0, T), ROWS4.k(T, 2 * T)
    kD, kT = ROWS4.k(2 * T, 3 * T), ROWS4.k(3 * T, 4 * T)
    b.op("DVE", "scalar_tensor_tensor", r_d, r_sel, float(C), r_rnk,
         op0=ALU.mult, op1=ALU.add, r=kS + kR, w=kD)
    b.op("DVE", "tensor_scalar", r_t, r_rnk, float(C), None, ALU.is_lt,
         r=kR, w=kT)
    b.op("DVE", "tensor_tensor", r_d, r_d, r_t, ALU.mult, r=kD + kT, w=kD)
    b.op("DVE", "tensor_scalar", r_t, r_t, -1.0, 1.0, ALU.mult, ALU.add,
         r=kT, w=kT)
    b.op("DVE", "tensor_tensor", r_t, r_t, I128.ap(0, T, 0, 1), ALU.mult,
         r=kT + I128.k(), w=kT)
    b.op("DVE", "tensor_tensor", r_d, r_d, r_t, ALU.add, r=kD + kT, w=kD)
    for tt in range(NT):
        b.transp(7, ROWS4.ap(2 * T + tt * 128, 2 * T + (tt + 1) * 128, 0, 1),
                 cst["EYEF"].ap(0, 1, 0, 1), b.pf(7, 0, 1, 0, 128),
                 r=kD + cst["EYEF"].k())
        b.op("DVE", "tensor_copy", DCOL.ap(tt, tt + 1), b.pf(7, 0, 1, 0, 128),
             r=b.pk(7), w=DCOL.k() + b.pk(7))
    if "mrd" in DBG:
        key = ("DBG", "mrd", bi, 0)
        b.dbg_keys.append(key)
        b.dma("SOq", b.dram["dbg_mrd"][bi * NT * 128: bi * NT * 128 + 16, 0:T],
              MR.ap(0, T, 0, 16), r=MR.k(0, T), w=[key])
        key2 = ("DBG", "mrd", bi, 1)
        b.dbg_keys.append(key2)
        b.dma("SOq", b.dram["dbg_mrd"][bi * NT * 128 + 16: bi * NT * 128 + 20, 0:T],
              ROWS4.ap3(0, [(T, 4), (1, T)], 0, 1), r=ROWS4.k(0, 4 * T), w=[key2])
    b.freebuf("MR", "RNK", "ROWS4", "I128", "R16")

    EXP9 = b.buf("EXP9", F32, 9 * 128)
    b.dma("Sq", EXP9.ap(0, 9 * 128, 0, 16), d["c_exp9"], w=EXP9.k())
    SBC = b.buf("SBC", BF16, T)
    for tc in range(4):
        b.mm(7, EXP9.ap(8 * 128, 9 * 128, 0, 16),
             SROWS.ap(tc * 512, tc * 512 + 512, 0, 16), True, True,
             r=EXP9.k() + SROWS.k(tc * 512, tc * 512 + 512))
        b.op("DVE", "tensor_copy", SBC.ap(tc * 512, tc * 512 + 512),
             b.pf(7, 0, 512), r=b.pk(7),
             w=SBC.k(tc * 512, tc * 512 + 512) + b.pk(7))
    b.freebuf("SROWS", "EXP9")

    if bi == 0:
        ZT = b.buf("ZT", BF16, D)
        b.op("DVE", "memset", ZT.ap(), 0.0, w=ZT.k())
        for rr in range(65):
            b.dma("SOq", d["RG"][rr * 128:(rr + 1) * 128, :], ZT.ap(),
                  r=ZT.k(), w=[("RG",)])
        b.dma("SOq", d["OGT"][ZBASE:ZBASE + 128, :], ZT.ap(), r=ZT.k(),
              w=[("OGTZ",)])
        b.freebuf("ZT")
    TMT = b.buf("TMT", BF16, 2 * D)
    for tt in range(NT):
        sl = tt % 2
        for g in range(2):
            bank = 6 + g
            for c in range(4):
                ch = g * 4 + c
                b.transp(bank, RN2.ap(ch * T + tt * 128, ch * T + (tt + 1) * 128),
                         cst["EYEB"].ap(), b.pb(bank, c * 128, (c + 1) * 128),
                         r=RN2.k(ch * T + tt * 128, ch * T + (tt + 1) * 128)
                           + cst["EYEB"].k())
            b.op("DVE", "tensor_copy",
                 TMT.ap(sl * D + g * 512, sl * D + (g + 1) * 512),
                 b.pb(bank, 0, 512), r=b.pk(bank),
                 w=TMT.k(sl * D + g * 512, sl * D + (g + 1) * 512) + b.pk(bank))
        b.idma(d["RG"], DCOL.ap(tt, tt + 1), TMT.ap(sl * D, (sl + 1) * D), None,
               r=TMT.k(sl * D, (sl + 1) * D) + DCOL.k(), w=[("RG",)])
    b.freebuf("TMT")

    G0 = b.cbuf("G0", BF16, 8, 512)
    G1 = b.cbuf("G1", BF16, 8, 512)
    USC = b.buf("USC", BF16, 512)
    WSL = [b.buf(f"WC{i}", BF16, 8 * 1024) for i in range(4)]
    XG = XGT = OUTC = None
    gbufs = [G0, G1]

    steps = [(8, 0), (8, 1)] + [(e, F_) for e in range(E) for F_ in range(2)]

    def w_src(e, F_, which):
        nm = ["ew1", "ew3", "ew2"][which] if e < E else ["sw1", "sw3", "sw2"][which]
        ap = d[nm]
        if e < E:
            return ap[e:e + 1, F_:F_ + 1]
        return ap[F_:F_ + 1]

    def load_step(si):
        e, F_ = steps[si]
        for j in range(3):
            sl = WSL[(si * 3 + j) % 4]
            b.dma("Pq", sl.ap(), w_src(e, F_, j), w=sl.k())

    load_step(0)
    pend_w2 = []

    def emit_w2_shared(tc, gsl, w2s):
        g = gbufs[gsl]
        for dch in range(NCH):
            bank = 4 + (dch % 2)
            for fc in range(NCH):
                b.mm(bank, w2s.ap(fc * 1024 + dch * 128, fc * 1024 + (dch + 1) * 128),
                     g.ap(fc * 512, (fc + 1) * 512), fc == 0, fc == NCH - 1,
                     r=w2s.k() + g.k(fc * 512, (fc + 1) * 512))
            c0 = dch * T + tc * 512
            b.op("DVE", "tensor_tensor", RES.ap(c0, c0 + 512), b.pf(bank, 0, 512),
                 RES.ap(c0, c0 + 512), ALU.add,
                 r=RES.k(c0, c0 + 512) + b.pk(bank),
                 w=RES.k(c0, c0 + 512) + b.pk(bank))

    def emit_w2_routed(e, F_, tc, gsl, w2s):
        g = gbufs[gsl]
        for st in range(4):
            slot0 = tc * 512 + st * 128
            sti = slot0 // 128
            for dcol in range(2):
                bank = 4 + (st % 2)
                for fc in range(NCH):
                    b.mm(bank, g.ap(fc * 512 + st * 128, fc * 512 + (st + 1) * 128),
                         w2s.ap(fc * 1024 + dcol * 512, fc * 1024 + dcol * 512 + 512),
                         fc == 0, fc == NCH - 1,
                         r=w2s.k(fc * 512, (fc + 1) * 512)
                           + g.k(fc * 512, (fc + 1) * 512))
                oc0 = sti * D + dcol * 512
                if F_ == 0:
                    b.op("DVE", "tensor_copy", OUTC.ap(oc0, oc0 + 512),
                         b.pf(bank, 0, 512), r=b.pk(bank),
                         w=OUTC.k(oc0, oc0 + 512) + b.pk(bank))
                else:
                    b.op("DVE", "tensor_tensor", OUTC.ap(oc0, oc0 + 512),
                         b.pf(bank, 0, 512), OUTC.ap(oc0, oc0 + 512), ALU.add,
                         r=OUTC.k(oc0, oc0 + 512) + b.pk(bank),
                         w=OUTC.k(oc0, oc0 + 512) + b.pk(bank))
            if F_ == 1:
                row0 = e * C + slot0
                b.dma("Pq", d["OGT"][row0:row0 + 128, :], OUTC.ap(sti * D, (sti + 1) * D),
                      r=OUTC.k(sti * D, (sti + 1) * D), w=[("OGT", e, slot0)])

    for si, (e, F_) in enumerate(steps):
        w1s = WSL[(si * 3 + 0) % 4]
        w3s = WSL[(si * 3 + 1) % 4]
        w2s = WSL[(si * 3 + 2) % 4]
        shared = e == 8
        if e == 0 and F_ == 0:
            b.freebuf("X2B")
            XG = [b.cbuf("XGa", BF16, 8, C), b.cbuf("XGb", BF16, 8, C)]
            XGT = b.buf("XGT", BF16, 2 * D)
            OUTC = b.cbuf("OUTC", BF16, 8, D)
        def xg_fill(ee):
            xg = XG[ee % 2]
            for st in range(8):
                sl2 = st % 2
                b.dma("Sq", XGT.ap(sl2 * D, (sl2 + 1) * D),
                      d["RG"][ee * C + st * 128: ee * C + (st + 1) * 128, :],
                      r=[("RG",)], w=XGT.k(sl2 * D, (sl2 + 1) * D))
                for g2 in range(2):
                    bank = 6 + g2
                    for c2 in range(4):
                        ch2 = g2 * 4 + c2
                        b.transp(bank,
                                 XGT.ap(sl2 * D + ch2 * 128, sl2 * D + (ch2 + 1) * 128),
                                 cst["EYEB"].ap(), b.pb(bank, c2 * 128, (c2 + 1) * 128),
                                 r=XGT.k(sl2 * D, (sl2 + 1) * D) + cst["EYEB"].k())
                    for c2 in range(4):
                        ch2 = g2 * 4 + c2
                        c0x = ch2 * C + st * 128
                        b.op("DVE", "tensor_copy", xg.ap(c0x, c0x + 128),
                             b.pb(bank, c2 * 128, (c2 + 1) * 128), r=b.pk(bank),
                             w=xg.k(c0x, c0x + 128) + b.pk(bank))

        if e == 0 and F_ == 0:
            xg_fill(0)
        if not shared and F_ == 1 and e + 1 < E:
            xg_fill(e + 1)
        SRC = RN2 if shared else XG[e % 2]
        ntc = 4 if shared else 2
        srcw = T if shared else C
        for tc in range(ntc):
            gsl = tc % 2
            g = gbufs[gsl]
            for fc in range(NCH):
                hb = fc % 2
                ub = 2 + (fc % 2)
                for dc in range(NCH):
                    b.mm(hb, w1s.ap(dc * 1024 + fc * 128, dc * 1024 + (fc + 1) * 128),
                         SRC.ap(dc * srcw + tc * 512, dc * srcw + tc * 512 + 512),
                         dc == 0, dc == NCH - 1,
                         r=w1s.k() + SRC.k(dc * srcw + tc * 512,
                                           dc * srcw + tc * 512 + 512))
                b.op("ACT", "activation", g.ap(fc * 512, (fc + 1) * 512),
                     b.pf(hb, 0, 512), AF.Silu, r=b.pk(hb),
                     w=g.k(fc * 512, (fc + 1) * 512) + b.pk(hb))
                for dc in range(NCH):
                    b.mm(ub, w3s.ap(dc * 1024 + fc * 128, dc * 1024 + (fc + 1) * 128),
                         SRC.ap(dc * srcw + tc * 512, dc * srcw + tc * 512 + 512),
                         dc == 0, dc == NCH - 1,
                         r=w3s.k() + SRC.k(dc * srcw + tc * 512,
                                           dc * srcw + tc * 512 + 512))
                if shared:
                    b.op("DVE", "tensor_tensor", USC.ap(0, 512), b.pf(ub, 0, 512),
                         SBC.ap(tc * 512, tc * 512 + 512), ALU.mult,
                         r=SBC.k(tc * 512, tc * 512 + 512) + b.pk(ub),
                         w=USC.k() + b.pk(ub))
                    b.op("DVE", "tensor_tensor", g.ap(fc * 512, (fc + 1) * 512),
                         g.ap(fc * 512, (fc + 1) * 512), USC.ap(0, 512), ALU.mult,
                         r=g.k(fc * 512, (fc + 1) * 512) + USC.k(),
                         w=g.k(fc * 512, (fc + 1) * 512))
                else:
                    b.op("DVE", "tensor_tensor", g.ap(fc * 512, (fc + 1) * 512),
                         g.ap(fc * 512, (fc + 1) * 512), b.pf(ub, 0, 512), ALU.mult,
                         r=g.k(fc * 512, (fc + 1) * 512) + b.pk(ub),
                         w=g.k(fc * 512, (fc + 1) * 512) + b.pk(ub))
            pend_w2.append((e, F_, tc, gsl, w2s))
            if tc == ntc - 1 and si + 1 < len(steps):
                load_step(si + 1)
            if len(pend_w2) >= 2:
                ee, ff, tcc, gg, ww = pend_w2.pop(0)
                if ee == 8:
                    emit_w2_shared(tcc, gg, ww)
                else:
                    emit_w2_routed(ee, ff, tcc, gg, ww)
    while pend_w2:
        ee, ff, tcc, gg, ww = pend_w2.pop(0)
        if ee == 8:
            emit_w2_shared(tcc, gg, ww)
        else:
            emit_w2_routed(ee, ff, tcc, gg, ww)
    b.freebuf("SBC", "G0", "G1", "USC", "XGa", "XGb", "XGT", "OUTC",
              "WC0", "WC1", "WC2", "WC3")


def stage_f(b, cst, bi):
    d = b.dram
    RES = b.bufs["RES"]
    ESC = b.bufs["ESC"]
    DCOL = b.bufs["DCOL"]
    C = 1024
    ZTM = b.buf("ZTMF", F32, D)
    SQA = b.buf("SQAF", BF16, D)
    COLS = b.buf("COLSF", F32, 8)
    MOET = b.buf("MOET", BF16, 2 * D)
    ogt_keys = [("OGT", e, s0) for e in range(E) for s0 in range(0, C, 128)]
    ogt_keys += [("OGTZ",)]
    for tt in range(NT):
        sl = tt % 2
        b.idma(MOET.ap(sl * D, (sl + 1) * D), None, d["OGT"],
               DCOL.ap(tt, tt + 1),
               r=ogt_keys + DCOL.k(), w=MOET.k(sl * D, (sl + 1) * D))
        for g in range(2):
            bank = 6 + g
            for c in range(4):
                ch = g * 4 + c
                b.transp(bank, RES.ap(ch * T + tt * 128, ch * T + (tt + 1) * 128),
                         cst["EYEF"].ap(), b.pf(bank, c * 128, (c + 1) * 128),
                         r=RES.k(ch * T + tt * 128, ch * T + (tt + 1) * 128)
                           + cst["EYEF"].k())
            b.op("DVE", "tensor_copy", ZTM.ap(g * 512, g * 512 + 512),
                 b.pf(bank, 0, 512), r=b.pk(bank),
                 w=ZTM.k(g * 512, g * 512 + 512) + b.pk(bank))
        b.op("DVE", "scalar_tensor_tensor", ZTM.ap(), MOET.ap(sl * D, (sl + 1) * D),
             ESC.ap(tt, tt + 1), ZTM.ap(), op0=ALU.mult, op1=ALU.add,
             r=MOET.k(sl * D, (sl + 1) * D) + ESC.k() + ZTM.k(), w=ZTM.k())
        layernorm_tm(b, cst, ZTM, SQA, COLS)
        b.dma("SOq", d["out"][bi * T + tt * 128: bi * T + (tt + 1) * 128, :],
              ZTM.ap(), r=ZTM.k(), w=[("OUT", bi, tt)])
    b.freebuf("ZTMF", "SQAF", "COLSF", "MOET", "RES", "ESC", "DCOL")


def _lay(w):
    din, dout = w.shape
    return np.ascontiguousarray(
        w.reshape(din // 128, 128, dout).transpose(1, 0, 2).reshape(128, -1)
    ).astype(np.float32)


def _layF(w):
    return np.ascontiguousarray(
        w.reshape(8, 128, 2, 1024).transpose(2, 1, 0, 3).reshape(2, 128, 8 * 1024)
    ).astype(np.float32)


def _layF2(w):
    return np.ascontiguousarray(
        w.reshape(2, 8, 128, 1024).transpose(0, 2, 1, 3).reshape(2, 128, 8 * 1024)
    ).astype(np.float32)


def _prep_inputs(inp):
    g = {}
    g["wq_a"] = _lay(inp["L0_attn_Wq"])
    g["wk_a"] = _lay(inp["L0_attn_Wkv"][:, :D])
    g["wv_a"] = _lay(inp["L0_attn_Wkv"][:, D:])
    g["wo_a"] = _lay(inp["L0_attn_Wo"])
    g["c_qg"] = np.ascontiguousarray(
        inp["L0_attn_q_gamma"].reshape(-1).reshape(8, 128).T).astype(np.float32)
    g["c_kg"] = np.ascontiguousarray(
        inp["L0_attn_k_gamma"].reshape(-1).reshape(8, 128).T).astype(np.float32)
    g["wq_f"] = _lay(inp["L0_ffn_Wq"])
    g["wkv_f"] = _lay(inp["L0_ffn_Wkv"])
    g["wk_up"] = _lay(np.ascontiguousarray(
        inp["L0_ffn_Wk_up"].transpose(1, 0, 2).reshape(L, H * DH)))
    g["wv_up"] = _lay(np.ascontiguousarray(
        inp["L0_ffn_Wv_up"].transpose(1, 0, 2).reshape(L, H * DH)))
    g["wo_f"] = _lay(inp["L0_ffn_Wo"])
    g["wr"] = _lay(inp["L0_ffn_Wr"])
    g["c_rb"] = inp["L0_ffn_r_bias"].reshape(1, E).astype(np.float32)
    g["sw1"] = _layF(inp["L0_ffn_sw1"])
    g["sw3"] = _layF(inp["L0_ffn_sw3"])
    g["sw2"] = _layF2(inp["L0_ffn_sw2"])
    g["ew1"] = np.stack([_layF(inp["L0_ffn_ew1"][e]) for e in range(E)])
    g["ew3"] = np.stack([_layF(inp["L0_ffn_ew3"][e]) for e in range(E)])
    g["ew2"] = np.stack([_layF2(inp["L0_ffn_ew2"][e]) for e in range(E)])

    g["c_eye"] = np.eye(128, dtype=np.float32)
    m = np.zeros((128, 128), dtype=np.float32)
    m[np.triu_indices(128, 1)] = NEG
    g["c_mask"] = m
    inv = 1.0 / (100000.0 ** (np.arange(0, DH, 2, dtype=np.float64) / DH))
    tpos = np.arange(T, dtype=np.float64)[:, None] * inv[None, :]
    g["c_cos"] = np.ascontiguousarray(np.cos(tpos).astype(np.float32)
                                      .reshape(NT, 128, 32).transpose(1, 0, 2)
                                      .reshape(128, NT * 32))
    g["c_sin"] = np.ascontiguousarray(np.sin(tpos).astype(np.float32)
                                      .reshape(NT, 128, 32).transpose(1, 0, 2)
                                      .reshape(128, NT * 32))
    gain = np.zeros((128, 24), dtype=np.float32)
    gain[:, 0:8] = inp["L0_attn_norm_w"].reshape(8, 128).T
    gain[:, 8:16] = inp["L0_ffn_ln1_w"].reshape(8, 128).T
    gain[:, 16:24] = inp["L0_ffn_ln2_w"].reshape(8, 128).T
    g["c_gain"] = gain
    g["c_io8"] = np.tile(np.arange(8, dtype=np.float32), (128, 1))
    ind2 = np.zeros((128, 2), dtype=np.float32)
    ind2[0:64, 0] = 1.0
    ind2[64:128, 1] = 1.0
    g["c_ind2"] = ind2
    row2 = np.zeros((2, 256), dtype=np.float32)
    row2[0, 0:64] = 1.0
    row2[1, 64:128] = 1.0
    row2[0, 128:256] = 1.0
    g["c_row2"] = row2
    exp9 = np.zeros((16, 9 * 128), dtype=np.float32)
    for e in range(9):
        exp9[e, e * 128:(e + 1) * 128] = 1.0
    g["c_exp9"] = exp9
    g["c_i128"] = (8 * 1024 + (np.arange(T) % 128)).astype(np.float32).reshape(1, T)
    r16 = np.zeros((16, 32), dtype=np.float32)
    r16[:16, :16] = np.eye(16)
    r16[0:8, 16] = 1.0
    g["c_r16"] = r16
    return g


def kernel(**inputs):
    inputs = {k: np.asarray(v, dtype=np.float32) for k, v in inputs.items()}
    g = _prep_inputs(inputs)
    toks = np.ascontiguousarray(inputs["L0_attn_tokens"].reshape(16 * T, D))
    nc = build_nc()
    in_maps = []
    for c in range(N_CORES):
        m = dict(g)
        m["x0"] = np.ascontiguousarray(
            toks[c * B_PER_CORE * T:(c + 1) * B_PER_CORE * T, :])
        in_maps.append(m)
    res = run_bass_kernel_spmd(nc, in_maps, core_ids=list(range(N_CORES)))
    out = np.concatenate([r["out"].reshape(B_PER_CORE, T, D) for r in res.results],
                         axis=0)
    return out


# version 19 (final)
# speedup vs baseline: 1.0200x; 1.0090x over previous
import numpy as np
import concourse.bass as bass
import concourse.mybir as mybir
from concourse.bass_utils import run_bass_kernel_spmd

F32 = mybir.dt.float32
BF16 = mybir.dt.bfloat16
AF = mybir.ActivationFunctionType
ALU = mybir.AluOpType
AX = mybir.AxisListType

N_CORES = 8
B_PER_CORE = 2
T = 2048
D = 1024
H = 16
DH = 64
NCH = D // 128
NT = T // 128
L = 256
E = 8
EPS_A = float(np.finfo(np.float32).eps)
EPS_F = 1e-6
EPS_LN = 1e-5
NEG = -1.0e30

STAGES = ("A", "B", "C", "F")
DBG = ()

KEYBLK = 2048


class Sched:
    def __init__(self):
        self.ops = []
        self.last_w = {}
        self.readers = {}

    def add(self, ch, emit, r=(), w=(), dma=None):
        i = len(self.ops)
        raw = set()
        other = set()
        for k in r:
            j = self.last_w.get(k)
            if j is not None:
                raw.add(j)
        for k in w:
            j = self.last_w.get(k)
            if j is not None:
                other.add(j)
            for j2 in self.readers.get(k, ()):
                other.add(j2)
        for k in r:
            self.readers.setdefault(k, []).append(i)
        for k in w:
            self.last_w[k] = i
            self.readers[k] = []
        self.ops.append({"ch": ch, "emit": emit, "dma": dma, "raw": raw,
                         "other": other})
        return i

    def finalize(self):
        ops = self.ops
        n = len(ops)
        deps = [None] * n
        for i, op in enumerate(ops):
            d = set()
            for j in op["raw"]:
                pj = ops[j]
                if (pj["ch"] == op["ch"] == "PE"
                        and pj["dma"] is None and op["dma"] is None):
                    continue
                d.add(j)
            for j in op["other"]:
                pj = ops[j]
                if (pj["ch"] == op["ch"] == "PE"
                        and pj["dma"] is None and op["dma"] is None):
                    continue
                d.add(j)
            d.discard(i)
            deps[i] = d
        need = [False] * n
        for i in range(n):
            for j in deps[i]:
                need[j] = True
        NSUB = {"Pq": 36, "Sq": 36, "POq": 1, "SOq": 1}
        cnt = {}
        inc_val = [None] * n
        sub_of = [None] * n
        dma_cum = [None] * n
        cur_dma = {}
        rr = {}
        for i, op in enumerate(ops):
            dma_cum[i] = dict(cur_dma)
            if need[i]:
                if op["dma"]:
                    q = op["dma"]
                    rr[q] = rr.get(q, -1) + 1
                    sem = f"{q}{rr[q] % NSUB[q]}"
                    sub_of[i] = sem
                    cnt[sem] = cnt.get(sem, 0) + 16
                    inc_val[i] = cnt[sem]
                    cur_dma[sem] = cnt[sem]
                else:
                    sem = op["ch"]
                    cnt[sem] = cnt.get(sem, 0) + 1
                    inc_val[i] = cnt[sem]
        per_ch = {}
        clock = {}
        for i, op in enumerate(ops):
            waits = []
            for j in sorted(deps[i]):
                pj = ops[j]
                if pj["dma"]:
                    sem = sub_of[j]
                    v = dma_cum[i].get(sem, 0)
                else:
                    sem = pj["ch"]
                    v = inc_val[j]
                key = (op["ch"], sem)
                if clock.get(key, 0) >= v:
                    continue
                clock[key] = v
                waits.append((sem, v))
            inc = None
            if need[i]:
                sem = sub_of[i] if op["dma"] else op["ch"]
                inc = (sem, 16 if op["dma"] else 1)
            per_ch.setdefault(op["ch"], []).append(
                {"emit": op["emit"], "waits": waits, "inc": inc})
        self.sem_totals = cnt
        return per_ch


class Arena:
    def __init__(self, nbytes):
        self.free_l = [(0, nbytes)]
        self.used = {}
        self.hw = 0

    def alloc(self, name, nb, hi=False):
        nb = (nb + 31) & ~31
        idxs = range(len(self.free_l))
        if hi:
            idxs = reversed(list(idxs))
        for idx in idxs:
            off, sz = self.free_l[idx]
            if sz >= nb:
                if hi:
                    self.used[name] = (off + sz - nb, nb)
                    if sz == nb:
                        self.free_l.pop(idx)
                    else:
                        self.free_l[idx] = (off, sz - nb)
                    return off + sz - nb
                self.used[name] = (off, nb)
                if sz == nb:
                    self.free_l.pop(idx)
                else:
                    self.free_l[idx] = (off + nb, sz - nb)
                return off
        raise MemoryError(f"arena: cannot alloc {name} ({nb}B); free={self.free_l}")

    def release(self, name):
        off, nb = self.used.pop(name)
        self.free_l.append((off, nb))
        self.free_l.sort()
        merged = []
        for o, s in self.free_l:
            if merged and merged[-1][0] + merged[-1][1] == o:
                merged[-1] = (merged[-1][0], merged[-1][1] + s)
            else:
                merged.append((o, s))
        self.free_l = merged


class Buf:
    def __init__(self, bld, name, dtype, cols, hi=True):
        self.bld = bld
        self.name = name
        self.dtype = dtype
        self.isz = 2 if dtype == BF16 else 4
        self.cols = cols
        self.off_b = bld.arena.alloc(name, cols * self.isz, hi=hi)

    def ap(self, c0=0, c1=None, p0=0, p1=128):
        c1 = self.cols if c1 is None else c1
        h, W = self.bld.handle(self.dtype)
        off = p0 * W + self.off_b // self.isz + c0
        return bass.AP(h, off, [[W, p1 - p0], [1, c1 - c0]])

    def ap3(self, c0, dims, p0=0, p1=128):
        h, W = self.bld.handle(self.dtype)
        off = p0 * W + self.off_b // self.isz + c0
        return bass.AP(h, off, [[W, p1 - p0]] + [[s, c] for s, c in dims])

    def k(self, c0=0, c1=None):
        c1 = self.cols if c1 is None else c1
        b0 = self.off_b + c0 * self.isz
        b1 = self.off_b + c1 * self.isz
        return [("S", x) for x in range(b0 // KEYBLK, (b1 - 1) // KEYBLK + 1)]


class CBuf:

    def __init__(self, bld, name, dtype, nch, cols_per, hi=True):
        self.name = name
        self.cols_per = cols_per
        self.chunks = [Buf(bld, f"{name}.{i}", dtype, cols_per, hi=hi)
                       for i in range(nch)]

    def ap(self, c0=0, c1=None, p0=0, p1=128):
        ch = c0 // self.cols_per
        assert c1 is not None and c1 <= (ch + 1) * self.cols_per, (c0, c1)
        base = ch * self.cols_per
        return self.chunks[ch].ap(c0 - base, c1 - base, p0, p1)

    def ap3(self, c0, dims, p0=0, p1=128):
        ch = c0 // self.cols_per
        base = ch * self.cols_per
        ext = c0 - base + sum(s_ * (n_ - 1) for s_, n_ in dims) + 1
        assert ext <= self.cols_per, (c0, dims)
        return self.chunks[ch].ap3(c0 - base, dims, p0, p1)

    def k(self, c0=0, c1=None):
        if c1 is None:
            c1 = len(self.chunks) * self.cols_per
        out = []
        ch0 = c0 // self.cols_per
        ch1 = (c1 - 1) // self.cols_per
        for ch in range(ch0, ch1 + 1):
            base = ch * self.cols_per
            a = max(c0, base) - base
            bcl = min(c1, base + self.cols_per) - base
            out += self.chunks[ch].k(a, bcl)
        return out

    def release_all(self, arena):
        for c in self.chunks:
            arena.release(c.name)


class Builder:
    def __init__(self, nc, sched, arena_t, arena_cols, psums, dram):
        self.nc = nc
        self.s = sched
        self.arena = Arena(arena_cols * 2)
        self._hb = arena_t.ap()
        self._hf = arena_t.ap().bitcast(F32)
        self._hu = arena_t.ap().bitcast(mybir.dt.uint32)
        self.psf_t = [p.ap().bitcast(F32) for p in psums]
        self.psb_t = [p.ap() for p in psums]
        self.dram = dram
        self.bufs = {}
        self.dbg_keys = []

    def handle(self, dtype):
        ap = (self._hb if dtype == BF16 else
              self._hu if dtype == mybir.dt.uint32 else self._hf)
        return ap.tensor, ap.ap[0][0]

    LOW = {"RES", "Q", "K", "V", "OFMB", "X2B", "EYEF", "EYEB", "MASK",
           "GAIN", "QG", "KG", "IO8", "IND2", "ROW2", "RB8", "ONESC", "RBROW"}

    def _is_low(self, name):
        return name.split(".")[0] in self.LOW

    def buf(self, name, dtype, cols):
        bb = Buf(self, name, dtype, cols, hi=not self._is_low(name))
        self.bufs[name] = bb
        return bb

    def cbuf(self, name, dtype, nch, cols_per):
        bb = CBuf(self, name, dtype, nch, cols_per, hi=not self._is_low(name))
        self.bufs[name] = bb
        return bb

    def freebuf(self, *names):
        for nm in names:
            bb = self.bufs.pop(nm)
            if isinstance(bb, CBuf):
                bb.release_all(self.arena)
            else:
                self.arena.release(bb.name)

    def pf(self, i, c0=0, c1=512, p0=0, p1=128):
        ap = self.psf_t[i]
        return bass.AP(ap.tensor, p0 * 512 + c0, [[512, p1 - p0], [1, c1 - c0]])

    def pb(self, i, c0=0, c1=1024, p0=0, p1=128):
        ap = self.psb_t[i]
        return bass.AP(ap.tensor, p0 * 1024 + c0, [[1024, p1 - p0], [1, c1 - c0]])

    def pk(self, i):
        return [("P", i)]

    def dma(self, q, dst_ap, src_ap, r=(), w=(), transpose=False):
        eng = "POOL" if q in ("Pq", "POq") else "SP"
        self.s.add(eng, lambda e, d_=dst_ap, s_=src_ap, t_=transpose:
                   e.dma_start(d_, s_, transpose=t_), r=r, w=w, dma=q)

    def idma(self, dst_ap, dst_off, src_ap, src_off, r=(), w=()):
        from bass_rust import IndirectOffsetOnAxis
        oo = IndirectOffsetOnAxis(ap=dst_off, axis=0) if dst_off is not None else None
        io = IndirectOffsetOnAxis(ap=src_off, axis=0) if src_off is not None else None
        self.s.add("POOL", lambda e, d_=dst_ap, s_=src_ap, o_=oo, i_=io:
                   e.indirect_dma_start(d_, o_, s_, i_), r=r, w=w, dma="Pq")

    def mm(self, bank, lhsT, rhs, start, stop, r=(), out=None):
        o = out if out is not None else self.pf(bank)
        self.s.add("PE", lambda e, o_=o, l_=lhsT, r_=rhs, st=start, sp=stop:
                   e.matmul(o_, l_, r_, start=st, stop=sp),
                   r=list(r) + self.pk(bank), w=self.pk(bank))

    def transp(self, bank, in_ap, ident_ap, out_ap, r=()):
        self.s.add("PE", lambda e, o_=out_ap, i_=in_ap, d_=ident_ap:
                   e.transpose(o_, i_, d_), r=list(r) + self.pk(bank),
                   w=self.pk(bank))

    def op(self, ch, fn, *args, r=(), w=(), **kw):
        self.s.add(ch, lambda e, a=args, k=kw: getattr(e, fn)(*a, **k), r=r, w=w)


def build_nc():
    nc = bass.Bass("TRN2", target_bir_lowering=False)
    dram = {}

    def din(name, shape):
        dram[name] = nc.dram_tensor(name, list(shape), F32, kind="ExternalInput").ap()

    din("x0", (B_PER_CORE * T, D))
    din("wq_a", (128, 8 * D)); din("wk_a", (128, 8 * D)); din("wv_a", (128, 8 * D))
    din("wo_a", (128, 8 * D))
    din("c_qg", (128, 8)); din("c_kg", (128, 8))
    din("wq_f", (128, 8 * D)); din("wkv_f", (128, 8 * L))
    din("wk_up", (128, 2 * D)); din("wv_up", (128, 2 * D))
    din("wo_f", (128, 8 * D)); din("wr", (128, 8 * E)); din("c_rb", (1, E))
    din("sw1", (2, 128, 8 * 1024)); din("sw3", (2, 128, 8 * 1024))
    din("sw2", (2, 128, 8 * 1024))
    din("ew1", (E, 2, 128, 8 * 1024)); din("ew3", (E, 2, 128, 8 * 1024))
    din("ew2", (E, 2, 128, 8 * 1024))
    din("c_eye", (128, 128)); din("c_mask", (128, 128))
    din("c_cos", (128, 512)); din("c_sin", (128, 512))
    din("c_gain", (128, 24)); din("c_io8", (128, 8))
    din("c_ind2", (128, 2)); din("c_row2", (2, 256)); din("c_exp9", (16, 9 * 128))
    din("c_i128", (1, T)); din("c_r16", (16, 32))
    dram["out"] = nc.dram_tensor("out", [B_PER_CORE * T, D], F32,
                                 kind="ExternalOutput").ap()
    dram["RG"] = nc.dram_tensor("RG", [8 * 1024 + 128, D], BF16).ap()
    dram["OGT"] = nc.dram_tensor("OGT", [8 * 1024 + 128, D], BF16).ap()
    for nm in DBG:
        dram["dbg_" + nm] = nc.dram_tensor(
            "dbg_" + nm, [B_PER_CORE * NT * 128, T], F32, kind="ExternalOutput").ap()

    ARENA_COLS = 106352
    with (
        nc.sbuf_tensor("arena", [128, ARENA_COLS], BF16) as arena_t,
        nc.psum_tensor("ps0", [128, 1024], BF16) as ps0,
        nc.psum_tensor("ps1", [128, 1024], BF16) as ps1,
        nc.psum_tensor("ps2", [128, 1024], BF16) as ps2,
        nc.psum_tensor("ps3", [128, 1024], BF16) as ps3,
        nc.psum_tensor("ps4", [128, 1024], BF16) as ps4,
        nc.psum_tensor("ps5", [128, 1024], BF16) as ps5,
        nc.psum_tensor("ps6", [128, 1024], BF16) as ps6,
        nc.psum_tensor("ps7", [128, 1024], BF16) as ps7,
        nc.semaphore("sPE") as sPE,
        nc.semaphore("sDVE") as sDVE,
        nc.semaphore("sACT") as sACT,
        nc.semaphore("sPOOL") as sPOOL,
        nc.semaphore("sSP") as sSP,
        nc.Block() as block,
    ):
        sched = Sched()
        bld = Builder(nc, sched, arena_t, ARENA_COLS,
                      [ps0, ps1, ps2, ps3, ps4, ps5, ps6, ps7], dram)
        build_program(bld)
        per_ch = sched.finalize()
        sems = {"PE": sPE, "DVE": sDVE, "ACT": sACT, "POOL": sPOOL, "SP": sSP}
        import contextlib
        _stack = contextlib.ExitStack()
        for q, nsub in [("Pq", 36), ("Sq", 36), ("POq", 1), ("SOq", 1)]:
            for si in range(nsub):
                sems[f"{q}{si}"] = _stack.enter_context(
                    nc.semaphore(f"s{q}{si}"))
        for sem, tot in sched.sem_totals.items():
            assert tot < 60000, f"sem {sem} overflow risk: {tot}"

        def emit_ch(section, ch):
            items = per_ch.get(ch, [])

            @section
            def _(eng):
                for it in items:
                    for (sm, v) in it["waits"]:
                        eng.wait_ge(sems[sm], v)
                    inst = it["emit"](eng)
                    if it["inc"]:
                        inst.then_inc(sems[it["inc"][0]], it["inc"][1])

        emit_ch(block.tensor, "PE")
        emit_ch(block.vector, "DVE")
        emit_ch(block.scalar, "ACT")
        emit_ch(block.gpsimd, "POOL")
        emit_ch(block.sync, "SP")
    return nc


def build_program(b):
    d = b.dram
    EYEF = b.buf("EYEF", F32, 128)
    EYEB = b.buf("EYEB", BF16, 128)
    MASK = b.buf("MASK", F32, 128)
    GAIN = b.buf("GAIN", F32, 24)
    QG = b.buf("QG", F32, 8)
    KG = b.buf("KG", F32, 8)
    IO8 = b.buf("IO8", F32, 8)
    IND2 = b.buf("IND2", F32, 2)
    ROW2 = b.buf("ROW2", F32, 256)
    RB8 = b.buf("RB8", F32, 8)
    ONESC = b.buf("ONESC", F32, 16)
    RBROW = b.buf("RBROW", F32, 8)
    EPSC = b.buf("EPSC", F32, 4)

    b.dma("Sq", EYEF.ap(), d["c_eye"], w=EYEF.k())
    b.dma("Sq", MASK.ap(), d["c_mask"], w=MASK.k())
    b.dma("Sq", GAIN.ap(), d["c_gain"], w=GAIN.k())
    b.dma("Sq", QG.ap(), d["c_qg"], w=QG.k())
    b.dma("Sq", KG.ap(), d["c_kg"], w=KG.k())
    b.dma("Sq", IO8.ap(), d["c_io8"], w=IO8.k())
    b.dma("Sq", IND2.ap(), d["c_ind2"], w=IND2.k())
    b.dma("Sq", ROW2.ap(0, 256, 0, 2), d["c_row2"], w=ROW2.k())
    b.dma("Sq", RBROW.ap(0, 8, 0, 1), d["c_rb"], w=RBROW.k())
    b.op("DVE", "tensor_copy", EYEB.ap(), EYEF.ap(), r=EYEF.k(), w=EYEB.k())
    b.op("DVE", "memset", ONESC.ap(0, 1), 1.0, w=ONESC.k())
    b.op("DVE", "memset", EPSC.ap(0, 1), EPS_A, w=EPSC.k())
    b.op("DVE", "memset", EPSC.ap(1, 2), EPS_F, w=EPSC.k())
    b.op("DVE", "memset", EPSC.ap(2, 3), EPS_LN, w=EPSC.k())
    b.op("DVE", "tensor_scalar", QG.ap(), QG.ap(), 1.0, 8.0, ALU.add, ALU.mult,
         r=QG.k(), w=QG.k())
    b.op("DVE", "tensor_scalar", KG.ap(), KG.ap(), 1.0, 8.0, ALU.add, ALU.mult,
         r=KG.k(), w=KG.k())
    b.mm(7, ROW2.ap(128, 256, 0, 1), RBROW.ap(0, 8, 0, 1), True, True,
         r=ROW2.k() + RBROW.k(), out=b.pf(7, 0, 8))
    b.op("DVE", "tensor_copy", RB8.ap(), b.pf(7, 0, 8), r=RB8.k(), w=RB8.k() + b.pk(7))

    cst = dict(EYEF=EYEF, EYEB=EYEB, MASK=MASK, GAIN=GAIN,
               QG=QG, KG=KG, IO8=IO8, IND2=IND2, ROW2=ROW2, RB8=RB8,
               ONESC=ONESC, EPSC=EPSC)

    for bi in range(B_PER_CORE):
        if "A" in STAGES:
            stage_a(b, cst, bi)
        if "B" in STAGES:
            stage_b(b, cst, bi)
        if "C" in STAGES:
            stage_c(b, cst, bi)
        if "F" in STAGES:
            stage_f(b, cst, bi)
    outkeys = [("OUT", bb, tt) for bb in range(B_PER_CORE) for tt in range(NT)]
    outkeys += b.dbg_keys
    b.s.add("SP", lambda e: e.nop(), r=outkeys)


def rinv(b, cst, ap, src_ap, wk, rk):
    b.op("ACT", "activation", ap, src_ap, AF.Ln, r=list(rk) + list(wk), w=wk)
    b.op("ACT", "activation", ap, ap, AF.Exp, scale=-1.0, r=wk, w=wk)


def rsqrt_rows(b, cst, dst_ap, src_ap, epsi, scale, wk, rk):
    b.op("ACT", "activation", dst_ap, src_ap, AF.Ln,
         bias=cst["EPSC"].ap(epsi, epsi + 1, 0, 1), scale=scale,
         r=list(rk) + cst["EPSC"].k() + list(wk), w=wk)
    b.op("ACT", "activation", dst_ap, dst_ap, AF.Exp, scale=-0.5, r=wk, w=wk)

def rmsnorm_rows(b, cst, SRC, SQ, ROWR, bank, epsi):
    for tc in range(4):
        c0 = tc * 512
        for ch in range(NCH):
            b.op("ACT", "activation", SQ.ap(0, 512),
                 SRC.ap(ch * T + c0, ch * T + c0 + 512), AF.Square,
                 r=SRC.k(ch * T + c0, ch * T + c0 + 512), w=SQ.k())
            b.mm(bank, cst["ONESC"].ap(0, 1), SQ.ap(0, 512), ch == 0, ch == NCH - 1,
                 r=cst["ONESC"].k() + SQ.k(), out=b.pf(bank, 0, 512, 0, 1))
        b.op("ACT", "activation", ROWR.ap(c0, c0 + 512, 0, 1),
             b.pf(bank, 0, 512, 0, 1), AF.Ln,
             bias=cst["EPSC"].ap(epsi, epsi + 1, 0, 1), scale=1.0 / D,
             r=cst["EPSC"].k() + b.pk(bank),
             w=ROWR.k(c0, c0 + 512) + b.pk(bank))
        b.op("ACT", "activation", ROWR.ap(c0, c0 + 512, 0, 1),
             ROWR.ap(c0, c0 + 512, 0, 1), AF.Exp, scale=-0.5,
             r=ROWR.k(c0, c0 + 512), w=ROWR.k(c0, c0 + 512))


def bcast_row(b, cst, ROWR, DSTB, bank):
    for tc in range(4):
        c0 = tc * 512
        b.mm(bank, cst["ROW2"].ap(128, 256, 0, 1), ROWR.ap(c0, c0 + 512, 0, 1),
             True, True, r=cst["ROW2"].k() + ROWR.k(c0, c0 + 512))
        b.op("DVE", "tensor_copy", DSTB.ap(c0, c0 + 512), b.pf(bank, 0, 512),
             r=b.pk(bank), w=DSTB.k(c0, c0 + 512) + b.pk(bank))


def layernorm_tm(b, cst, ZTM, SQA, COLS):
    b.op("DVE", "tensor_reduce", COLS.ap(0, 1), ZTM.ap(), axis=AX.X, op=ALU.add,
         r=ZTM.k(), w=COLS.k(0, 1))
    b.op("DVE", "tensor_scalar", COLS.ap(0, 1), COLS.ap(0, 1), 1.0 / D, None,
         ALU.mult, r=COLS.k(0, 1), w=COLS.k(0, 1))
    b.op("ACT", "activation", SQA.ap(), ZTM.ap(), AF.Square,
         r=ZTM.k(), w=SQA.k())
    b.op("DVE", "tensor_reduce", COLS.ap(1, 2), SQA.ap(), axis=AX.X, op=ALU.add,
         r=SQA.k(), w=COLS.k(1, 2))
    b.op("DVE", "tensor_scalar", COLS.ap(1, 2), COLS.ap(1, 2), 1.0 / D, None,
         ALU.mult, r=COLS.k(1, 2), w=COLS.k(1, 2))
    b.op("DVE", "tensor_tensor", COLS.ap(2, 3), COLS.ap(0, 1), COLS.ap(0, 1),
         ALU.mult, r=COLS.k(0, 1), w=COLS.k(2, 3))
    b.op("DVE", "tensor_tensor", COLS.ap(2, 3), COLS.ap(1, 2), COLS.ap(2, 3),
         ALU.subtract, r=COLS.k(1, 2) + COLS.k(2, 3), w=COLS.k(2, 3))
    b.op("ACT", "activation", COLS.ap(2, 3), COLS.ap(2, 3), AF.Ln,
         bias=cst["EPSC"].ap(2, 3), r=COLS.k(2, 3) + cst["EPSC"].k(),
         w=COLS.k(2, 3))
    b.op("ACT", "activation", COLS.ap(2, 3), COLS.ap(2, 3), AF.Exp, scale=-0.5,
         r=COLS.k(2, 3), w=COLS.k(2, 3))
    b.op("DVE", "tensor_scalar", ZTM.ap(), ZTM.ap(), COLS.ap(0, 1), COLS.ap(2, 3),
         ALU.subtract, ALU.mult,
         r=ZTM.k() + COLS.k(0, 1) + COLS.k(2, 3), w=ZTM.k())


def fm_transpose_in(b, cst, SRCTM, sc0, DSTFM, tt):
    for g in range(2):
        bank = 6 + g
        for c in range(4):
            ch = g * 4 + c
            b.transp(bank, SRCTM.ap(sc0 + ch * 128, sc0 + (ch + 1) * 128),
                     cst["EYEF"].ap(),
                     b.pf(bank, c * 128, (c + 1) * 128),
                     r=SRCTM.k(sc0, sc0 + D) + cst["EYEF"].k())
        for c in range(4):
            ch = g * 4 + c
            c0 = ch * T + tt * 128
            b.op("DVE", "tensor_copy", DSTFM.ap(c0, c0 + 128),
                 b.pf(bank, c * 128, (c + 1) * 128), r=b.pk(bank),
                 w=DSTFM.k(c0, c0 + 128) + b.pk(bank))


def dump_bf(b, nm, bi, SRC, ncols=T, nch=NCH):
    if nm not in DBG:
        return
    dst = b.dram["dbg_" + nm]
    r0 = bi * NT * 128
    for ch in range(nch):
        key = ("DBG", nm, bi, ch)
        b.dbg_keys.append(key)
        step = min(ncols, 512)
        for c0 in range(0, ncols, step):
            b.dma("POq", dst[r0 + ch * 128: r0 + (ch + 1) * 128, c0:c0 + step],
                  SRC.ap(ch * ncols + c0, ch * ncols + c0 + step),
                  r=SRC.k(ch * ncols + c0, ch * ncols + c0 + step), w=[key])


def dump_fm(b, nm, bi, SRC):
    if nm not in DBG:
        return
    dst = b.dram["dbg_" + nm]
    r0 = bi * NT * 128
    for ch in range(NCH):
        key = ("DBG", nm, bi, ch)
        b.dbg_keys.append(key)
        b.dma("SOq", dst[r0 + ch * 128: r0 + (ch + 1) * 128, :],
              SRC.ap(ch * T, (ch + 1) * T), r=SRC.k(ch * T, (ch + 1) * T),
              w=[key])


def stage_a(b, cst, bi):
    d = b.dram
    t_off = bi * T

    XTM = b.buf("XTM", F32, D)
    SQA = b.buf("SQA", BF16, D)
    RCOL = b.buf("RCOL", F32, 16)
    RROW1 = b.buf("RROW1", F32, 128)
    RSTDT = b.buf("RSTDT", F32, 128)
    RN = b.cbuf("RN", BF16, NCH, T)
    for tt in range(NT):
        b.dma("Sq", XTM.ap(), d["x0"][t_off + tt * 128: t_off + (tt + 1) * 128, :],
              w=XTM.k())
        b.op("ACT", "activation", SQA.ap(), XTM.ap(), AF.Square,
             r=XTM.k(), w=SQA.k())
        b.op("DVE", "tensor_reduce", RCOL.ap(0, 1), SQA.ap(), axis=AX.X,
             op=ALU.add, r=SQA.k(), w=RCOL.k())
        b.op("ACT", "activation", RCOL.ap(0, 1), RCOL.ap(0, 1), AF.Ln,
             bias=cst["EPSC"].ap(0, 1), scale=1.0 / D,
             r=RCOL.k() + cst["EPSC"].k(), w=RCOL.k())
        b.op("ACT", "activation", RCOL.ap(0, 1), RCOL.ap(0, 1), AF.Exp,
             scale=-0.5, r=RCOL.k(), w=RCOL.k())
        b.transp(2, RCOL.ap(0, 1), cst["EYEF"].ap(), b.pf(2, 0, 128, 0, 1),
                 r=RCOL.k() + cst["EYEF"].k())
        b.op("DVE", "tensor_copy", RROW1.ap(0, 128, 0, 1), b.pf(2, 0, 128, 0, 1),
             r=b.pk(2), w=RROW1.k() + b.pk(2))
        b.mm(3, cst["ROW2"].ap(128, 256, 0, 1), RROW1.ap(0, 128, 0, 1), True, True,
             r=cst["ROW2"].k() + RROW1.k(), out=b.pf(3, 0, 128))
        b.op("DVE", "tensor_copy", RSTDT.ap(), b.pf(3, 0, 128),
             r=b.pk(3), w=RSTDT.k() + b.pk(3))
        for g in range(2):
            bank = 6 + g
            for c in range(4):
                ch = g * 4 + c
                b.transp(bank, XTM.ap(ch * 128, (ch + 1) * 128), cst["EYEF"].ap(),
                         b.pf(bank, c * 128, (c + 1) * 128),
                         r=XTM.k() + cst["EYEF"].k())
            for c in range(4):
                ch = g * 4 + c
                c0 = ch * T + tt * 128
                b.op("DVE", "scalar_tensor_tensor", RN.ap(c0, c0 + 128),
                     b.pf(bank, c * 128, (c + 1) * 128),
                     cst["GAIN"].ap(ch, ch + 1), RSTDT.ap(),
                     op0=ALU.mult, op1=ALU.mult,
                     r=cst["GAIN"].k() + RSTDT.k() + b.pk(bank),
                     w=RN.k(c0, c0 + 128) + b.pk(bank))
    b.freebuf("SQA", "RCOL", "RROW1", "RSTDT", "XTM")
    dump_bf(b, "rn", bi, RN)

    WS0 = b.buf("WSA0", BF16, 8 * D)
    WS1 = b.buf("WSA1", BF16, 8 * D)
    Q = b.cbuf("Q", BF16, NCH, T)
    K = b.cbuf("K", BF16, NCH, T)
    V = b.cbuf("V", BF16, NT, H * 65)
    SQH = b.buf("SQH", F32, 512)
    RNB = b.buf("RNB", F32, 512)
    HROW = b.buf("HROW", F32, 512)
    b.dma("Pq", WS0.ap(), d["wq_a"], w=WS0.k())
    b.dma("Pq", WS1.ap(), d["wk_a"], w=WS1.k())

    def headnorm(DST, GCOL, fc, tc):
        c0 = fc * T + tc * 512
        dk = DST.k(c0, c0 + 512)
        b.op("ACT", "activation", SQH.ap(0, 512), DST.ap(c0, c0 + 512), AF.Square,
             r=dk, w=SQH.k())
        b.mm(2, cst["IND2"].ap(0, 2), SQH.ap(0, 512), True, True,
             r=cst["IND2"].k() + SQH.k(), out=b.pf(2, 0, 512, 0, 2))
        b.op("ACT", "activation", HROW.ap(0, 512, 0, 2), b.pf(2, 0, 512, 0, 2),
             AF.Ln, r=b.pk(2), w=HROW.k() + b.pk(2))
        b.op("ACT", "activation", HROW.ap(0, 512, 0, 2), HROW.ap(0, 512, 0, 2),
             AF.Exp, scale=-0.5, r=HROW.k(), w=HROW.k())
        b.op("DVE", "tensor_scalar", HROW.ap(0, 512, 0, 2), HROW.ap(0, 512, 0, 2),
             1e12, None, ALU.min, r=HROW.k(), w=HROW.k())
        b.mm(3, cst["ROW2"].ap(0, 128, 0, 2), HROW.ap(0, 512, 0, 2), True, True,
             r=cst["ROW2"].k() + HROW.k())
        b.op("DVE", "tensor_copy", RNB.ap(0, 512), b.pf(3, 0, 512), r=b.pk(3),
             w=RNB.k() + b.pk(3))
        b.op("DVE", "scalar_tensor_tensor", DST.ap(c0, c0 + 512),
             DST.ap(c0, c0 + 512), GCOL.ap(fc, fc + 1), RNB.ap(0, 512),
             op0=ALU.mult, op1=ALU.mult, r=dk + GCOL.k() + RNB.k(), w=dk)

    for which, (DST, GCOL, ws) in enumerate(
            [(Q, cst["QG"], WS0), (K, cst["KG"], WS1)]):
        wk = ws.k()
        for fc in range(NCH):
            for tc in range(4):
                bank = tc % 2
                for dc in range(NCH):
                    b.mm(bank, ws.ap(dc * D + fc * 128, dc * D + (fc + 1) * 128),
                         RN.ap(dc * T + tc * 512, dc * T + tc * 512 + 512),
                         dc == 0, dc == NCH - 1,
                         r=wk + RN.k(dc * T + tc * 512, dc * T + tc * 512 + 512))
                c0 = fc * T + tc * 512
                b.op("DVE", "tensor_copy", DST.ap(c0, c0 + 512), b.pf(bank, 0, 512),
                     r=b.pk(bank), w=DST.k(c0, c0 + 512) + b.pk(bank))
                headnorm(DST, GCOL, fc, tc)
        if which == 0:
            b.dma("Pq", WS0.ap(), d["wv_a"], w=WS0.k())

    for tt in range(NT):
        b.op("DVE", "memset", V.ap3(tt * H * 65 + 64, [(65, 16), (1, 1)]), 1.0,
             w=V.k(tt * H * 65, (tt + 1) * H * 65))
        for fcol in range(2):
            bank = fcol % 2
            for dc in range(NCH):
                b.mm(bank, RN.ap(dc * T + tt * 128, dc * T + (tt + 1) * 128),
                     WS0.ap(dc * D + fcol * 512, dc * D + fcol * 512 + 512),
                     dc == 0, dc == NCH - 1,
                     r=WS0.k() + RN.k(dc * T + tt * 128, dc * T + (tt + 1) * 128))
            dst = V.ap3(tt * H * 65 + fcol * 8 * 65, [(65, 8), (1, 64)])
            b.op("DVE", "tensor_copy", dst, b.pf(bank, 0, 512), r=b.pk(bank),
                 w=V.k(tt * H * 65, (tt + 1) * H * 65) + b.pk(bank))
    dump_bf(b, "qh", bi, Q)
    dump_bf(b, "kh", bi, K)
    dump_bf(b, "vv", bi, V, ncols=H * 65, nch=NT)
    b.freebuf("SQH", "RNB", "HROW", "RN", "WSA0")
    b.dma("Pq", WS1.ap(), d["wo_a"], w=WS1.k())

    RES = b.cbuf("RES", F32, NCH, T)
    ICW = 256
    PT = b.buf("PT", BF16, NT * ICW)
    OFM = b.buf("OFM", BF16, NCH * ICW)
    OSC = b.buf("OSC", F32, 256)
    RROW = b.buf("RROW", F32, 256)
    ZTM = b.buf("ZTM", F32, D)
    SQA2 = b.buf("SQA2", BF16, D)
    COLS = b.buf("COLS", F32, 8)
    XT2 = b.buf("XT2", F32, D)
    wo_k = WS1.k()

    TSA = b.cbuf("TSA", BF16, 4, 512)
    for ic in range(T // ICW):
        i0 = ic * ICW

        def norm_part(h):
            qc, prow = h // 2, (h % 2) * 64
            obank = 4 + (h % 2)
            b.op("DVE", "reciprocal", RROW.ap(0, ICW, 0, 1),
                 b.pf(obank, 0, ICW, 64, 65), r=b.pk(obank),
                 w=RROW.k() + b.pk(obank))
            b.mm(6, cst["ROW2"].ap(128, 192, 0, 1), RROW.ap(0, ICW, 0, 1),
                 True, True, r=cst["ROW2"].k() + RROW.k(),
                 out=b.pf(6, 0, ICW, 0, 64))
            b.op("DVE", "tensor_copy", OSC.ap(0, ICW, 0, 64),
                 b.pf(6, 0, ICW, 0, 64), r=b.pk(6), w=OSC.k() + b.pk(6))
            b.op("DVE", "tensor_tensor",
                 OFM.ap(qc * ICW, (qc + 1) * ICW, prow, prow + 64),
                 b.pf(obank, 0, ICW, 0, 64), OSC.ap(0, ICW, 0, 64), ALU.mult,
                 r=OSC.k() + b.pk(obank),
                 w=OFM.k(qc * ICW, (qc + 1) * ICW) + b.pk(obank))

        def av_wave(h, jts):
            obank = 4 + (h % 2)
            for jt in jts:
                b.mm(obank, V.ap(jt * H * 65 + h * 65, jt * H * 65 + h * 65 + 65),
                     PT.ap(jt * ICW, (jt + 1) * ICW), jt == 0, jt == NT - 1,
                     r=V.k(jt * H * 65 + h * 65, jt * H * 65 + (h + 1) * 65)
                       + PT.k(jt * ICW, (jt + 1) * ICW),
                     out=b.pf(obank, 0, ICW, 0, 65))

        def s_wave(h, prs):
            qc, prow = h // 2, (h % 2) * 64
            for p in prs:
                for half in range(2):
                    jt = 2 * p + half
                    sb = p % 4
                    b.mm(sb, K.ap(qc * T + jt * 128, qc * T + (jt + 1) * 128,
                                  prow, prow + 64),
                         Q.ap(qc * T + i0, qc * T + i0 + ICW, prow, prow + 64),
                         True, True,
                         r=K.k(qc * T + jt * 128, qc * T + (jt + 1) * 128)
                           + Q.k(qc * T + i0, qc * T + i0 + ICW),
                         out=b.pf(sb, half * ICW, (half + 1) * ICW))

        def act_wave(prs):
            for p in prs:
                sb = p % 4
                b.op("ACT", "activation", TSA.ap((p % 4) * 512, (p % 4 + 1) * 512),
                     b.pf(sb, 0, 2 * ICW), AF.Tanh, scale=1.0 / 50.0,
                     r=b.pk(sb), w=TSA.k((p % 4) * 512, (p % 4 + 1) * 512) + b.pk(sb))
            for p in prs:
                b.op("ACT", "activation", PT.ap(2 * p * ICW, 2 * (p + 1) * ICW),
                     TSA.ap((p % 4) * 512, (p % 4 + 1) * 512), AF.Exp, scale=6.25,
                     r=TSA.k((p % 4) * 512, (p % 4 + 1) * 512),
                     w=PT.k(2 * p * ICW, 2 * (p + 1) * ICW))

        for h in range(H):
            s_wave(h, range(0, 4))
            if h > 0:
                av_wave(h - 1, range(0, 8))
            act_wave(range(0, 4))
            s_wave(h, range(4, 8))
            if h > 0:
                av_wave(h - 1, range(8, 16))
            act_wave(range(4, 8))
            if h > 0:
                norm_part(h - 1)
        av_wave(H - 1, range(0, 16))
        norm_part(H - 1)
        for t2 in range(2):
            tt = ic * 2 + t2
            b.dma("Sq", XT2.ap(),
                  d["x0"][t_off + tt * 128: t_off + (tt + 1) * 128, :],
                  w=XT2.k())
            for dcol in range(2):
                bank = 2 + dcol
                for c in range(NCH):
                    b.mm(bank, OFM.ap(c * ICW + t2 * 128, c * ICW + t2 * 128 + 128),
                         WS1.ap(c * D + dcol * 512, c * D + dcol * 512 + 512),
                         c == 0, c == NCH - 1,
                         r=OFM.k(c * ICW, (c + 1) * ICW) + wo_k)
                b.op("DVE", "tensor_tensor", ZTM.ap(dcol * 512, dcol * 512 + 512),
                     b.pf(bank, 0, 512),
                     XT2.ap(dcol * 512, dcol * 512 + 512), ALU.add,
                     r=XT2.k() + b.pk(bank),
                     w=ZTM.k(dcol * 512, dcol * 512 + 512) + b.pk(bank))
            if "z0" in DBG:
                key = ("DBG", "z0", bi, tt)
                b.dbg_keys.append(key)
                b.dma("SOq", b.dram["dbg_z0"][bi * NT * 128 + tt * 128:
                                             bi * NT * 128 + (tt + 1) * 128, 0:D],
                      ZTM.ap(), r=ZTM.k(), w=[key])
            layernorm_tm(b, cst, ZTM, SQA2, COLS)
            fm_transpose_in(b, cst, ZTM, 0, RES, tt)
    b.freebuf("PT", "OFM", "OSC", "RROW", "ZTM", "SQA2", "COLS", "XT2", "TSA",
              "Q", "K", "V", "WSA1")
    dump_fm(b, "x1", bi, RES)


def stage_b(b, cst, bi):
    d = b.dram
    RES = b.bufs["RES"]

    COS = b.buf("COS", F32, 512)
    SIN = b.buf("SIN", F32, 512)
    b.dma("Sq", COS.ap(), d["c_cos"], w=COS.k())
    b.dma("Sq", SIN.ap(), d["c_sin"], w=SIN.k())
    cst = dict(cst, COS=COS, SIN=SIN)
    SQ = b.buf("SQB0", F32, 512)
    ROWR = b.buf("ROWRB", F32, T)
    RSTDB = b.buf("RSTDBB", F32, T)
    RN1 = b.cbuf("RN1", BF16, NCH, T)
    rmsnorm_rows(b, cst, RES, SQ, ROWR, 2, 1)
    bcast_row(b, cst, ROWR, RSTDB, 3)
    for ch in range(NCH):
        for tc in range(4):
            c0 = ch * T + tc * 512
            b.op("DVE", "scalar_tensor_tensor",
                 RN1.ap(c0, c0 + 512), RES.ap(c0, c0 + 512),
                 cst["GAIN"].ap(8 + ch, 9 + ch), RSTDB.ap(tc * 512, tc * 512 + 512),
                 op0=ALU.mult, op1=ALU.mult,
                 r=RES.k(c0, c0 + 512) + cst["GAIN"].k()
                   + RSTDB.k(tc * 512, tc * 512 + 512),
                 w=RN1.k(c0, c0 + 512))
    b.freebuf("SQB0", "ROWRB", "RSTDBB")

    WKV = b.buf("WKV", BF16, 8 * L)
    KV = b.buf("KV", BF16, 2 * T)
    b.dma("Pq", WKV.ap(), d["wkv_f"], w=WKV.k())
    for lc in range(2):
        for tc in range(4):
            bank = tc % 2
            for dc in range(NCH):
                b.mm(bank, WKV.ap(dc * L + lc * 128, dc * L + (lc + 1) * 128),
                     RN1.ap(dc * T + tc * 512, dc * T + tc * 512 + 512),
                     dc == 0, dc == NCH - 1,
                     r=WKV.k() + RN1.k(dc * T + tc * 512, dc * T + tc * 512 + 512))
            c0 = lc * T + tc * 512
            b.op("DVE", "tensor_scalar", KV.ap(c0, c0 + 512), b.pf(bank, 0, 512),
                 0.25, None, ALU.mult, r=b.pk(bank), w=KV.k(c0, c0 + 512) + b.pk(bank))
    b.freebuf("WKV")

    WQF = b.buf("WQF", BF16, 8 * D)
    WKU = b.buf("WKU", BF16, 2 * D)
    WVU = b.buf("WVU", BF16, 2 * D)
    Q = b.cbuf("Q", BF16, NCH, T)
    QST = b.buf("QST", BF16, 2 * D)
    TMP = b.buf("TMPR", F32, 4 * 512)
    b.dma("Pq", WQF.ap(), d["wq_f"], w=WQF.k())
    b.dma("Pq", WKU.ap(), d["wk_up"], w=WKU.k())
    b.dma("Pq", WVU.ap(), d["wv_up"], w=WVU.k())

    def rope_tile(tt, sl):
        base = sl * D
        kst = QST.k(base, base + D)
        x1v = QST.ap3(base, [(64, 16), (1, 32)])
        x2v = QST.ap3(base + 32, [(64, 16), (1, 32)])
        cosb = cst["COS"].ap3(tt * 32, [(0, 16), (1, 32)])
        sinb = cst["SIN"].ap3(tt * 32, [(0, 16), (1, 32)])
        tvs = [TMP.ap3(i * 512, [(32, 16), (1, 32)]) for i in range(4)]
        tks = [TMP.k(i * 512, (i + 1) * 512) for i in range(4)]
        b.op("DVE", "tensor_tensor", tvs[0], x1v, cosb, ALU.mult,
             r=kst + cst["COS"].k(), w=tks[0])
        b.op("DVE", "tensor_tensor", tvs[1], x2v, sinb, ALU.mult,
             r=kst + cst["SIN"].k(), w=tks[1])
        b.op("DVE", "tensor_tensor", tvs[2], x1v, sinb, ALU.mult,
             r=kst + cst["SIN"].k(), w=tks[2])
        b.op("DVE", "tensor_tensor", tvs[3], x2v, cosb, ALU.mult,
             r=kst + cst["COS"].k(), w=tks[3])
        b.op("DVE", "tensor_tensor", x1v, tvs[0], tvs[1], ALU.subtract,
             r=tks[0] + tks[1], w=kst)
        b.op("DVE", "tensor_tensor", x2v, tvs[2], tvs[3], ALU.add,
             r=tks[2] + tks[3], w=kst)

    def fm_transpose_bf16(SRC, sc0, DSTFM, tt):
        for g in range(2):
            bank = 6 + g
            for c in range(4):
                ch = g * 4 + c
                b.transp(bank, SRC.ap(sc0 + ch * 128, sc0 + (ch + 1) * 128),
                         cst["EYEB"].ap(), b.pb(bank, c * 128, (c + 1) * 128),
                         r=SRC.k(sc0, sc0 + D) + cst["EYEB"].k())
            for c in range(4):
                ch = g * 4 + c
                c0 = ch * T + tt * 128
                b.op("DVE", "tensor_copy", DSTFM.ap(c0, c0 + 128),
                     b.pb(bank, c * 128, (c + 1) * 128), r=b.pk(bank),
                     w=DSTFM.k(c0, c0 + 128) + b.pk(bank))

    for tt in range(NT):
        sl = tt % 2
        for fcol in range(2):
            bank = fcol
            for dc in range(NCH):
                b.mm(bank, RN1.ap(dc * T + tt * 128, dc * T + (tt + 1) * 128),
                     WQF.ap(dc * D + fcol * 512, dc * D + fcol * 512 + 512),
                     dc == 0, dc == NCH - 1,
                     r=WQF.k() + RN1.k(dc * T + tt * 128, dc * T + (tt + 1) * 128))
            b.op("DVE", "tensor_copy",
                 QST.ap(sl * D + fcol * 512, sl * D + fcol * 512 + 512),
                 b.pf(bank, 0, 512), r=b.pk(bank),
                 w=QST.k(sl * D + fcol * 512, sl * D + fcol * 512 + 512) + b.pk(bank))
        rope_tile(tt, sl)
        fm_transpose_bf16(QST, sl * D, Q, tt)
    b.freebuf("RN1", "WQF")
    K = b.cbuf("K", BF16, NCH, T)
    for tt in range(NT):
        sl = tt % 2
        for fcol in range(2):
            bank = fcol
            for lc in range(2):
                b.mm(bank, KV.ap(lc * T + tt * 128, lc * T + (tt + 1) * 128),
                     WKU.ap(lc * D + fcol * 512, lc * D + fcol * 512 + 512),
                     lc == 0, lc == 1,
                     r=WKU.k() + KV.k(lc * T + tt * 128, lc * T + (tt + 1) * 128))
            b.op("DVE", "tensor_copy",
                 QST.ap(sl * D + fcol * 512, sl * D + fcol * 512 + 512),
                 b.pf(bank, 0, 512), r=b.pk(bank),
                 w=QST.k(sl * D + fcol * 512, sl * D + fcol * 512 + 512) + b.pk(bank))
        rope_tile(tt, sl)
        fm_transpose_bf16(QST, sl * D, K, tt)
    b.freebuf("WKU")
    V = b.cbuf("V", BF16, NT, H * 65)
    for tt in range(NT):
        b.op("DVE", "memset", V.ap3(tt * H * 65 + 64, [(65, 16), (1, 1)]), 1.0,
             w=V.k(tt * H * 65, (tt + 1) * H * 65))
        for fcol in range(2):
            bank = fcol
            for lc in range(2):
                b.mm(bank, KV.ap(lc * T + tt * 128, lc * T + (tt + 1) * 128),
                     WVU.ap(lc * D + fcol * 512, lc * D + fcol * 512 + 512),
                     lc == 0, lc == 1,
                     r=WVU.k() + KV.k(lc * T + tt * 128, lc * T + (tt + 1) * 128))
            dst = V.ap3(tt * H * 65 + fcol * 8 * 65, [(65, 8), (1, 64)])
            b.op("DVE", "tensor_copy", dst, b.pf(bank, 0, 512), r=b.pk(bank),
                 w=V.k(tt * H * 65, (tt + 1) * H * 65) + b.pk(bank))
    b.freebuf("QST", "TMPR", "WVU", "KV", "COS", "SIN")

    P = b.buf("P", BF16, T)
    PT = b.buf("PTB", BF16, T)
    OFMB = b.cbuf("OFMB", BF16, NCH * 4, 512)
    RROW = b.buf("RROWB", F32, 128)
    OSC = b.buf("OSCB", F32, 128)
    CB = b.buf("CB", F32, 8)

    RROW = b.bufs["RROWB"]
    OSC = b.bufs["OSCB"]

    def stg_s(it, h):
        qc, prow = h // 2, (h % 2) * 64
        jmax = (it + 1) * 128
        njc = (jmax + 511) // 512
        for jc in range(njc):
            jw = min(512, jmax - jc * 512)
            b.mm(jc, Q.ap(qc * T + it * 128, qc * T + (it + 1) * 128,
                          prow, prow + 64),
                 K.ap(qc * T + jc * 512, qc * T + jc * 512 + jw, prow, prow + 64),
                 True, True,
                 r=Q.k(qc * T + it * 128, qc * T + (it + 1) * 128)
                   + K.k(qc * T + jc * 512, qc * T + jc * 512 + jw),
                 out=b.pf(jc, 0, jw))
        jcd = (it * 128) // 512
        coff = it * 128 - jcd * 512
        for jc in range(njc):
            if jc == jcd:
                continue
            jw = min(512, jmax - jc * 512)
            b.op("DVE", "tensor_reduce", CB.ap(jc, jc + 1), b.pf(jc, 0, jw),
                 axis=AX.X, op=ALU.max, r=b.pk(jc), w=CB.k() + b.pk(jc))
        b.op("DVE", "tensor_tensor", b.pf(jcd, coff, coff + 128),
             b.pf(jcd, coff, coff + 128), cst["MASK"].ap(), ALU.add,
             r=cst["MASK"].k() + b.pk(jcd), w=b.pk(jcd))
        jw = min(512, jmax - jcd * 512)
        b.op("DVE", "tensor_reduce", CB.ap(jcd, jcd + 1), b.pf(jcd, 0, jw),
             axis=AX.X, op=ALU.max, r=b.pk(jcd), w=CB.k() + b.pk(jcd))
        for step in range(1, njc):
            b.op("DVE", "tensor_tensor", CB.ap(0, 1), CB.ap(0, 1),
                 CB.ap(step, step + 1), ALU.max, r=CB.k(), w=CB.k())
        b.op("DVE", "tensor_scalar", CB.ap(4, 5), CB.ap(0, 1), -0.125, None,
             ALU.mult, r=CB.k(), w=CB.k())

    def stg_exp(it, h):
        jmax = (it + 1) * 128
        njc = (jmax + 511) // 512
        for jc in range(njc):
            jw = min(512, jmax - jc * 512)
            b.op("ACT", "activation", P.ap(jc * 512, jc * 512 + jw),
                 b.pf(jc, 0, jw), AF.Exp, scale=0.125, bias=CB.ap(4, 5),
                 r=CB.k() + b.pk(jc), w=P.k(jc * 512, jc * 512 + jw) + b.pk(jc))

    def stg_transp(it, h):
        for g in range((it + 1 + 3) // 4):
            gw = min(4, it + 1 - g * 4)
            for c in range(gw):
                jt = g * 4 + c
                b.transp(6, P.ap(jt * 128, (jt + 1) * 128), cst["EYEB"].ap(),
                         b.pb(6, c * 128, (c + 1) * 128),
                         r=P.k(jt * 128, (jt + 1) * 128) + cst["EYEB"].k())
            b.op("DVE", "tensor_copy", PT.ap(g * 512, g * 512 + gw * 128),
                 b.pb(6, 0, gw * 128), r=b.pk(6),
                 w=PT.k(g * 512, g * 512 + gw * 128) + b.pk(6))

    def stg_av(it, h, ob):
        qc, prow = h // 2, (h % 2) * 64
        for jt in range(it + 1):
            b.mm(ob, V.ap(jt * H * 65 + h * 65, jt * H * 65 + h * 65 + 65),
                 PT.ap(jt * 128, (jt + 1) * 128), jt == 0, jt == it,
                 r=V.k(jt * H * 65, (jt + 1) * H * 65)
                   + PT.k(jt * 128, (jt + 1) * 128),
                 out=b.pf(ob, 0, 128, 0, 65))
        rinv(b, cst, RROW.ap(0, 128, 0, 1), b.pf(ob, 0, 128, 64, 65),
             RROW.k(), b.pk(ob))
        b.mm(4, cst["ROW2"].ap(128, 192, 0, 1), RROW.ap(0, 128, 0, 1),
             True, True, r=cst["ROW2"].k() + RROW.k(),
             out=b.pf(4, 0, 128, 0, 64))
        b.op("ACT", "copy", OSC.ap(0, 128, 0, 64), b.pf(4, 0, 128, 0, 64),
             r=b.pk(4), w=OSC.k() + b.pk(4))
        b.op("DVE", "tensor_tensor",
             OFMB.ap(qc * T + it * 128, qc * T + (it + 1) * 128, prow, prow + 64),
             b.pf(ob, 0, 128, 0, 64), OSC.ap(0, 128, 0, 64), ALU.mult,
             r=OSC.k() + b.pk(ob),
             w=OFMB.k(qc * T + it * 128, qc * T + (it + 1) * 128) + b.pk(ob))

    jobs = [(it, h) for it in range(NT) for h in range(H)]
    for k in range(len(jobs) + 3):
        if k >= 2 and k - 2 < len(jobs):
            stg_transp(*jobs[k - 2])
        if k >= 1 and k - 1 < len(jobs):
            stg_exp(*jobs[k - 1])
        if k >= 3:
            stg_av(*jobs[k - 3], 5 if (k - 3) % 2 == 0 else 7)
        if k < len(jobs):
            stg_s(*jobs[k])
    b.freebuf("P", "PTB", "RROWB", "OSCB", "CB", "Q", "K", "V")

    WOF = b.buf("WOF", BF16, 8 * D)
    X2B = b.cbuf("X2B", BF16, NCH * 4, 512)
    X2SC = b.buf("X2SC", F32, 512)
    SQB = b.buf("SQB", F32, 512)
    ROWR2 = b.buf("ROWR2", F32, T)
    b.dma("Pq", WOF.ap(), d["wo_f"], w=WOF.k())
    for grp in range(4):
        g0 = grp * 512
        for dch in range(NCH):
            bank = 2 + (dch % 2)
            for c in range(NCH):
                b.mm(bank, WOF.ap(c * D + dch * 128, c * D + (dch + 1) * 128),
                     OFMB.ap(c * T + g0, c * T + g0 + 512),
                     c == 0, c == NCH - 1,
                     r=WOF.k() + OFMB.k(c * T + g0, c * T + g0 + 512))
            c0 = dch * T + g0
            b.op("DVE", "tensor_tensor", X2SC.ap(0, 512), b.pf(bank, 0, 512),
                 RES.ap(c0, c0 + 512), ALU.add,
                 r=RES.k(c0, c0 + 512) + b.pk(bank), w=X2SC.k() + b.pk(bank))
            b.op("DVE", "tensor_copy", X2B.ap(c0, c0 + 512), X2SC.ap(0, 512),
                 r=X2SC.k(), w=X2B.k(c0, c0 + 512))
            b.op("ACT", "activation", SQB.ap(0, 512), X2SC.ap(0, 512), AF.Square,
                 r=X2SC.k(), w=SQB.k())
            b.mm(7, cst["ONESC"].ap(0, 1), SQB.ap(0, 512), dch == 0,
                 dch == NCH - 1, r=cst["ONESC"].k() + SQB.k(),
                 out=b.pf(7, 0, 512, 0, 1))
            b.op("DVE", "tensor_tensor", RES.ap(c0, c0 + 512), RES.ap(c0, c0 + 512),
                 X2SC.ap(0, 512), ALU.add,
                 r=RES.k(c0, c0 + 512) + X2SC.k(), w=RES.k(c0, c0 + 512))
        b.op("ACT", "activation", ROWR2.ap(g0, g0 + 512, 0, 1),
             b.pf(7, 0, 512, 0, 1), AF.Ln,
             bias=cst["EPSC"].ap(1, 2, 0, 1), scale=1.0 / D,
             r=cst["EPSC"].k() + b.pk(7), w=ROWR2.k(g0, g0 + 512) + b.pk(7))
        b.op("ACT", "activation", ROWR2.ap(g0, g0 + 512, 0, 1),
             ROWR2.ap(g0, g0 + 512, 0, 1), AF.Exp, scale=-0.5,
             r=ROWR2.k(g0, g0 + 512), w=ROWR2.k(g0, g0 + 512))
    b.freebuf("WOF", "OFMB", "X2SC", "SQB")
    RSTDB2 = b.buf("RSTDB2", F32, T)
    bcast_row(b, cst, ROWR2, RSTDB2, 3)
    for ch in range(NCH):
        for tc in range(4):
            c0 = ch * T + tc * 512
            b.op("DVE", "scalar_tensor_tensor", X2B.ap(c0, c0 + 512),
                 X2B.ap(c0, c0 + 512), cst["GAIN"].ap(16 + ch, 17 + ch),
                 RSTDB2.ap(tc * 512, tc * 512 + 512), op0=ALU.mult, op1=ALU.mult,
                 r=X2B.k(c0, c0 + 512) + cst["GAIN"].k()
                   + RSTDB2.k(tc * 512, tc * 512 + 512),
                 w=X2B.k(c0, c0 + 512))
    b.freebuf("ROWR2", "RSTDB2")
    dump_fm(b, "x2s", bi, RES)
    dump_bf(b, "rn2", bi, X2B)


def stage_c(b, cst, bi):
    d = b.dram
    RES = b.bufs["RES"]
    RN2 = b.bufs["X2B"]
    C = 1024
    ZBASE = 8 * C

    WRS = b.buf("WRS", BF16, 8 * E)
    I128 = b.buf("I128", F32, T)
    R16 = b.buf("R16", F32, 32)
    SROWS = b.buf("SROWS", F32, T)
    MR = b.buf("MR", F32, T)
    RNK = b.buf("RNK", F32, T)
    ROWS4 = b.buf("ROWS4", F32, 4 * T)
    ESC = b.buf("ESC", F32, 16)
    DCOL = b.buf("DCOL", mybir.dt.uint32, 16)
    PRB = b.buf("PRB", F32, 8)
    EQ = b.buf("EQ", F32, 8)
    CC = b.buf("CC", F32, 8)
    SMAT = b.buf("SMAT", F32, 16)
    SM2 = b.buf("SM2", F32, 16)
    b.dma("Pq", WRS.ap(), d["wr"], w=WRS.k())
    b.dma("Pq", I128.ap(0, T, 0, 1), d["c_i128"], w=I128.k())
    b.dma("Pq", R16.ap(0, 32, 0, 16), d["c_r16"], w=R16.k())
    PROBS = b.buf("PROBS", F32, 16 * E)
    PCOL = b.buf("PCOL", F32, 16)
    RTW = b.buf("RTW", F32, 16)
    b.op("DVE", "memset", SMAT.ap(), 0.0, w=SMAT.k())
    b.op("DVE", "memset", SM2.ap(), 0.0, w=SM2.k())
    for tt in range(NT):
        bank = 6 + tt % 2
        for dc in range(NCH):
            b.mm(bank, RN2.ap(dc * T + tt * 128, dc * T + (tt + 1) * 128),
                 WRS.ap(dc * E, (dc + 1) * E), dc == 0, dc == NCH - 1,
                 r=WRS.k() + RN2.k(dc * T + tt * 128, dc * T + (tt + 1) * 128),
                 out=b.pf(bank, 0, E))
        b.op("DVE", "tensor_tensor", PROBS.ap(tt * E, (tt + 1) * E),
             b.pf(bank, 0, E), cst["RB8"].ap(), ALU.add,
             r=cst["RB8"].k() + b.pk(bank),
             w=PROBS.k(tt * E, (tt + 1) * E) + b.pk(bank))
    b.op("DVE", "tensor_scalar", PROBS.ap(), PROBS.ap(), -50.0, 50.0,
         ALU.max, ALU.min, r=PROBS.k(), w=PROBS.k())
    b.op("ACT", "activation", PROBS.ap(), PROBS.ap(), AF.Sigmoid,
         r=PROBS.k(), w=PROBS.k())
    for tt in range(NT):
        b.op("DVE", "tensor_reduce", PCOL.ap(tt, tt + 1),
             PROBS.ap(tt * E, (tt + 1) * E), axis=AX.X, op=ALU.max,
             r=PROBS.k(), w=PCOL.k())
    b.op("DVE", "tensor_scalar", PCOL.ap(), PCOL.ap(), 1e-8, 1.0,
         ALU.max, ALU.min, r=PCOL.k(), w=PCOL.k())
    b.op("DVE", "tensor_scalar", RTW.ap(), PCOL.ap(), 0.5, 1e-8,
         ALU.add, ALU.add, r=PCOL.k(), w=RTW.k())
    rinv(b, cst, RTW.ap(), RTW.ap(), RTW.k(), RTW.k())
    b.op("DVE", "tensor_tensor", ESC.ap(), PCOL.ap(), RTW.ap(), ALU.mult,
         r=PCOL.k() + RTW.k(), w=ESC.k())
    for tt in range(NT):
        b.op("DVE", "tensor_scalar", EQ.ap(), PROBS.ap(tt * E, (tt + 1) * E),
             PCOL.ap(tt, tt + 1), None, ALU.is_equal,
             r=PROBS.k() + PCOL.k(), w=EQ.k())
        b.op("DVE", "tensor_scalar", EQ.ap(), EQ.ap(), -1e9, 1e9,
             ALU.mult, ALU.add, r=EQ.k(), w=EQ.k())
        b.op("DVE", "tensor_tensor", EQ.ap(), EQ.ap(), cst["IO8"].ap(), ALU.add,
             r=EQ.k() + cst["IO8"].k(), w=EQ.k())
        b.op("DVE", "tensor_reduce", CC.ap(1, 2), EQ.ap(), axis=AX.X, op=ALU.min,
             r=EQ.k(), w=CC.k())
        b.op("DVE", "tensor_scalar", SMAT.ap(8, 9), RTW.ap(tt, tt + 1), 0.5, None,
             ALU.mult, r=RTW.k(), w=SMAT.k())
        for e in range(E):
            b.op("DVE", "tensor_scalar", SM2.ap(e, e + 1), CC.ap(1, 2),
                 float(e), None, ALU.is_equal, r=CC.k(), w=SM2.k())
        b.op("DVE", "tensor_copy", SM2.ap(8, 9), CC.ap(1, 2),
             r=CC.k(), w=SM2.k())
        b.transp(7, SMAT.ap(), cst["EYEF"].ap(), b.pf(7, 0, 128, 0, 16),
                 r=SMAT.k() + cst["EYEF"].k())
        b.op("DVE", "tensor_copy", SROWS.ap(tt * 128, (tt + 1) * 128, 0, 16),
             b.pf(7, 0, 128, 0, 16), r=b.pk(7),
             w=SROWS.k(tt * 128, (tt + 1) * 128) + b.pk(7))
        b.transp(6, SM2.ap(), cst["EYEF"].ap(), b.pf(6, 0, 128, 0, 16),
                 r=SM2.k() + cst["EYEF"].k())
        b.op("DVE", "tensor_copy", MR.ap(tt * 128, (tt + 1) * 128, 0, 16),
             b.pf(6, 0, 128, 0, 16), r=b.pk(6),
             w=MR.k(tt * 128, (tt + 1) * 128) + b.pk(6))
    b.freebuf("WRS", "PRB", "EQ", "SMAT", "SM2", "PROBS", "PCOL", "RTW")

    ones16 = bass.AP(cst["ONESC"].ap().tensor, cst["ONESC"].ap().offset,
                     [list(cst["ONESC"].ap().ap[0])[:1] + [16], [0, T]])
    b.op("DVE", "tensor_tensor_scan", RNK.ap(0, T, 0, 16), ones16,
         MR.ap(0, T, 0, 16), 0.0, op0=ALU.mult, op1=ALU.add,
         r=cst["ONESC"].k() + MR.k(0, T), w=RNK.k(0, T))
    b.op("DVE", "tensor_tensor", RNK.ap(0, T, 0, 16), RNK.ap(0, T, 0, 16),
         MR.ap(0, T, 0, 16), ALU.subtract, r=RNK.k(0, T) + MR.k(0, T),
         w=RNK.k(0, T))
    for tc in range(4):
        b.mm(7, R16.ap(8, 9, 0, 16), MR.ap(tc * 512, tc * 512 + 512, 0, 16),
             True, True, r=R16.k() + MR.k(tc * 512, tc * 512 + 512),
             out=b.pf(7, 0, 512, 0, 1))
        b.op("DVE", "tensor_copy", ROWS4.ap(tc * 512, tc * 512 + 512, 0, 1),
             b.pf(7, 0, 512, 0, 1), r=b.pk(7),
             w=ROWS4.k(tc * 512, tc * 512 + 512) + b.pk(7))
    b.op("DVE", "tensor_tensor", MR.ap(0, T, 0, 16), MR.ap(0, T, 0, 16),
         RNK.ap(0, T, 0, 16), ALU.mult, r=MR.k(0, T) + RNK.k(0, T),
         w=MR.k(0, T))
    for tc in range(4):
        b.mm(7, R16.ap(16, 17, 0, 16), MR.ap(tc * 512, tc * 512 + 512, 0, 16),
             True, True, r=R16.k() + MR.k(tc * 512, tc * 512 + 512),
             out=b.pf(7, 0, 512, 0, 1))
        b.op("DVE", "tensor_copy", ROWS4.ap(T + tc * 512, T + tc * 512 + 512, 0, 1),
             b.pf(7, 0, 512, 0, 1), r=b.pk(7),
             w=ROWS4.k(T + tc * 512, T + tc * 512 + 512) + b.pk(7))
    r_sel = ROWS4.ap(0, T, 0, 1)
    r_rnk = ROWS4.ap(T, 2 * T, 0, 1)
    r_d = ROWS4.ap(2 * T, 3 * T, 0, 1)
    r_t = ROWS4.ap(3 * T, 4 * T, 0, 1)
    kS, kR = ROWS4.k(0, T), ROWS4.k(T, 2 * T)
    kD, kT = ROWS4.k(2 * T, 3 * T), ROWS4.k(3 * T, 4 * T)
    b.op("DVE", "scalar_tensor_tensor", r_d, r_sel, float(C), r_rnk,
         op0=ALU.mult, op1=ALU.add, r=kS + kR, w=kD)
    b.op("DVE", "tensor_scalar", r_t, r_rnk, float(C), None, ALU.is_lt,
         r=kR, w=kT)
    b.op("DVE", "tensor_tensor", r_d, r_d, r_t, ALU.mult, r=kD + kT, w=kD)
    b.op("DVE", "tensor_scalar", r_t, r_t, -1.0, 1.0, ALU.mult, ALU.add,
         r=kT, w=kT)
    b.op("DVE", "tensor_tensor", r_t, r_t, I128.ap(0, T, 0, 1), ALU.mult,
         r=kT + I128.k(), w=kT)
    b.op("DVE", "tensor_tensor", r_d, r_d, r_t, ALU.add, r=kD + kT, w=kD)
    for tt in range(NT):
        b.transp(7, ROWS4.ap(2 * T + tt * 128, 2 * T + (tt + 1) * 128, 0, 1),
                 cst["EYEF"].ap(0, 1, 0, 1), b.pf(7, 0, 1, 0, 128),
                 r=kD + cst["EYEF"].k())
        b.op("DVE", "tensor_copy", DCOL.ap(tt, tt + 1), b.pf(7, 0, 1, 0, 128),
             r=b.pk(7), w=DCOL.k() + b.pk(7))
    if "mrd" in DBG:
        key = ("DBG", "mrd", bi, 0)
        b.dbg_keys.append(key)
        b.dma("SOq", b.dram["dbg_mrd"][bi * NT * 128: bi * NT * 128 + 16, 0:T],
              MR.ap(0, T, 0, 16), r=MR.k(0, T), w=[key])
        key2 = ("DBG", "mrd", bi, 1)
        b.dbg_keys.append(key2)
        b.dma("SOq", b.dram["dbg_mrd"][bi * NT * 128 + 16: bi * NT * 128 + 20, 0:T],
              ROWS4.ap3(0, [(T, 4), (1, T)], 0, 1), r=ROWS4.k(0, 4 * T), w=[key2])
    b.freebuf("MR", "RNK", "ROWS4", "I128", "R16")

    EXP9 = b.buf("EXP9", F32, 9 * 128)
    b.dma("Sq", EXP9.ap(0, 9 * 128, 0, 16), d["c_exp9"], w=EXP9.k())
    SBC = b.buf("SBC", BF16, T)
    for tc in range(4):
        b.mm(7, EXP9.ap(8 * 128, 9 * 128, 0, 16),
             SROWS.ap(tc * 512, tc * 512 + 512, 0, 16), True, True,
             r=EXP9.k() + SROWS.k(tc * 512, tc * 512 + 512))
        b.op("DVE", "tensor_copy", SBC.ap(tc * 512, tc * 512 + 512),
             b.pf(7, 0, 512), r=b.pk(7),
             w=SBC.k(tc * 512, tc * 512 + 512) + b.pk(7))
    b.freebuf("SROWS", "EXP9")

    if bi == 0:
        ZT = b.buf("ZT", BF16, D)
        b.op("DVE", "memset", ZT.ap(), 0.0, w=ZT.k())
        for rr in range(65):
            b.dma("SOq", d["RG"][rr * 128:(rr + 1) * 128, :], ZT.ap(),
                  r=ZT.k(), w=[("RG",)])
        b.dma("SOq", d["OGT"][ZBASE:ZBASE + 128, :], ZT.ap(), r=ZT.k(),
              w=[("OGTZ",)])
        b.freebuf("ZT")
    TMT = b.buf("TMT", BF16, 2 * D)
    for tt in range(NT):
        sl = tt % 2
        for g in range(2):
            bank = 6 + g
            for c in range(4):
                ch = g * 4 + c
                b.transp(bank, RN2.ap(ch * T + tt * 128, ch * T + (tt + 1) * 128),
                         cst["EYEB"].ap(), b.pb(bank, c * 128, (c + 1) * 128),
                         r=RN2.k(ch * T + tt * 128, ch * T + (tt + 1) * 128)
                           + cst["EYEB"].k())
            b.op("DVE", "tensor_copy",
                 TMT.ap(sl * D + g * 512, sl * D + (g + 1) * 512),
                 b.pb(bank, 0, 512), r=b.pk(bank),
                 w=TMT.k(sl * D + g * 512, sl * D + (g + 1) * 512) + b.pk(bank))
        b.idma(d["RG"], DCOL.ap(tt, tt + 1), TMT.ap(sl * D, (sl + 1) * D), None,
               r=TMT.k(sl * D, (sl + 1) * D) + DCOL.k(), w=[("RG",)])
    b.freebuf("TMT")

    G0 = b.cbuf("G0", BF16, 8, 512)
    G1 = b.cbuf("G1", BF16, 8, 512)
    USC = b.buf("USC", BF16, 512)
    WSL = [b.buf(f"WC{i}", BF16, 8 * 1024) for i in range(4)]
    XG = XGT = OUTC = None
    gbufs = [G0, G1]

    steps = [(8, 0), (8, 1)] + [(e, F_) for e in range(E) for F_ in range(2)]

    def w_src(e, F_, which):
        nm = ["ew1", "ew3", "ew2"][which] if e < E else ["sw1", "sw3", "sw2"][which]
        ap = d[nm]
        if e < E:
            return ap[e:e + 1, F_:F_ + 1]
        return ap[F_:F_ + 1]

    def load_step(si):
        e, F_ = steps[si]
        for j in range(3):
            sl = WSL[(si * 3 + j) % 4]
            b.dma("Pq", sl.ap(), w_src(e, F_, j), w=sl.k())

    load_step(0)
    pend_w2 = []

    def emit_w2_shared(tc, gsl, w2s):
        g = gbufs[gsl]
        for dch in range(NCH):
            bank = 4 + (dch % 2)
            for fc in range(NCH):
                b.mm(bank, w2s.ap(fc * 1024 + dch * 128, fc * 1024 + (dch + 1) * 128),
                     g.ap(fc * 512, (fc + 1) * 512), fc == 0, fc == NCH - 1,
                     r=w2s.k() + g.k(fc * 512, (fc + 1) * 512))
            c0 = dch * T + tc * 512
            b.op("DVE", "tensor_tensor", RES.ap(c0, c0 + 512), b.pf(bank, 0, 512),
                 RES.ap(c0, c0 + 512), ALU.add,
                 r=RES.k(c0, c0 + 512) + b.pk(bank),
                 w=RES.k(c0, c0 + 512) + b.pk(bank))

    def emit_w2_routed(e, F_, tc, gsl, w2s):
        g = gbufs[gsl]
        for st in range(4):
            slot0 = tc * 512 + st * 128
            sti = slot0 // 128
            for dcol in range(2):
                bank = 4 + (st % 2)
                for fc in range(NCH):
                    b.mm(bank, g.ap(fc * 512 + st * 128, fc * 512 + (st + 1) * 128),
                         w2s.ap(fc * 1024 + dcol * 512, fc * 1024 + dcol * 512 + 512),
                         fc == 0, fc == NCH - 1,
                         r=w2s.k(fc * 512, (fc + 1) * 512)
                           + g.k(fc * 512, (fc + 1) * 512))
                oc0 = sti * D + dcol * 512
                if F_ == 0:
                    b.op("DVE", "tensor_copy", OUTC.ap(oc0, oc0 + 512),
                         b.pf(bank, 0, 512), r=b.pk(bank),
                         w=OUTC.k(oc0, oc0 + 512) + b.pk(bank))
                else:
                    b.op("DVE", "tensor_tensor", OUTC.ap(oc0, oc0 + 512),
                         b.pf(bank, 0, 512), OUTC.ap(oc0, oc0 + 512), ALU.add,
                         r=OUTC.k(oc0, oc0 + 512) + b.pk(bank),
                         w=OUTC.k(oc0, oc0 + 512) + b.pk(bank))
            if F_ == 1:
                row0 = e * C + slot0
                b.dma("Pq", d["OGT"][row0:row0 + 128, :], OUTC.ap(sti * D, (sti + 1) * D),
                      r=OUTC.k(sti * D, (sti + 1) * D), w=[("OGT", e, slot0)])

    for si, (e, F_) in enumerate(steps):
        w1s = WSL[(si * 3 + 0) % 4]
        w3s = WSL[(si * 3 + 1) % 4]
        w2s = WSL[(si * 3 + 2) % 4]
        shared = e == 8
        if e == 0 and F_ == 0:
            b.freebuf("X2B")
            XG = [b.cbuf("XGa", BF16, 8, C), b.cbuf("XGb", BF16, 8, C)]
            XGT = b.buf("XGT", BF16, 2 * D)
            OUTC = b.cbuf("OUTC", BF16, 8, D)
        def xg_fill(ee):
            xg = XG[ee % 2]
            for st in range(8):
                sl2 = st % 2
                b.dma("Sq", XGT.ap(sl2 * D, (sl2 + 1) * D),
                      d["RG"][ee * C + st * 128: ee * C + (st + 1) * 128, :],
                      r=[("RG",)], w=XGT.k(sl2 * D, (sl2 + 1) * D))
                for g2 in range(2):
                    bank = 6 + g2
                    for c2 in range(4):
                        ch2 = g2 * 4 + c2
                        b.transp(bank,
                                 XGT.ap(sl2 * D + ch2 * 128, sl2 * D + (ch2 + 1) * 128),
                                 cst["EYEB"].ap(), b.pb(bank, c2 * 128, (c2 + 1) * 128),
                                 r=XGT.k(sl2 * D, (sl2 + 1) * D) + cst["EYEB"].k())
                    for c2 in range(4):
                        ch2 = g2 * 4 + c2
                        c0x = ch2 * C + st * 128
                        b.op("DVE", "tensor_copy", xg.ap(c0x, c0x + 128),
                             b.pb(bank, c2 * 128, (c2 + 1) * 128), r=b.pk(bank),
                             w=xg.k(c0x, c0x + 128) + b.pk(bank))

        if e == 0 and F_ == 0:
            xg_fill(0)
        if not shared and F_ == 1 and e + 1 < E:
            xg_fill(e + 1)
        SRC = RN2 if shared else XG[e % 2]
        ntc = 4 if shared else 2
        srcw = T if shared else C
        for tc in range(ntc):
            gsl = tc % 2
            g = gbufs[gsl]
            for fc in range(NCH):
                hb = fc % 2
                ub = 2 + (fc % 2)
                for dc in range(NCH):
                    b.mm(hb, w1s.ap(dc * 1024 + fc * 128, dc * 1024 + (fc + 1) * 128),
                         SRC.ap(dc * srcw + tc * 512, dc * srcw + tc * 512 + 512),
                         dc == 0, dc == NCH - 1,
                         r=w1s.k() + SRC.k(dc * srcw + tc * 512,
                                           dc * srcw + tc * 512 + 512))
                b.op("ACT", "activation", g.ap(fc * 512, (fc + 1) * 512),
                     b.pf(hb, 0, 512), AF.Silu, r=b.pk(hb),
                     w=g.k(fc * 512, (fc + 1) * 512) + b.pk(hb))
                for dc in range(NCH):
                    b.mm(ub, w3s.ap(dc * 1024 + fc * 128, dc * 1024 + (fc + 1) * 128),
                         SRC.ap(dc * srcw + tc * 512, dc * srcw + tc * 512 + 512),
                         dc == 0, dc == NCH - 1,
                         r=w3s.k() + SRC.k(dc * srcw + tc * 512,
                                           dc * srcw + tc * 512 + 512))
                if shared:
                    b.op("DVE", "tensor_tensor", USC.ap(0, 512), b.pf(ub, 0, 512),
                         SBC.ap(tc * 512, tc * 512 + 512), ALU.mult,
                         r=SBC.k(tc * 512, tc * 512 + 512) + b.pk(ub),
                         w=USC.k() + b.pk(ub))
                    b.op("DVE", "tensor_tensor", g.ap(fc * 512, (fc + 1) * 512),
                         g.ap(fc * 512, (fc + 1) * 512), USC.ap(0, 512), ALU.mult,
                         r=g.k(fc * 512, (fc + 1) * 512) + USC.k(),
                         w=g.k(fc * 512, (fc + 1) * 512))
                else:
                    b.op("DVE", "tensor_tensor", g.ap(fc * 512, (fc + 1) * 512),
                         g.ap(fc * 512, (fc + 1) * 512), b.pf(ub, 0, 512), ALU.mult,
                         r=g.k(fc * 512, (fc + 1) * 512) + b.pk(ub),
                         w=g.k(fc * 512, (fc + 1) * 512) + b.pk(ub))
            pend_w2.append((e, F_, tc, gsl, w2s))
            if tc == ntc - 1 and si + 1 < len(steps):
                load_step(si + 1)
            if len(pend_w2) >= 2:
                ee, ff, tcc, gg, ww = pend_w2.pop(0)
                if ee == 8:
                    emit_w2_shared(tcc, gg, ww)
                else:
                    emit_w2_routed(ee, ff, tcc, gg, ww)
    while pend_w2:
        ee, ff, tcc, gg, ww = pend_w2.pop(0)
        if ee == 8:
            emit_w2_shared(tcc, gg, ww)
        else:
            emit_w2_routed(ee, ff, tcc, gg, ww)
    b.freebuf("SBC", "G0", "G1", "USC", "XGa", "XGb", "XGT", "OUTC",
              "WC0", "WC1", "WC2", "WC3")


def stage_f(b, cst, bi):
    d = b.dram
    RES = b.bufs["RES"]
    ESC = b.bufs["ESC"]
    DCOL = b.bufs["DCOL"]
    C = 1024
    ZTM = b.buf("ZTMF", F32, D)
    SQA = b.buf("SQAF", BF16, D)
    COLS = b.buf("COLSF", F32, 8)
    MOET = b.buf("MOET", BF16, 2 * D)
    ogt_keys = [("OGT", e, s0) for e in range(E) for s0 in range(0, C, 128)]
    ogt_keys += [("OGTZ",)]
    for tt in range(NT):
        sl = tt % 2
        b.idma(MOET.ap(sl * D, (sl + 1) * D), None, d["OGT"],
               DCOL.ap(tt, tt + 1),
               r=ogt_keys + DCOL.k(), w=MOET.k(sl * D, (sl + 1) * D))
        for g in range(2):
            bank = 6 + g
            for c in range(4):
                ch = g * 4 + c
                b.transp(bank, RES.ap(ch * T + tt * 128, ch * T + (tt + 1) * 128),
                         cst["EYEF"].ap(), b.pf(bank, c * 128, (c + 1) * 128),
                         r=RES.k(ch * T + tt * 128, ch * T + (tt + 1) * 128)
                           + cst["EYEF"].k())
            b.op("DVE", "tensor_copy", ZTM.ap(g * 512, g * 512 + 512),
                 b.pf(bank, 0, 512), r=b.pk(bank),
                 w=ZTM.k(g * 512, g * 512 + 512) + b.pk(bank))
        b.op("DVE", "scalar_tensor_tensor", ZTM.ap(), MOET.ap(sl * D, (sl + 1) * D),
             ESC.ap(tt, tt + 1), ZTM.ap(), op0=ALU.mult, op1=ALU.add,
             r=MOET.k(sl * D, (sl + 1) * D) + ESC.k() + ZTM.k(), w=ZTM.k())
        layernorm_tm(b, cst, ZTM, SQA, COLS)
        b.dma("SOq", d["out"][bi * T + tt * 128: bi * T + (tt + 1) * 128, :],
              ZTM.ap(), r=ZTM.k(), w=[("OUT", bi, tt)])
    b.freebuf("ZTMF", "SQAF", "COLSF", "MOET", "RES", "ESC", "DCOL")


def _lay(w):
    din, dout = w.shape
    return np.ascontiguousarray(
        w.reshape(din // 128, 128, dout).transpose(1, 0, 2).reshape(128, -1)
    ).astype(np.float32)


def _layF(w):
    return np.ascontiguousarray(
        w.reshape(8, 128, 2, 1024).transpose(2, 1, 0, 3).reshape(2, 128, 8 * 1024)
    ).astype(np.float32)


def _layF2(w):
    return np.ascontiguousarray(
        w.reshape(2, 8, 128, 1024).transpose(0, 2, 1, 3).reshape(2, 128, 8 * 1024)
    ).astype(np.float32)


def _prep_inputs(inp):
    g = {}
    g["wq_a"] = _lay(inp["L0_attn_Wq"])
    g["wk_a"] = _lay(inp["L0_attn_Wkv"][:, :D])
    g["wv_a"] = _lay(inp["L0_attn_Wkv"][:, D:])
    g["wo_a"] = _lay(inp["L0_attn_Wo"])
    g["c_qg"] = np.ascontiguousarray(
        inp["L0_attn_q_gamma"].reshape(-1).reshape(8, 128).T).astype(np.float32)
    g["c_kg"] = np.ascontiguousarray(
        inp["L0_attn_k_gamma"].reshape(-1).reshape(8, 128).T).astype(np.float32)
    g["wq_f"] = _lay(inp["L0_ffn_Wq"])
    g["wkv_f"] = _lay(inp["L0_ffn_Wkv"])
    g["wk_up"] = _lay(np.ascontiguousarray(
        inp["L0_ffn_Wk_up"].transpose(1, 0, 2).reshape(L, H * DH)))
    g["wv_up"] = _lay(np.ascontiguousarray(
        inp["L0_ffn_Wv_up"].transpose(1, 0, 2).reshape(L, H * DH)))
    g["wo_f"] = _lay(inp["L0_ffn_Wo"])
    g["wr"] = _lay(inp["L0_ffn_Wr"])
    g["c_rb"] = inp["L0_ffn_r_bias"].reshape(1, E).astype(np.float32)
    g["sw1"] = _layF(inp["L0_ffn_sw1"])
    g["sw3"] = _layF(inp["L0_ffn_sw3"])
    g["sw2"] = _layF2(inp["L0_ffn_sw2"])
    g["ew1"] = np.stack([_layF(inp["L0_ffn_ew1"][e]) for e in range(E)])
    g["ew3"] = np.stack([_layF(inp["L0_ffn_ew3"][e]) for e in range(E)])
    g["ew2"] = np.stack([_layF2(inp["L0_ffn_ew2"][e]) for e in range(E)])

    g["c_eye"] = np.eye(128, dtype=np.float32)
    m = np.zeros((128, 128), dtype=np.float32)
    m[np.triu_indices(128, 1)] = NEG
    g["c_mask"] = m
    inv = 1.0 / (100000.0 ** (np.arange(0, DH, 2, dtype=np.float64) / DH))
    tpos = np.arange(T, dtype=np.float64)[:, None] * inv[None, :]
    g["c_cos"] = np.ascontiguousarray(np.cos(tpos).astype(np.float32)
                                      .reshape(NT, 128, 32).transpose(1, 0, 2)
                                      .reshape(128, NT * 32))
    g["c_sin"] = np.ascontiguousarray(np.sin(tpos).astype(np.float32)
                                      .reshape(NT, 128, 32).transpose(1, 0, 2)
                                      .reshape(128, NT * 32))
    gain = np.zeros((128, 24), dtype=np.float32)
    gain[:, 0:8] = inp["L0_attn_norm_w"].reshape(8, 128).T
    gain[:, 8:16] = inp["L0_ffn_ln1_w"].reshape(8, 128).T
    gain[:, 16:24] = inp["L0_ffn_ln2_w"].reshape(8, 128).T
    g["c_gain"] = gain
    g["c_io8"] = np.tile(np.arange(8, dtype=np.float32), (128, 1))
    ind2 = np.zeros((128, 2), dtype=np.float32)
    ind2[0:64, 0] = 1.0
    ind2[64:128, 1] = 1.0
    g["c_ind2"] = ind2
    row2 = np.zeros((2, 256), dtype=np.float32)
    row2[0, 0:64] = 1.0
    row2[1, 64:128] = 1.0
    row2[0, 128:256] = 1.0
    g["c_row2"] = row2
    exp9 = np.zeros((16, 9 * 128), dtype=np.float32)
    for e in range(9):
        exp9[e, e * 128:(e + 1) * 128] = 1.0
    g["c_exp9"] = exp9
    g["c_i128"] = (8 * 1024 + (np.arange(T) % 128)).astype(np.float32).reshape(1, T)
    r16 = np.zeros((16, 32), dtype=np.float32)
    r16[:16, :16] = np.eye(16)
    r16[0:8, 16] = 1.0
    g["c_r16"] = r16
    return g


def kernel(**inputs):
    inputs = {k: np.asarray(v, dtype=np.float32) for k, v in inputs.items()}
    g = _prep_inputs(inputs)
    toks = np.ascontiguousarray(inputs["L0_attn_tokens"].reshape(16 * T, D))
    nc = build_nc()
    in_maps = []
    for c in range(N_CORES):
        m = dict(g)
        m["x0"] = np.ascontiguousarray(
            toks[c * B_PER_CORE * T:(c + 1) * B_PER_CORE * T, :])
        in_maps.append(m)
    res = run_bass_kernel_spmd(nc, in_maps, core_ids=list(range(N_CORES)))
    out = np.concatenate([r["out"].reshape(B_PER_CORE, T, D) for r in res.results],
                         axis=0)
    return out
